# version 10
# speedup vs baseline: 1.0745x; 1.0239x over previous
import copy
import math
import numpy as np
from contextlib import ExitStack

import concourse.bass as bass
import concourse.mybir as mybir
from concourse.bass_utils import run_bass_kernel_spmd

F32 = mybir.dt.float32
BF16 = mybir.dt.bfloat16
AF = mybir.ActivationFunctionType
ALU = mybir.AluOpType
AX = mybir.AxisListType

D = 1024
S = 2048
NT = 8
NEG = -1.0e37
LN_EPS = 1e-5
RMS_EPS = 1.1920929e-07


ENGS = ("sp", "gp", "te", "ve", "ac")


class _Op:
    __slots__ = ("idx", "eng", "emit", "reads", "writes", "dma_sem", "deps",
                 "need_inc", "sem", "val", "tag", "wait_list")

    def __init__(self, idx, eng, emit, reads, writes, dma_sem, tag):
        self.idx, self.eng, self.emit = idx, eng, emit
        self.reads, self.writes = reads, writes
        self.dma_sem = dma_sem
        self.deps = []
        self.need_inc = False
        self.sem = None
        self.val = 0
        self.tag = tag


def _keys(ks):
    return [k if isinstance(k, tuple) else (k,) for k in ks]


class Prog:
    def __init__(self, nc):
        self.nc = nc
        self.ops = []
        self._track = {}
        self.store_sems = set()
        self._floor = None

    def op(self, eng, emit, reads=(), writes=(), dma_sem=None, tag="", store=False):
        o = _Op(len(self.ops), eng, emit, _keys(reads), _keys(writes), dma_sem, tag)
        self.ops.append(o)
        if store and dma_sem:
            self.store_sems.add(dma_sem)
        deps = {}

        def dep(p):
            if p is not o:
                deps[p.idx] = p

        for key in o.reads:
            name, sub = key[0], key[1:] or None
            t = self._track.setdefault(name, {"w": {}, "r": {}})
            if sub is None:
                for p in t["w"].values():
                    dep(p)
            else:
                if None in t["w"]:
                    dep(t["w"][None])
                if sub in t["w"]:
                    dep(t["w"][sub])
            t["r"].setdefault(sub, []).append(o)
        for key in o.writes:
            name, sub = key[0], key[1:] or None
            t = self._track.setdefault(name, {"w": {}, "r": {}})
            if sub is None:
                for p in t["w"].values():
                    dep(p)
                for lst in t["r"].values():
                    for p in lst:
                        dep(p)
                t["w"] = {None: o}
                t["r"] = {}
            else:
                if None in t["w"]:
                    dep(t["w"][None])
                if sub in t["w"]:
                    dep(t["w"][sub])
                for p in t["r"].get(sub, ()):
                    dep(p)
                for p in t["r"].get(None, ()):
                    dep(p)
                t["w"][sub] = o
                t["r"][sub] = []
        if not deps and self._floor is not None:
            deps[self._floor.idx] = self._floor
        o.deps = list(deps.values())
        return o

    def barrier(self, tag="bar"):
        last_eng, last_sem = {}, {}
        for p in self.ops:
            if p.dma_sem is not None:
                last_sem[p.dma_sem] = p
            else:
                last_eng[p.eng] = p
        o = self.op("ve", lambda e: e.engine_nop(), tag=tag)
        deps = {}
        for p in list(last_eng.values()) + list(last_sem.values()):
            if p is not o:
                deps[p.idx] = p
        o.deps = list(deps.values())
        self._track = {}
        self._floor = o
        return o

    def dma(self, queue, out, in_, sem, reads=(), writes=(), tag="", store=False, **kw):
        def emit(eng):
            return eng.dma_start(out=out, in_=in_, **kw)
        return self.op(queue, emit, reads=reads, writes=writes, dma_sem=sem,
                       tag=tag, store=store)

    def finish(self, final_engine="gp"):
        nc = self.nc
        for o in self.ops:
            for p in o.deps:
                if p.eng == o.eng == "te" and p.dma_sem is None and o.dma_sem is None:
                    continue
                p.need_inc = True
        for o in self.ops:
            if o.dma_sem is not None:
                o.need_inc = True
        counters, sem_owner = {}, {}
        for o in self.ops:
            if not o.need_inc:
                continue
            o.sem = o.dma_sem if o.dma_sem is not None else "c_" + o.eng
            if o.dma_sem is not None:
                own = sem_owner.setdefault(o.sem, o.eng)
                assert own == o.eng, f"dma sem {o.sem}: queues {own} and {o.eng}"
            amt = 16 if o.dma_sem is not None else 1
            counters[o.sem] = counters.get(o.sem, 0) + amt
            o.val = counters[o.sem]
        self.final_counts = dict(counters)
        dma_steps = {}
        for o in self.ops:
            if o.dma_sem is not None:
                dma_steps.setdefault(o.sem, []).append((o.idx, o.val))

        def dma_wait_val(sem, consumer_idx):
            v = 0
            for (i, cv) in dma_steps[sem]:
                if i < consumer_idx:
                    v = cv
                else:
                    break
            return v

        floors = {e: {} for e in ENGS}
        for o in self.ops:
            waits = {}
            for p in o.deps:
                if p.eng == o.eng == "te" and p.dma_sem is None and o.dma_sem is None:
                    continue
                v = dma_wait_val(p.sem, o.idx) if p.dma_sem is not None else p.val
                waits[p.sem] = max(waits.get(p.sem, 0), v)
            fl = floors[o.eng]
            o.wait_list = []
            for s in sorted(waits):
                v = waits[s]
                if fl.get(s, 0) >= v:
                    continue
                fl[s] = v
                o.wait_list.append((s, v))
        self.sem_handles = {n: nc.alloc_semaphore(n) for n in counters}
        final_waits = [(s, self.final_counts[s]) for s in sorted(self.store_sems)]
        by_eng = {e: [] for e in ENGS}
        for o in self.ops:
            by_eng[o.eng].append(o)
        prog = self

        def run(eng_name, eng):
            for o in by_eng[eng_name]:
                for (s, v) in o.wait_list:
                    eng.wait_ge(prog.sem_handles[s], v)
                inst = o.emit(eng)
                if o.need_inc:
                    if isinstance(inst, list):
                        inst = inst[-1]
                    assert inst is not None, f"op {o.tag}: inc needed, no instruction"
                    inst.then_inc(prog.sem_handles[o.sem],
                                  16 if o.dma_sem is not None else 1)
            if eng_name == final_engine:
                for (s, v) in final_waits:
                    eng.wait_ge(prog.sem_handles[s], v)

        with nc.Block() as block:
            @block.sync
            def _(eng):
                run("sp", eng)

            @block.gpsimd
            def _(eng):
                run("gp", eng)

            @block.tensor
            def _(eng):
                run("te", eng)

            @block.vector
            def _(eng):
                run("ve", eng)

            @block.scalar
            def _(eng):
                run("ac", eng)



def host_constants():
    c = {}
    c["ident"] = np.eye(128, dtype=np.float32)
    c["ones_col"] = np.ones((128, 1), np.float32)
    c["ones_row"] = np.ones((1, 128), np.float32)
    cidx = np.arange(128)[:, None]
    pidx = np.arange(128)[None, :]
    c["m_up"] = np.where(cidx > pidx, NEG, 0.0).astype(np.float32)
    c["m_low"] = np.where(cidx < pidx, NEG, 0.0).astype(np.float32)
    sig = np.zeros((128, 128), np.float32)
    for p in range(128):
        h, dh = divmod(p, 64)
        sig[h * 64 + ((dh + 32) % 64), p] = 1.0
    c["p_sigma"] = sig
    DH, WINDOW = 64, 256
    half = np.arange(0, DH, 2, dtype=np.float64)
    inv_freq = 1.0 / 10000.0 ** (half / DH)
    t = np.arange(S, dtype=np.float64)
    freqs = np.concatenate([t[:, None] * inv_freq[None, :]] * 2, -1)
    base_scale = (half + 0.4 * DH) / (1.4 * DH)
    power = (t - S // 2) / (WINDOW // 2)
    scale = np.concatenate([base_scale[None, :] ** power[:, None]] * 2, -1)
    cos, sin = np.cos(freqs), np.sin(freqs)
    sgn = np.where(np.arange(DH) < 32, -1.0, 1.0)
    c["rot_aq"] = np.tile((cos * scale).T, (2, 1)).astype(np.float32)
    c["rot_bq"] = np.tile((sin * scale).T * sgn[:, None], (2, 1)).astype(np.float32)
    c["rot_ak"] = np.tile((cos / scale).T, (2, 1)).astype(np.float32)
    c["rot_bk"] = np.tile((sin / scale).T * sgn[:, None], (2, 1)).astype(np.float32)
    blk2 = np.zeros((128, 2), np.float32)
    blk2[0:64, 0] = 1.0
    blk2[64:128, 1] = 1.0
    c["blk2"] = blk2
    c["blk2t"] = blk2.T.copy()
    return c


CONST_DTYPES = {k: BF16 for k in ("ident", "ones_col", "ones_row", "m_up",
                                  "m_low", "p_sigma", "rot_aq", "rot_bq",
                                  "rot_ak", "rot_bk", "blk2", "blk2t")}



class Ctx:
    def __init__(self, nc, P, stack):
        self.nc, self.P, self.stack = nc, P, stack
        self.sb = {}
        self.dram = {}
        self.rows_alloc = {}
        self.prefix = ""

    def sbuf(self, name, shape, dtype):
        t = self.stack.enter_context(
            self.nc.sbuf_tensor(self.prefix + name, list(shape), dtype))
        self.sb[name] = t
        return t

    def psum(self, name, shape, dtype=F32):
        t = self.stack.enter_context(self.nc.psum_tensor(name, list(shape), dtype))
        self.sb[name] = t
        return t

    def din(self, name, shape, dtype=F32):
        d = self.nc.dram_tensor(name, list(shape), dtype, kind="ExternalInput").ap()
        self.dram[name] = d
        return d

    def dout(self, name, shape, dtype=F32):
        d = self.nc.dram_tensor(name, list(shape), dtype, kind="ExternalOutput").ap()
        self.dram[name] = d
        return d


ROT_NAMES = ("rot_aq", "rot_bq", "rot_ak", "rot_bk")


def setup_common(ctx, consts_np, const_names=None, cols_w=120, sbuf_skip=()):
    P = ctx.P
    for name, arr in consts_np.items():
        if const_names is not None and name not in const_names:
            continue
        if name in sbuf_skip:
            continue
        d = ctx.din("c_" + name, arr.shape)
        t = ctx.sbuf(name, arr.shape, CONST_DTYPES[name])
        P.dma("gp", t[:, :], d, sem="ld_c", writes=[(name,)], tag="ldc_" + name)
    ctx.dram["rowscr"] = ctx.nc.dram_tensor(
        "rowscr", [4, S], F32, kind="Internal").ap()
    ctx.sbuf("bcb", (128, 2 * 512), F32)
    ctx.psum("psA", (128, 2048))
    ctx.psum("psB", (128, 2048))
    ctx.sbuf("rowp32", (128, S), F32)
    ctx.sbuf("rowp16", (1, 2 * S), BF16)
    ctx.sbuf("cols", (128, cols_w), F32)
    ctx.sbuf("colsb", (128, 16), BF16)


def psA(ctx, bank, p, f0, f1):
    t = ctx.sb["psA"]
    return t[0:p, bank * 512 + f0: bank * 512 + f1]


def psB(ctx, bank, p, f0, f1):
    t = ctx.sb["psB"]
    return t[0:p, bank * 512 + f0: bank * 512 + f1]


def row32(ctx, i, f0=0, f1=S):
    assert 0 <= i < 4
    return ctx.sb["rowp32"][i * 32: i * 32 + 1, f0:f1]


def row16(ctx, i, f0=0, f1=S):
    assert 0 <= i < 2
    return ctx.sb["rowp16"][0:1, i * S + f0: i * S + f1]


def emit_row_pair(ctx, out_ap_fn, out_key, in0_ap_fn, in0_keys, in1_ap_fn,
                  in1_keys, op, tagp, via=AF.Copy):
    P = ctx.P
    for c in range(4):
        f0, f1 = c * 512, (c + 1) * 512
        pr = psA(ctx, 2, 1, 0, 512)
        P.op("ac", lambda e, f0=f0, f1=f1, pr=pr: e.activation(
            pr, in1_ap_fn(f0, f1), via),
            reads=in1_keys, writes=[RA(2)], tag=f"{tagp}p{c}")
        P.op("ve", lambda e, f0=f0, f1=f1, pr=pr: e.tensor_tensor(
            out_ap_fn(f0, f1), in0_ap_fn(f0, f1), pr, op),
            reads=in0_keys + [RA(2)], writes=[out_key], tag=f"{tagp}t{c}")


def RA(bank):
    return ("psA", bank)


def RB(bank):
    return ("psB", bank)


def R32(i):
    return ("rowp32", i)


def R16(i):
    return ("rowp16", i)



def emit_cast_x_to_bf(ctx, dst_name, shifted_buf=True, tagp="cx"):
    P = ctx.P
    x = ctx.sb["xcur"]
    dst = ctx.sb[dst_name]
    for i in range(NT):
        P.op("ve", lambda e, i=i: e.memset(dst[:, i * (S + 1): i * (S + 1) + 1], 0.0),
             writes=[(dst_name, i)], tag=f"{tagp}z{i}")
        P.op("ac", lambda e, i=i: e.activation(
            dst[:, i * (S + 1) + 1: (i + 1) * (S + 1)], x[:, i * S:(i + 1) * S], AF.Copy),
            reads=[("xcur", i)], writes=[(dst_name, i)], tag=f"{tagp}c{i}")


def xv(ctx, buf_name, i, shifted=False, f0=0, f1=S):
    t = ctx.sb[buf_name]
    off = i * (S + 1) + (0 if shifted else 1)
    return t[:, off + f0: off + f1]


def emit_ln_stats(ctx, src_tiles, sqc_name, r_mu, r_ex2, r_var, r_rstd,
                  eps, tagp, rstd_extra=None):
    P = ctx.P
    sq = ctx.sb[sqc_name]
    ones = ctx.sb["ones_col"]
    CW = sq.shape[1] // NT if hasattr(sq, "shape") else 256
    for c in range(S // CW):
        f0, f1 = c * CW, (c + 1) * CW
        for i, (apf, keys) in enumerate(src_tiles):
            P.op("ac", lambda e, apf=apf, i=i, f0=f0, f1=f1: e.activation(
                sq[:, i * CW:(i + 1) * CW], apf(f0, f1), AF.Square),
                reads=keys, writes=[(sqc_name, i)], tag=f"{tagp}sq{c}_{i}")
            P.op("te", lambda e, apf=apf, i=i, f0=f0, f1=f1: e.matmul(
                psA(ctx, 0, 1, 0, CW), ones[:, :], apf(f0, f1),
                start=(i == 0), stop=(i == NT - 1)),
                reads=keys + [("ones_col",)], writes=[RA(0)], tag=f"{tagp}mx{c}_{i}")
            P.op("te", lambda e, i=i: e.matmul(
                psA(ctx, 1, 1, 0, CW), ones[:, :], sq[:, i * CW:(i + 1) * CW],
                start=(i == 0), stop=(i == NT - 1)),
                reads=[(sqc_name, i), ("ones_col",)], writes=[RA(1)], tag=f"{tagp}mq{c}_{i}")
        P.op("ve", lambda e, c=c, f0=f0, f1=f1: e.tensor_scalar(
            row32(ctx, r_mu, f0, f1), psA(ctx, 0, 1, 0, CW),
            1.0 / D, None, ALU.mult),
            reads=[RA(0)], writes=[R32(r_mu)], tag=f"{tagp}emu{c}")
        P.op("ve", lambda e, c=c, f0=f0, f1=f1: e.tensor_scalar(
            row32(ctx, r_ex2, f0, f1), psA(ctx, 1, 1, 0, CW),
            1.0 / D, None, ALU.mult),
            reads=[RA(1)], writes=[R32(r_ex2)], tag=f"{tagp}eex{c}")
    emit_row_pair(ctx, lambda f0, f1: row32(ctx, r_var, f0, f1), R32(r_var),
                  lambda f0, f1: row32(ctx, r_ex2, f0, f1), [R32(r_ex2)],
                  lambda f0, f1: row32(ctx, r_mu, f0, f1), [R32(r_mu)],
                  ALU.subtract, f"{tagp}var", via=AF.Square)
    P.op("ve", lambda e: e.tensor_scalar(
        row32(ctx, r_var), row32(ctx, r_var), float(eps), None, ALU.add),
        reads=[R32(r_var)], writes=[R32(r_var)], tag=f"{tagp}veps")
    P.op("ac", lambda e: e.activation(
        row32(ctx, r_rstd), row32(ctx, r_var), AF.Ln),
        reads=[R32(r_var)], writes=[R32(r_rstd)], tag=f"{tagp}ln")
    P.op("ac", lambda e: e.activation(
        row32(ctx, r_rstd), row32(ctx, r_rstd), AF.Exp, scale=-0.5),
        reads=[R32(r_rstd)], writes=[R32(r_rstd)], tag=f"{tagp}exp")
    if rstd_extra is not None:
        rstd_extra()


def emit_row_to_bf(ctx, r_src, rb_dst, tagp, scale=1.0):
    P = ctx.P
    P.op("ve", lambda e: e.tensor_scalar(
        row16(ctx, rb_dst), row32(ctx, r_src), float(scale), None, ALU.mult),
        reads=[R32(r_src)], writes=[R16(rb_dst)], tag=tagp)


def emit_row_to_dram(ctx, r_idx, slot, tagp):
    P = ctx.P
    P.dma("sp", ctx.dram["rowscr"][slot: slot + 1, :], row32(ctx, r_idx),
          sem=f"st_row{slot}", reads=[R32(r_idx)], writes=[("rowscr", slot)],
          tag=tagp)


def emit_bcast_dram(ctx, slot, bcslot, tagp, f0=0, f1=S):
    P = ctx.P
    w = f1 - f0
    ap = ctx.sb["bcb"][:, bcslot * 512: bcslot * 512 + w]
    key = ("bcb", bcslot)
    P.dma("sp", ap, ctx.dram["rowscr"][slot: slot + 1, f0:f1].partition_broadcast(128),
          sem=f"ld_bc{bcslot}", reads=[("rowscr", slot)], writes=[key], tag=tagp)
    return ap, key


def emit_bcast_row(ctx, rb_idx, pool, bank, nparts, tagp, f0=0, f1=S):
    P = ctx.P
    onesr = ctx.sb["ones_row"]
    pfun = psA if pool == "A" else psB
    ap = pfun(ctx, bank, nparts, 0, f1 - f0)
    key = RA(bank) if pool == "A" else RB(bank)
    P.op("te", lambda e: e.matmul(
        ap, onesr[0:1, 0:nparts], row16(ctx, rb_idx, f0, f1), start=True, stop=True),
        reads=[R16(rb_idx), ("ones_row",)], writes=[key], tag=tagp)
    return ap, key


def psbf(ctx, pool, bank, p, f0, f1):
    t = ctx.sb["psA" if pool == "A" else "psB"]
    return t[0:p, bank * 512:(bank + 1) * 512].bitcast(BF16)[0:p, f0:f1]


def emit_wload(ctx, dram_ap, rows, cols, dst_name, sem, tagp, dtile=128):
    P = ctx.P
    rt = rows // dtile
    dst = ctx.sb[dst_name]
    src = dram_ap.rearrange("(r p) c -> p r c", p=dtile)
    d = dst[:, 0: rt * cols].rearrange("p (r c) -> p r c", r=rt)
    P.dma("gp", d, src, sem=sem, writes=[(dst_name,)], tag=tagp)


def wtile(ctx, name, r, cols, f0=0, f1=None):
    t = ctx.sb[name]
    if f1 is None:
        f1 = cols
    return t[:, r * cols + f0: r * cols + f1]


def emit_wtranspose(ctx, src_name, rows, cols, dst_name, tagp,
                    fold_col=None):
    P = ctx.P
    src, dst = ctx.sb[src_name], ctx.sb[dst_name]
    ident = ctx.sb["ident"]
    rt, ct = rows // 128, cols // 128
    for r in range(rt):
        for c in range(ct):
            pap = psbf(ctx, "B", 3, 128, 0, 128)
            P.op("te", lambda e, r=r, c=c, pap=pap: e.transpose(
                pap, src[:, r * cols + c * 128: r * cols + (c + 1) * 128],
                ident[:, :]),
                reads=[(src_name,), ("ident",)], writes=[RB(3)], tag=f"{tagp}t{r}_{c}")
            dap = dst[:, c * rows + r * 128: c * rows + (r + 1) * 128]
            P.op("ac", lambda e, dap=dap, pap=pap: e.activation(dap, pap, AF.Copy),
                 reads=[RB(3)], writes=[(dst_name, "t")], tag=f"{tagp}e{r}_{c}")


def emit_colload(ctx, dram_ap, n, dst_name, col0, sem, tagp):
    P = ctx.P
    dst = ctx.sb[dst_name]
    for r in range(n):
        P.dma("gp", dst[:, col0 + r: col0 + r + 1],
              dram_ap[r * 128:(r + 1) * 128].unsqueeze(1),
              sem=sem, writes=[(dst_name, col0 + r)], tag=f"{tagp}{r}")



def emit_l0_ffn(ctx):
    P = ctx.P
    x = ctx.sb["xcur"]
    TAU = 256
    emit_cast_x_to_bf(ctx, "xbf", tagp="f0cx")
    src_tiles = [((lambda f0, f1, i=i: xv(ctx, "xbf", i, False, f0, f1)),
                  [("xbf", i)]) for i in range(NT)]
    emit_ln_stats(ctx, src_tiles, "sqc", 0, 1, 1, 2, LN_EPS, "f0st")
    emit_row_pair(ctx, lambda f0, f1: row32(ctx, 3, f0, f1), R32(3),
                  lambda f0, f1: row32(ctx, 0, f0, f1), [R32(0)],
                  lambda f0, f1: row32(ctx, 2, f0, f1), [R32(2)],
                  ALU.mult, "f0mr")
    emit_row_to_dram(ctx, 2, 0, "f0sd0")
    emit_row_to_dram(ctx, 3, 1, "f0sd1")
    for c in range(4):
        f0, f1 = c * 512, (c + 1) * 512
        bc_r, kr = emit_bcast_dram(ctx, 0, 0, f"f0br{c}", f0, f1)
        bc_m, km = emit_bcast_dram(ctx, 1, 1, f"f0bm{c}", f0, f1)
        for i in range(NT):
            sc = ctx.sb["scr32"]
            P.op("ve", lambda e, i=i, f0=f0, f1=f1, bc_r=bc_r: e.tensor_tensor(
                sc[:, 0:512], x[:, i * S + f0: i * S + f1], bc_r, ALU.mult),
                reads=[("xcur", i), kr], writes=[("scr32", 0)], tag=f"f0l1_{i}_{c}")
            P.op("ve", lambda e, i=i, f0=f0, f1=f1, bc_m=bc_m: e.tensor_tensor(
                xv(ctx, "xbf", i, False, f0, f1), sc[:, 0:512], bc_m, ALU.subtract),
                reads=[("scr32", 0), km], writes=[("xbf", i)], tag=f"f0l2_{i}_{c}")
    P.op("ve", lambda e: e.engine_nop(),
         reads=[("wnat",)], writes=[("wnat",)], tag="f0fence")
    xr, xk = ctx.sb["mixr"], ctx.sb["mixk"]
    for c in range(S // TAU):
        f0, f1 = c * TAU, (c + 1) * TAU
        for i in range(NT):
            P.op("ve", lambda e, i=i, f0=f0, f1=f1: e.tensor_scalar(
                xr[:, i * TAU:(i + 1) * TAU], xv(ctx, "xbf", i, False, f0, f1),
                ctx.sb["cols"][:, 0 + i: 1 + i], None, ALU.mult),
                reads=[("xbf", i), ("cols", 0 + i)], writes=[("mixr", i)], tag=f"f0xr{i}_{c}")
            P.op("ve", lambda e, i=i, f0=f0, f1=f1: e.scalar_tensor_tensor(
                xr[:, i * TAU:(i + 1) * TAU], xv(ctx, "xbf", i, True, f0, f1),
                ctx.sb["cols"][:, 8 + i: 9 + i], xr[:, i * TAU:(i + 1) * TAU],
                ALU.mult, ALU.add),
                reads=[("xbf", i), ("cols", 8 + i), ("mixr", i)],
                writes=[("mixr", i)], tag=f"f0xr2{i}_{c}")
            P.op("ve", lambda e, i=i, f0=f0, f1=f1: e.tensor_scalar(
                xk[:, i * TAU:(i + 1) * TAU], xv(ctx, "xbf", i, False, f0, f1),
                ctx.sb["cols"][:, 16 + i: 17 + i], None, ALU.mult),
                reads=[("xbf", i), ("cols", 16 + i)], writes=[("mixk", i)], tag=f"f0xk{i}_{c}")
            P.op("ve", lambda e, i=i, f0=f0, f1=f1: e.scalar_tensor_tensor(
                xk[:, i * TAU:(i + 1) * TAU], xv(ctx, "xbf", i, True, f0, f1),
                ctx.sb["cols"][:, 24 + i: 25 + i], xk[:, i * TAU:(i + 1) * TAU],
                ALU.mult, ALU.add),
                reads=[("xbf", i), ("cols", 24 + i), ("mixk", i)],
                writes=[("mixk", i)], tag=f"f0xk2{i}_{c}")
        for m in range(NT):
            bk = m % 4
            ap = psA(ctx, bk, 128, 0, TAU)
            for i in range(NT):
                P.op("te", lambda e, m=m, i=i, ap=ap: e.matmul(
                    ap, wtile(ctx, "wrT", i, D, m * 128, (m + 1) * 128),
                    xr[:, i * TAU:(i + 1) * TAU],
                    start=(i == 0), stop=(i == NT - 1)),
                    reads=[("wrT", "t"), ("mixr", i)], writes=[RA(bk)],
                    tag=f"f0rm{m}_{i}_{c}")
            P.op("ac", lambda e, m=m, ap=ap: e.activation(
                ctx.sb["wnat"][:, m * TAU:(m + 1) * TAU], ap, AF.Sigmoid),
                reads=[RA(bk)], writes=[("wnat", "rb", m)], tag=f"f0re{m}_{c}")
            ap2 = psB(ctx, m % 4, 128, 0, TAU)
            for i in range(NT):
                P.op("te", lambda e, m=m, i=i, ap2=ap2: e.matmul(
                    ap2, wtile(ctx, "wkT", i, D, m * 128, (m + 1) * 128),
                    xk[:, i * TAU:(i + 1) * TAU],
                    start=(i == 0), stop=(i == NT - 1)),
                    reads=[("wkT", "t"), ("mixk", i)], writes=[RB(m % 4)],
                    tag=f"f0km{m}_{i}_{c}")
            P.op("ac", lambda e, m=m, ap2=ap2: e.activation(
                ctx.sb["wnat"][:, 2048 + m * TAU:2048 + (m + 1) * TAU], ap2, AF.Relu),
                reads=[RB(m % 4)], writes=[("wnat", "ka", m)], tag=f"f0ke{m}_{c}")
            P.op("ac", lambda e, m=m: e.activation(
                ctx.sb["wnat"][:, 2048 + m * TAU:2048 + (m + 1) * TAU],
                ctx.sb["wnat"][:, 2048 + m * TAU:2048 + (m + 1) * TAU], AF.Square),
                reads=[("wnat", "ka", m)], writes=[("wnat", "ka", m)], tag=f"f0ks{m}_{c}")
        for m in range(NT):
            bk = m % 4
            ap = psA(ctx, bk, 128, 0, TAU)
            for i in range(NT):
                P.op("te", lambda e, m=m, i=i, ap=ap: e.matmul(
                    ap, wtile(ctx, "wvT", i, D, m * 128, (m + 1) * 128),
                    ctx.sb["wnat"][:, 2048 + i * TAU:2048 + (i + 1) * TAU],
                    start=(i == 0), stop=(i == NT - 1)),
                    reads=[("wvT", "t"), ("wnat", "ka", i)], writes=[RA(bk)],
                    tag=f"f0vm{m}_{i}_{c}")
            sc = ctx.sb["scr32"]
            sl = m % 2
            P.op("ve", lambda e, m=m, ap=ap, sl=sl: e.tensor_tensor(
                sc[:, sl * 256: sl * 256 + TAU], ap,
                ctx.sb["wnat"][:, m * TAU:(m + 1) * TAU], ALU.mult),
                reads=[RA(bk), ("wnat", "rb", m)], writes=[("scr32", 0)], tag=f"f0o1{m}_{c}")
            P.op("ve", lambda e, m=m, f0=f0, f1=f1, sl=sl: e.tensor_tensor(
                x[:, m * S + f0: m * S + f1], x[:, m * S + f0: m * S + f1],
                sc[:, sl * 256: sl * 256 + TAU], ALU.add),
                reads=[("xcur", m), ("scr32", 0)], writes=[("xcur", m)],
                tag=f"f0o2{m}_{c}")


def emit_l0_ffn_loads(ctx):
    P = ctx.P
    wr = ctx.din("L0_ffn_W_r", (D, D))
    wk = ctx.din("L0_ffn_W_k", (D, D))
    wv = ctx.din("L0_ffn_W_v", (D, D))
    mur = ctx.din("L0_ffn_mu_r", (D,))
    muk = ctx.din("L0_ffn_mu_k", (D,))
    emit_wload(ctx, wr, D, D, "wnat", "ld_wr", "f0lwr")
    emit_wtranspose(ctx, "wnat", D, D, "wrT", "f0twr")
    emit_wload(ctx, wk, D, D, "wnat", "ld_wk", "f0lwk")
    emit_wtranspose(ctx, "wnat", D, D, "wkT", "f0twk")
    emit_wload(ctx, wv, D, D, "wnat", "ld_wv", "f0lwv")
    emit_wtranspose(ctx, "wnat", D, D, "wvT", "f0twv")
    emit_colload(ctx, mur, NT, "cols", 0, "ld_mu", "f0lmr")
    emit_colload(ctx, muk, NT, "cols", 16, "ld_mu", "f0lmk")
    for i in range(NT):
        P.op("ve", lambda e, i=i: e.tensor_scalar(
            ctx.sb["cols"][:, 8 + i: 9 + i], ctx.sb["cols"][:, 0 + i: 1 + i],
            -1.0, 1.0, ALU.mult, ALU.add),
            reads=[("cols", 0 + i)], writes=[("cols", 8 + i)], tag=f"f0c1m{i}")
        P.op("ve", lambda e, i=i: e.tensor_scalar(
            ctx.sb["cols"][:, 24 + i: 25 + i], ctx.sb["cols"][:, 16 + i: 17 + i],
            -1.0, 1.0, ALU.mult, ALU.add),
            reads=[("cols", 16 + i)], writes=[("cols", 24 + i)], tag=f"f0c1k{i}")


def alloc_l0_ffn(ctx):
    TAU = 256
    ctx.sbuf("xbf", (128, NT * (S + 1)), BF16)
    ctx.sbuf("sqc", (128, NT * 256), BF16)
    ctx.sbuf("scr32", (128, 512), F32)
    ctx.sbuf("mixr", (128, NT * TAU), BF16)
    ctx.sbuf("mixk", (128, NT * TAU), BF16)
    ctx.sbuf("wnat", (128, NT * D), BF16)
    ctx.sbuf("wrT", (128, NT * D), BF16)
    ctx.sbuf("wkT", (128, NT * D), BF16)
    ctx.sbuf("wvT", (128, NT * D), BF16)



def emit_l0_attn(ctx):
    P = ctx.P
    x = ctx.sb["xcur"]
    SC = 1.0 / math.sqrt(32.0)
    emit_cast_x_to_bf(ctx, "xbf", tagp="t0cx")
    src_tiles = [((lambda f0, f1, i=i: xv(ctx, "xbf", i, False, f0, f1)),
                  [("xbf", i)]) for i in range(NT)]
    emit_ln_stats(ctx, src_tiles, "sqc", 0, 1, 1, 2, LN_EPS, "t0st")
    P.op("ve", lambda e: e.reciprocal(row32(ctx, 3), row32(ctx, 1)),
         reads=[R32(1)], writes=[R32(3)], tag="t0u")
    P.op("ve", lambda e: e.tensor_scalar(
        row32(ctx, 3), row32(ctx, 3), -LN_EPS, 1.0 + RMS_EPS, ALU.mult, ALU.add),
        reads=[R32(3)], writes=[R32(3)], tag="t0u2")
    P.op("ac", lambda e: e.activation(row32(ctx, 3), row32(ctx, 3), AF.Ln),
         reads=[R32(3)], writes=[R32(3)], tag="t0ln")
    P.op("ac", lambda e: e.activation(row32(ctx, 3), row32(ctx, 3), AF.Exp, scale=-0.5),
         reads=[R32(3)], writes=[R32(3)], tag="t0ex")
    emit_row_pair(ctx, lambda f0, f1: row32(ctx, 1, f0, f1), R32(1),
                  lambda f0, f1: row32(ctx, 2, f0, f1), [R32(2)],
                  lambda f0, f1: row32(ctx, 3, f0, f1), [R32(3)],
                  ALU.mult, "t0g")
    emit_row_pair(ctx, lambda f0, f1: row32(ctx, 3, f0, f1), R32(3),
                  lambda f0, f1: row32(ctx, 0, f0, f1), [R32(0)],
                  lambda f0, f1: row32(ctx, 1, f0, f1), [R32(1)],
                  ALU.mult, "t0mg")
    emit_row_to_dram(ctx, 1, 0, "t0sd0")
    emit_row_to_dram(ctx, 3, 1, "t0sd1")
    for c in range(4):
        f0, f1 = c * 512, (c + 1) * 512
        bc_r, kr = emit_bcast_dram(ctx, 0, 0, f"t0br{c}", f0, f1)
        bc_m, km = emit_bcast_dram(ctx, 1, 1, f"t0bm{c}", f0, f1)
        for i in range(NT):
            sc = ctx.sb["scr32"]
            P.op("ve", lambda e, i=i, f0=f0, f1=f1, bc_r=bc_r: e.tensor_tensor(
                sc[:, 0:512], x[:, i * S + f0: i * S + f1], bc_r, ALU.mult),
                reads=[("xcur", i), kr], writes=[("scr32", 0)], tag=f"t0l1_{i}_{c}")
            P.op("ve", lambda e, i=i, f0=f0, f1=f1, bc_m=bc_m: e.tensor_tensor(
                xv(ctx, "xbf", i, False, f0, f1), sc[:, 0:512], bc_m, ALU.subtract),
                reads=[("scr32", 0), km], writes=[("xbf", i)], tag=f"t0l2_{i}_{c}")
    for i in range(NT):
        P.op("ve", lambda e, i=i: e.tensor_scalar(
            ctx.sb["cols"][:, 8 + i: 9 + i], ctx.sb["cols"][:, 16 + i: 17 + i],
            ctx.sb["cols"][:, 0 + i: 1 + i], None, ALU.mult),
            reads=[("cols", 0 + i), ("cols", 16 + i)], writes=[("cols", 8 + i)],
            tag=f"t0wl{i}a")
        P.op("ve", lambda e, i=i: e.tensor_tensor(
            ctx.sb["cols"][:, 8 + i: 9 + i], ctx.sb["cols"][:, 16 + i: 17 + i],
            ctx.sb["cols"][:, 8 + i: 9 + i], ALU.subtract),
            reads=[("cols", 8 + i), ("cols", 16 + i)], writes=[("cols", 8 + i)],
            tag=f"t0wl{i}b")
        for c in range(4):
            f0, f1 = c * 512, (c + 1) * 512
            P.op("ve", lambda e, i=i, f0=f0, f1=f1: e.tensor_scalar(
                ctx.sb["d1s"][:, 0:512], xv(ctx, "xbf", i, False, f0, f1),
                ctx.sb["cols"][:, 8 + i: 9 + i], None, ALU.mult),
                reads=[("xbf", i), ("cols", 8 + i)], writes=[("d1s",)],
                tag=f"t0d1_{i}_{c}")
            P.op("ve", lambda e, i=i, c=c: e.tensor_tensor_scan(
                ctx.sb["scr32"][:, 0:512],
                ctx.sb["cols"].ap()[:, 0 + i: 1 + i].broadcast_to((128, 512)),
                ctx.sb["d1s"][:, 0:512],
                0.0 if c == 0 else ctx.sb["cols"].ap()[:, 37:38],
                ALU.mult, ALU.add),
                reads=[("d1s",), ("cols", 0 + i), ("cols", 37)],
                writes=[("scr32", 0)], tag=f"t0sc{i}_{c}")
            P.op("ve", lambda e: e.tensor_copy(
                ctx.sb["cols"][:, 37:38], ctx.sb["scr32"][:, 511:512]),
                reads=[("scr32", 0)], writes=[("cols", 37)], tag=f"t0cy{i}_{c}")
            P.op("ac", lambda e, i=i, f0=f0, f1=f1: e.activation(
                ctx.sb["mixed"][:, i * S + f0: i * S + f1],
                ctx.sb["scr32"][:, 0:512], AF.Copy),
                reads=[("scr32", 0)], writes=[("mixed", i)], tag=f"t0se{i}_{c}")
    for i in range(NT):
        P.op("ve", lambda e, i=i: e.reduce_sum(
            ctx.sb["cols"][:, 38 + i: 39 + i], xv(ctx, "xbf", i), axis=AX.X),
            reads=[("xbf", i)], writes=[("cols", 38 + i)], tag=f"t0rm{i}")
        P.op("ve", lambda e, i=i: e.scalar_tensor_tensor(
            ctx.sb["colsb"][:, i: i + 1], ctx.sb["cols"][:, 38 + i: 39 + i],
            1.0 / S, ctx.sb["cols"][:, 16 + i: 17 + i], ALU.mult, ALU.mult),
            reads=[("cols", 38 + i), ("cols", 16 + i)], writes=[("colsb", i)],
            tag=f"t0ri{i}")
    for m4 in range(4):
        for i in range(NT):
            P.op("te", lambda e, m4=m4, i=i: e.matmul(
                psA(ctx, 3, 128, 0, 1),
                wtile(ctx, "rw1", i, 512, m4 * 128, (m4 + 1) * 128),
                ctx.sb["colsb"][:, i: i + 1],
                start=(i == 0), stop=(i == NT - 1)),
                reads=[("rw1",), ("colsb", i)], writes=[RA(3)], tag=f"t0h{m4}_{i}")
        P.op("ac", lambda e, m4=m4: e.activation(
            ctx.sb["colsb"][:, 8 + m4: 9 + m4], psA(ctx, 3, 128, 0, 1), AF.Silu),
            reads=[RA(3)], writes=[("colsb", 8 + m4)], tag=f"t0he{m4}")
    for m4 in range(4):
        P.op("te", lambda e, m4=m4: e.matmul(
            psA(ctx, 3, 1, 0, 3), ctx.sb["colsb"][:, 8 + m4: 9 + m4],
            wtile(ctx, "rw2", m4, 3), start=(m4 == 0), stop=(m4 == 3)),
            reads=[("rw2",), ("colsb", 8 + m4)], writes=[RA(3)], tag=f"t0lg{m4}")
    P.op("ve", lambda e: e.tensor_tensor(
        row32(ctx, 3, 0, 3), psA(ctx, 3, 1, 0, 3), ctx.sb["b2row"][0:1, 0:3],
        ALU.add), reads=[RA(3), ("b2row",)], writes=[R32(3)], tag="t0lgb")
    P.op("ve", lambda e: e.reduce_max(
        row32(ctx, 3, 4, 5), row32(ctx, 3, 0, 3), axis=AX.X, negate=True),
        reads=[R32(3)], writes=[R32(3)], tag="t0sm1")
    P.op("ac", lambda e: e.activation(
        row32(ctx, 3, 0, 3), row32(ctx, 3, 0, 3), AF.Exp,
        bias=row32(ctx, 3, 4, 5)),
        reads=[R32(3)], writes=[R32(3)], tag="t0sm2")
    P.op("ve", lambda e: e.reduce_sum(
        row32(ctx, 3, 4, 5), row32(ctx, 3, 0, 3), axis=AX.X),
        reads=[R32(3)], writes=[R32(3)], tag="t0sm3")
    P.op("ve", lambda e: e.reciprocal(row32(ctx, 3, 4, 5), row32(ctx, 3, 4, 5)),
         reads=[R32(3)], writes=[R32(3)], tag="t0sm4")
    P.op("ve", lambda e: e.tensor_tensor(
        row32(ctx, 3, 5, 6), row32(ctx, 3, 1, 2), row32(ctx, 3, 2, 3), ALU.add),
        reads=[R32(3)], writes=[R32(3)], tag="t0sm5")
    P.op("ve", lambda e: e.tensor_scalar(
        row16(ctx, 1, 0, 1), row32(ctx, 3, 5, 6), row32(ctx, 3, 4, 5), None,
        ALU.mult), reads=[R32(3)], writes=[R16(1)], tag="t0sm6")
    P.op("ve", lambda e: e.tensor_scalar(
        row16(ctx, 1, 1, 2), row32(ctx, 3, 2, 3), row32(ctx, 3, 4, 5), None,
        ALU.mult), reads=[R32(3)], writes=[R16(1)], tag="t0sm7")
    for j, cidx in ((0, 32), (1, 33)):
        P.op("te", lambda e, j=j: e.matmul(
            psA(ctx, 3, 128, 0, 1), ctx.sb["ones_row"][0:1, :],
            row16(ctx, 1, j, j + 1), start=True, stop=True),
            reads=[R16(1), ("ones_row",)], writes=[RA(3)], tag=f"t0pb{j}")
        P.op("ve", lambda e, cidx=cidx: e.tensor_scalar(
            ctx.sb["cols"][:, cidx: cidx + 1], psA(ctx, 3, 128, 0, 1), 1.0,
            None, ALU.mult),
            reads=[RA(3)], writes=[("cols", cidx)], tag=f"t0pe{cidx}")
    for nm, buf in (("wq1", "q1b"), ("wq2", "q2b"), ("wk1", "k1b"), ("wk2", "k2b")):
        for c in range(4):
            f0, f1 = c * 512, (c + 1) * 512
            for i in range(NT):
                P.op("te", lambda e, nm=nm, i=i, f0=f0, f1=f1: e.matmul(
                    psA(ctx, 3, 32, 0, 512), wtile(ctx, nm, i, 32),
                    xv(ctx, "xbf", i, False, f0, f1),
                    start=(i == 0), stop=(i == NT - 1)),
                    reads=[(nm,), ("xbf", i)], writes=[RA(3)], tag=f"t0p{nm}{c}_{i}")
            P.op("ac", lambda e, buf=buf, f0=f0, f1=f1: e.activation(
                ctx.sb[buf][0:32, f0:f1], psA(ctx, 3, 32, 0, 512), AF.Copy),
                reads=[RA(3)], writes=[(buf, c)], tag=f"t0pe{buf}{c}")
    for tt in range(16):
        for i in range(NT):
            P.op("te", lambda e, tt=tt, i=i: e.matmul(
                psA(ctx, 3, 128, 0, 32),
                xv(ctx, "xbf", i, False, tt * 128, (tt + 1) * 128),
                wtile(ctx, "wv", i, 32),
                start=(i == 0), stop=(i == NT - 1)),
                reads=[("wv",), ("xbf", i)], writes=[RA(3)], tag=f"t0v{tt}_{i}")
        P.op("ac", lambda e, tt=tt: e.activation(
            ctx.sb["vtok"][:, tt * 32:(tt + 1) * 32], psA(ctx, 3, 128, 0, 32),
            AF.Copy), reads=[RA(3)], writes=[("vtok", tt)], tag=f"t0ve{tt}")
    for mi, (qb, kb) in enumerate((("q1b", "k1b"), ("q2b", "k2b"))):
        for kt in range(16):
            for qc in range(4):
                bk = (kt * 4 + qc) % 2
                P.op("te", lambda e, kt=kt, qc=qc, qb=qb, kb=kb, bk=bk: e.matmul(
                    psA(ctx, bk, 128, 0, 512),
                    ctx.sb[kb][0:32, kt * 128:(kt + 1) * 128],
                    ctx.sb[qb][0:32, qc * 512:(qc + 1) * 512],
                    start=True, stop=True),
                    reads=[(kb, kt // 4), (qb, qc)], writes=[RA(bk)],
                    tag=f"t0s{mi}m{kt}_{qc}")
                P.op("ve", lambda e, bk=bk: e.reduce_max(
                    ctx.sb["cols"][:, 40 + bk: 41 + bk], psA(ctx, bk, 128, 0, 512),
                    axis=AX.X), reads=[RA(bk)], writes=[("cols", 40 + bk)],
                    tag=f"t0s{mi}r{kt}_{qc}")
                first = (kt == 0 and qc == 0)
                P.op("ve", lambda e, bk=bk, first=first: (
                    e.tensor_copy(ctx.sb["cols"][:, 42:43],
                                  ctx.sb["cols"][:, 40 + bk: 41 + bk]) if first
                    else e.tensor_tensor(
                        ctx.sb["cols"][:, 42:43], ctx.sb["cols"][:, 42:43],
                        ctx.sb["cols"][:, 40 + bk: 41 + bk], ALU.max)),
                    reads=[("cols", 40 + bk), ("cols", 42)],
                    writes=[("cols", 42)], tag=f"t0s{mi}x{kt}_{qc}")
        P.op("ve", lambda e: e.tensor_scalar(
            ctx.sb["colsb"][:, 12:13], ctx.sb["cols"][:, 42:43], 1.0, None,
            ALU.mult), reads=[("cols", 42)], writes=[("colsb", 12)], tag=f"t0g{mi}a")
        P.op("te", lambda e: e.transpose(
            psbf(ctx, "B", 3, 1, 0, 128), ctx.sb["colsb"][:, 12:13],
            ctx.sb["ident"][:, :]),
            reads=[("colsb", 12), ("ident",)], writes=[RB(3)], tag=f"t0g{mi}t")
        P.op("ve", lambda e: e.reduce_max(
            row32(ctx, 3, 8, 9), psbf(ctx, "B", 3, 1, 0, 128), axis=AX.X),
            reads=[RB(3)], writes=[R32(3)], tag=f"t0g{mi}r")
        P.op("ve", lambda e: e.tensor_scalar(
            row32(ctx, 3, 8, 9), row32(ctx, 3, 8, 9), -SC, 30.0, ALU.mult, ALU.add),
            reads=[R32(3)], writes=[R32(3)], tag=f"t0g{mi}b")
        P.op("ve", lambda e: e.tensor_scalar(
            row16(ctx, 1, 4, 5), row32(ctx, 3, 8, 9), 1.0, None, ALU.mult),
            reads=[R32(3)], writes=[R16(1)], tag=f"t0g{mi}c")
        P.op("te", lambda e: e.matmul(
            psA(ctx, 3, 128, 0, 1), ctx.sb["ones_row"][0:1, :],
            row16(ctx, 1, 4, 5), start=True, stop=True),
            reads=[R16(1), ("ones_row",)], writes=[RA(3)], tag=f"t0g{mi}d")
        P.op("ve", lambda e, mi=mi: e.tensor_scalar(
            ctx.sb["cols"][:, 35 + mi: 36 + mi], psA(ctx, 3, 128, 0, 1), 1.0,
            None, ALU.mult),
            reads=[RA(3)], writes=[("cols", 35 + mi)], tag=f"t0g{mi}e")
        for qc in range(4):
            f0, f1 = qc * 512, (qc + 1) * 512
            for kt in range(16):
                bk = kt % 2
                P.op("te", lambda e, kt=kt, qc=qc, qb=qb, kb=kb, bk=bk: e.matmul(
                    psA(ctx, bk, 128, 0, 512),
                    ctx.sb[kb][0:32, kt * 128:(kt + 1) * 128],
                    ctx.sb[qb][0:32, qc * 512:(qc + 1) * 512],
                    start=True, stop=True),
                    reads=[(kb, kt // 4), (qb, qc)], writes=[RA(bk)],
                    tag=f"t0e{mi}m{kt}_{qc}")
                P.op("ac", lambda e, bk=bk, mi=mi: e.activation(
                    ctx.sb["ebuf"][:, bk * 512:(bk + 1) * 512],
                    psA(ctx, bk, 128, 0, 512), AF.Exp, scale=SC,
                    bias=ctx.sb["cols"][:, 35 + mi: 36 + mi]),
                    reads=[RA(bk), ("cols", 35 + mi)], writes=[("ebuf", bk)],
                    tag=f"t0e{mi}e{kt}_{qc}")
                P.op("te", lambda e, kt=kt, bk=bk: e.matmul(
                    psB(ctx, 0, 32, 0, 512), ctx.sb["vtok"][:, kt * 32:(kt + 1) * 32],
                    ctx.sb["ebuf"][:, bk * 512:(bk + 1) * 512],
                    start=(kt == 0), stop=(kt == 15)),
                    reads=[("vtok", kt), ("ebuf", bk)], writes=[RB(0)],
                    tag=f"t0e{mi}n{kt}_{qc}")
                P.op("te", lambda e, kt=kt, bk=bk: e.matmul(
                    psB(ctx, 1, 1, 0, 512), ctx.sb["ones_col"][:, :],
                    ctx.sb["ebuf"][:, bk * 512:(bk + 1) * 512],
                    start=(kt == 0), stop=(kt == 15)),
                    reads=[("ebuf", bk), ("ones_col",)], writes=[RB(1)],
                    tag=f"t0e{mi}d{kt}_{qc}")
            P.op("ve", lambda e: e.tensor_scalar(
                row32(ctx, 3, 16, 16 + 512), psB(ctx, 1, 1, 0, 512), 1.0, 1e-30,
                ALU.mult, ALU.add), reads=[RB(1)], writes=[R32(3)], tag=f"t0d{mi}a{qc}")
            P.op("ve", lambda e: e.reciprocal(
                row32(ctx, 3, 16, 16 + 512), row32(ctx, 3, 16, 16 + 512)),
                reads=[R32(3)], writes=[R32(3)], tag=f"t0d{mi}b{qc}")
            P.op("ve", lambda e: e.tensor_scalar(
                row16(ctx, 1, 16, 16 + 512), row32(ctx, 3, 16, 16 + 512), 1.0,
                None, ALU.mult), reads=[R32(3)], writes=[R16(1)], tag=f"t0d{mi}c{qc}")
            P.op("te", lambda e: e.matmul(
                psB(ctx, 2, 32, 0, 512), ctx.sb["ones_row"][0:1, 0:32],
                row16(ctx, 1, 16, 16 + 512), start=True, stop=True),
                reads=[R16(1), ("ones_row",)], writes=[RB(2)], tag=f"t0d{mi}d{qc}")
            P.op("ac", lambda e: e.activation(
                ctx.sb["nsc"][0:32, 0:512], psB(ctx, 0, 32, 0, 512), AF.Copy),
                reads=[RB(0)], writes=[("nsc", 0)], tag=f"t0d{mi}e{qc}")
            if mi == 0:
                P.op("ve", lambda e, f0=f0, f1=f1: e.tensor_tensor(
                    ctx.sb["ctxb"][0:32, f0:f1], ctx.sb["nsc"][0:32, 0:512],
                    psB(ctx, 2, 32, 0, 512), ALU.mult),
                    reads=[("nsc", 0), RB(2)], writes=[("ctxb", qc)],
                    tag=f"t0d{mi}f{qc}")
            else:
                P.op("ve", lambda e: e.tensor_tensor(
                    ctx.sb["nsc"][0:32, 512:1024], ctx.sb["nsc"][0:32, 0:512],
                    psB(ctx, 2, 32, 0, 512), ALU.mult),
                    reads=[("nsc", 0), RB(2)], writes=[("nsc", 1)],
                    tag=f"t0d{mi}f{qc}")
                P.op("ve", lambda e, f0=f0, f1=f1: e.scalar_tensor_tensor(
                    ctx.sb["ctxb"][0:32, f0:f1], ctx.sb["nsc"][0:32, 512:1024],
                    ctx.sb["cols"][0:32, 34:35], ctx.sb["ctxb"][0:32, f0:f1],
                    ALU.mult, ALU.add),
                    reads=[("nsc", 1), ("cols", 34), ("ctxb", qc)],
                    writes=[("ctxb", qc)], tag=f"t0d{mi}g{qc}")
    for m in range(NT):
        for c in range(4):
            bk = (m * 4 + c) % 2
            P.op("te", lambda e, m=m, c=c, bk=bk: e.matmul(
                psA(ctx, bk, 128, 0, 512), ctx.sb["wo"][0:32, m * 128:(m + 1) * 128],
                ctx.sb["ctxb"][0:32, c * 512:(c + 1) * 512],
                start=True, stop=True),
                reads=[("wo",), ("ctxb", c)], writes=[RA(bk)], tag=f"t0o{m}_{c}")
            P.op("ve", lambda e, m=m, c=c, bk=bk: e.scalar_tensor_tensor(
                ctx.sb["mixed"][:, m * S + c * 512: m * S + (c + 1) * 512],
                psA(ctx, bk, 128, 0, 512), ctx.sb["cols"][:, 32:33],
                ctx.sb["mixed"][:, m * S + c * 512: m * S + (c + 1) * 512],
                ALU.mult, ALU.add),
                reads=[RA(bk), ("cols", 32), ("mixed", m)], writes=[("mixed", m)],
                tag=f"t0oe{m}_{c}")
    for r in range(8):
        pap = psbf(ctx, "B", 3, 128, 0, 64)
        P.op("te", lambda e, r=r, pap=pap: e.transpose(
            pap, ctx.sb["lm"][0:64, r * 128:(r + 1) * 128],
            ctx.sb["ident"][0:64, 0:64]),
            reads=[("lm",), ("ident",)], writes=[RB(3)], tag=f"t0lt{r}")
        P.op("ac", lambda e, r=r, pap=pap: e.activation(
            ctx.sb["lmT"][:, r * 64:(r + 1) * 64], pap, AF.Copy),
            reads=[RB(3)], writes=[("lmT", r)], tag=f"t0le{r}")
        P.op("ve", lambda e, r=r: e.tensor_scalar(
            ctx.sb["lmT"][:, r * 64:(r + 1) * 64],
            ctx.sb["lmT"][:, r * 64:(r + 1) * 64],
            ctx.sb["cols"][:, 16 + r: 17 + r], None, ALU.mult),
            reads=[("lmT", r), ("cols", 16 + r)], writes=[("lmT", r)], tag=f"t0lf{r}")
    for r in range(8):
        P.op("te", lambda e, r=r: e.matmul(
            psA(ctx, 3, 64, 0, 1), ctx.sb["lmT"][:, r * 64:(r + 1) * 64],
            xv(ctx, "xbf", r, False, 2047, 2048),
            start=(r == 0), stop=(r == NT - 1)),
            reads=[("lmT", r), ("xbf", r)], writes=[RA(3)], tag=f"t0ls{r}")
    P.op("ve", lambda e: e.tensor_scalar(
        ctx.sb["colsb"][0:64, 13:14], psA(ctx, 3, 64, 0, 1), 1.0, None, ALU.mult),
        reads=[RA(3)], writes=[("colsb", 13)], tag="t0lse")
    P.op("te", lambda e: e.transpose(
        psbf(ctx, "B", 3, 1, 0, 64), ctx.sb["colsb"][0:64, 13:14],
        ctx.sb["ident"][0:64, 0:64]),
        reads=[("colsb", 13), ("ident",)], writes=[RB(3)], tag="t0lst")
    P.op("ve", lambda e: e.tensor_scalar(
        row32(ctx, 3, 16, 16 + 64), psbf(ctx, "B", 3, 1, 0, 64), 1.0 / 32.0,
        None, ALU.mult), reads=[RB(3)], writes=[R32(3)], tag="t0lsr")
    P.op("ve", lambda e: e.reduce_max(
        row32(ctx, 3, 0, 1), row32(ctx, 3, 16, 16 + 64), axis=AX.X, negate=True),
        reads=[R32(3)], writes=[R32(3)], tag="t0lm1")
    P.op("ac", lambda e: e.activation(
        row32(ctx, 3, 16, 16 + 64), row32(ctx, 3, 16, 16 + 64), AF.Exp,
        bias=row32(ctx, 3, 0, 1)),
        reads=[R32(3)], writes=[R32(3)], tag="t0lm2")
    P.op("ve", lambda e: e.reduce_sum(
        row32(ctx, 3, 0, 1), row32(ctx, 3, 16, 16 + 64), axis=AX.X),
        reads=[R32(3)], writes=[R32(3)], tag="t0lm3")
    P.op("ve", lambda e: e.reciprocal(row32(ctx, 3, 0, 1), row32(ctx, 3, 0, 1)),
         reads=[R32(3)], writes=[R32(3)], tag="t0lm4")
    P.op("ve", lambda e: e.tensor_scalar(
        row16(ctx, 1, 16, 16 + 64), row32(ctx, 3, 16, 16 + 64),
        row32(ctx, 3, 0, 1), None, ALU.mult),
        reads=[R32(3)], writes=[R16(1)], tag="t0lm5")
    P.op("te", lambda e: e.transpose(
        psbf(ctx, "B", 3, 64, 0, 1), row16(ctx, 1, 16, 16 + 64),
        ctx.sb["ident"][0:1, 0:1]),
        reads=[R16(1), ("ident",)], writes=[RB(3)], tag="t0lm6")
    P.op("ac", lambda e: e.activation(
        ctx.sb["colsb"][0:64, 14:15], psbf(ctx, "B", 3, 64, 0, 1), AF.Copy),
        reads=[RB(3)], writes=[("colsb", 14)], tag="t0lm7")
    for r in range(8):
        P.op("te", lambda e, r=r: e.matmul(
            psA(ctx, 3, 128, 0, 1), ctx.sb["lm"][0:64, r * 128:(r + 1) * 128],
            ctx.sb["colsb"][0:64, 14:15], start=True, stop=True),
            reads=[("lm",), ("colsb", 14)], writes=[RA(3)], tag=f"t0ar{r}")
        P.op("ve", lambda e, r=r: e.scalar_tensor_tensor(
            ctx.sb["mixed"][:, r * S + 2047: r * S + 2048],
            psA(ctx, 3, 128, 0, 1), ctx.sb["cols"][:, 33:34],
            ctx.sb["mixed"][:, r * S + 2047: r * S + 2048], ALU.mult, ALU.add),
            reads=[RA(3), ("cols", 33), ("mixed", r)], writes=[("mixed", r)],
            tag=f"t0ae{r}")
    P.op("ve", lambda e: e.engine_nop(),
         reads=[("q1b",), ("q2b",), ("k1b",), ("k2b",), ("ctxb",), ("ebuf",),
                ("nsc",), ("vtok",)] +
               [("q1b", c) for c in range(4)] + [("q2b", c) for c in range(4)] +
               [("k1b", c) for c in range(4)] + [("k2b", c) for c in range(4)] +
               [("ctxb", c) for c in range(4)] + [("ebuf", b) for b in range(2)] +
               [("nsc", j) for j in range(2)] + [("vtok", t) for t in range(16)],
         writes=[("mixo",)], tag="t0fence1")
    P.op("ve", lambda e: e.engine_nop(),
         reads=[("rw1",)], writes=[("mwb",)], tag="t0fence2")
    for m in range(NT):
        P.dma("gp", ctx.sb["mwb"][:, (m % 2) * 2048:(m % 2) * 2048 + 2048]
              .rearrange("p (r c) -> p r c", r=16),
              ctx.dram["L0_attn_mix_w"][:, m * 128:(m + 1) * 128]
              .rearrange("(r p) c -> p r c", p=128),
              sem=f"ld_mw{m % 2}", writes=[("mwb", m % 2)], tag=f"t0mw{m}")
        for j in range(8):
            P.op("ve", lambda e, m=m, j=j: e.tensor_scalar(
                ctx.sb["mwb"][:, (m % 2) * 2048 + (8 + j) * 128:
                              (m % 2) * 2048 + (9 + j) * 128],
                ctx.sb["mwb"][:, (m % 2) * 2048 + (8 + j) * 128:
                              (m % 2) * 2048 + (9 + j) * 128],
                ctx.sb["cols"][:, 16 + j: 17 + j], None, ALU.mult),
                reads=[("mwb", m % 2), ("cols", 16 + j)], writes=[("mwb", m % 2)],
                tag=f"t0mf{m}_{j}")
        for c in range(4):
            bk = 2 + (m * 4 + c) % 2
            f0, f1 = c * 512, (c + 1) * 512
            for j in range(16):
                rkey = ("mixed", j) if j < 8 else ("xbf", j - 8)
                P.op("te", lambda e, m=m, c=c, j=j, bk=bk, f0=f0, f1=f1: e.matmul(
                    psA(ctx, bk, 128, 0, 512),
                    ctx.sb["mwb"][:, (m % 2) * 2048 + j * 128:(m % 2) * 2048 + (j + 1) * 128],
                    (ctx.sb["mixed"][:, j * S + f0: j * S + f1] if j < 8
                     else xv(ctx, "xbf", j - 8, False, f0, f1)),
                    start=(j == 0), stop=(j == 15)),
                    reads=[("mwb", m % 2), rkey], writes=[RA(bk)],
                    tag=f"t0mm{m}_{c}_{j}")
            P.op("ac", lambda e, m=m, bk=bk, f0=f0, f1=f1: e.activation(
                ctx.sb["mixo"][:, m * S + f0: m * S + f1], psA(ctx, bk, 128, 0, 512),
                AF.Copy), reads=[RA(bk)], writes=[("mixo", m)], tag=f"t0me{m}_{c}")
    for c in range(8):
        f0, f1 = c * 256, (c + 1) * 256
        for i in range(NT):
            P.op("ac", lambda e, i=i, f0=f0, f1=f1: e.activation(
                ctx.sb["sqc"][:, i * 256:(i + 1) * 256],
                ctx.sb["mixo"][:, i * S + f0: i * S + f1], AF.Square),
                reads=[("mixo", i)], writes=[("sqc", i)], tag=f"t0q{c}_{i}")
            P.op("te", lambda e, i=i: e.matmul(
                psA(ctx, 1, 1, 0, 256), ctx.sb["ones_col"][:, :],
                ctx.sb["sqc"][:, i * 256:(i + 1) * 256],
                start=(i == 0), stop=(i == NT - 1)),
                reads=[("sqc", i), ("ones_col",)], writes=[RA(1)], tag=f"t0qm{c}_{i}")
        P.op("ve", lambda e, f0=f0, f1=f1: e.tensor_scalar(
            row32(ctx, 3, f0, f1), psA(ctx, 1, 1, 0, 256), 1.0 / D, RMS_EPS,
            ALU.mult, ALU.add),
            reads=[RA(1)], writes=[R32(3)], tag=f"t0qe{c}")
    P.op("ac", lambda e: e.activation(row32(ctx, 3), row32(ctx, 3), AF.Ln),
         reads=[R32(3)], writes=[R32(3)], tag="t0q1")
    P.op("ac", lambda e: e.activation(row32(ctx, 3), row32(ctx, 3), AF.Exp, scale=-0.5),
         reads=[R32(3)], writes=[R32(3)], tag="t0q2")
    emit_row_pair(ctx, lambda f0, f1: row32(ctx, 1, f0, f1), R32(1),
                  lambda f0, f1: row32(ctx, 0, f0, f1), [R32(0)],
                  lambda f0, f1: row32(ctx, 2, f0, f1), [R32(2)],
                  ALU.mult, "t0fmr")
    emit_row_to_dram(ctx, 2, 0, "t0fs0")
    emit_row_to_dram(ctx, 1, 1, "t0fs1")
    emit_row_to_dram(ctx, 3, 2, "t0fs2")
    for c in range(4):
        f0, f1 = c * 512, (c + 1) * 512
        bc_r, kr = emit_bcast_dram(ctx, 0, 0, f"t0fe{c}", f0, f1)
        bc_m, km = emit_bcast_dram(ctx, 1, 1, f"t0ff{c}", f0, f1)
        for i in range(NT):
            sc = ctx.sb["scr32"]
            P.op("ve", lambda e, i=i, f0=f0, f1=f1, bc_r=bc_r: e.tensor_tensor(
                sc[:, 0:512], x[:, i * S + f0: i * S + f1], bc_r, ALU.mult),
                reads=[("xcur", i), kr], writes=[("scr32", 0)], tag=f"t0fh{i}_{c}")
            P.op("ve", lambda e, i=i, bc_m=bc_m: e.tensor_tensor(
                sc[:, 0:512], sc[:, 0:512], bc_m, ALU.subtract),
                reads=[("scr32", 0), km], writes=[("scr32", 0)], tag=f"t0fi{i}_{c}")
            P.op("ve", lambda e, i=i, f0=f0, f1=f1: e.tensor_tensor(
                x[:, i * S + f0: i * S + f1], x[:, i * S + f0: i * S + f1],
                sc[:, 0:512], ALU.add),
                reads=[("xcur", i), ("scr32", 0)], writes=[("xcur", i)],
                tag=f"t0fk0{i}_{c}")
        bc_s, ks_ = emit_bcast_dram(ctx, 2, 0, f"t0fg{c}", f0, f1)
        for i in range(NT):
            P.op("ve", lambda e, i=i, f0=f0, f1=f1, bc_s=bc_s: e.tensor_tensor(
                ctx.sb["d1s"][:, 0:512], ctx.sb["mixo"][:, i * S + f0: i * S + f1],
                bc_s, ALU.mult),
                reads=[("mixo", i), ks_], writes=[("d1s",)], tag=f"t0fj{i}_{c}")
            P.op("ve", lambda e, i=i, f0=f0, f1=f1: e.scalar_tensor_tensor(
                x[:, i * S + f0: i * S + f1], ctx.sb["d1s"][:, 0:512],
                ctx.sb["cols"][:, 24 + i: 25 + i], x[:, i * S + f0: i * S + f1],
                ALU.mult, ALU.add),
                reads=[("d1s",), ("cols", 24 + i), ("xcur", i)],
                writes=[("xcur", i)], tag=f"t0fl{i}_{c}")
    P.op("ve", lambda e: e.engine_nop(),
         reads=[], writes=[("mixed", j) for j in range(NT)] + [("buse",)],
         tag="t0busfence")
    P.dma("gp", ctx.sb["buse"][:, :], ctx.dram["L0_attn_bus_expand"],
          sem="ld_be", writes=[("buse",)], tag="t0lbe")
    for i in range(NT):
        P.op("ve", lambda e, i=i: e.reduce_sum(
            ctx.sb["cols"][:, 38 + i: 39 + i], x[:, i * S:(i + 1) * S], axis=AX.X),
            reads=[("xcur", i)], writes=[("cols", 38 + i)], tag=f"t0bm{i}")
        P.op("ve", lambda e, i=i: e.tensor_scalar(
            ctx.sb["colsb"][:, i: i + 1], ctx.sb["cols"][:, 38 + i: 39 + i],
            1.0 / S, None, ALU.mult),
            reads=[("cols", 38 + i)], writes=[("colsb", i)], tag=f"t0bb{i}")
    for i in range(NT):
        P.op("te", lambda e, i=i: e.matmul(
            psA(ctx, 3, 128, 0, 1), wtile(ctx, "busc", i, 128),
            ctx.sb["colsb"][:, i: i + 1], start=(i == 0), stop=(i == NT - 1)),
            reads=[("busc",), ("colsb", i)], writes=[RA(3)], tag=f"t0bs{i}")
    P.op("ve", lambda e: e.tensor_scalar(
        ctx.sb["colsb"][:, 12:13], psA(ctx, 3, 128, 0, 1), 1.0, None, ALU.mult),
        reads=[RA(3)], writes=[("colsb", 12)], tag="t0bse")
    P.op("te", lambda e: e.matmul(
        psA(ctx, 3, 128, 0, 1), ctx.sb["busp"][:, :], ctx.sb["colsb"][:, 12:13],
        start=True, stop=True),
        reads=[("busp",), ("colsb", 12)], writes=[RA(3)], tag="t0bp")
    P.op("ve", lambda e: e.tensor_scalar(
        ctx.sb["colsb"][:, 13:14], psA(ctx, 3, 128, 0, 1), 1.0, None, ALU.mult),
        reads=[RA(3)], writes=[("colsb", 13)], tag="t0bpe")
    for m in range(NT):
        P.op("te", lambda e, m=m: e.matmul(
            psA(ctx, 3, 128, 0, 1), ctx.sb["buse"][:, m * 128:(m + 1) * 128],
            ctx.sb["colsb"][:, 13:14], start=True, stop=True),
            reads=[("buse",), ("colsb", 13)], writes=[RA(3)], tag=f"t0be{m}")
        P.op("ve", lambda e, m=m: e.tensor_scalar(
            ctx.sb["cols"][:, 46:47], psA(ctx, 3, 128, 0, 1), 0.1, None, ALU.mult),
            reads=[RA(3)], writes=[("cols", 46)], tag=f"t0bee{m}")
        P.op("ve", lambda e, m=m: e.tensor_scalar(
            x[:, m * S:(m + 1) * S], x[:, m * S:(m + 1) * S],
            ctx.sb["cols"][:, 46:47], None, ALU.add),
            reads=[("xcur", m), ("cols", 46)], writes=[("xcur", m)], tag=f"t0ba{m}")


def emit_l0_attn_loads(ctx):
    P = ctx.P
    d = ctx.din
    d("L0_attn_w_norm_in", (D,))
    d("L0_attn_w_norm_out", (D,))
    d("L0_attn_router_w1", (D, 512))
    d("L0_attn_router_w2", (512, 3))
    d("L0_attn_router_b2", (3,))
    d("L0_attn_lambdas", (8,))
    d("L0_attn_slr_wq1", (D, 32))
    d("L0_attn_slr_wq2", (D, 32))
    d("L0_attn_slr_wk1", (D, 32))
    d("L0_attn_slr_wk2", (D, 32))
    d("L0_attn_slr_wv", (D, 32))
    d("L0_attn_slr_wo", (32, D))
    d("L0_attn_slr_lam", (1,))
    d("L0_attn_landmarks", (64, D))
    d("L0_attn_mix_w", (2 * D, D))
    d("L0_attn_bus_compress", (D, 128))
    d("L0_attn_bus_expand", (128, D))
    d("L0_attn_bus_publish", (128, 128))
    for i in range(NT):
        P.dma("gp", ctx.sb["cols"][:, 0 + i: 1 + i],
              ctx.dram["L0_attn_lambdas"][i:i + 1].unsqueeze(0).partition_broadcast(128),
              sem="ld_lam", writes=[("cols", 0 + i)], tag=f"t0llam{i}")
        P.dma("gp", ctx.sb["cols"][:, 16 + i: 17 + i],
              ctx.dram["L0_attn_w_norm_in"][i * 128:(i + 1) * 128].unsqueeze(1),
              sem="ld_wn", writes=[("cols", 16 + i)], tag=f"t0lwi{i}")
        P.dma("gp", ctx.sb["cols"][:, 24 + i: 25 + i],
              ctx.dram["L0_attn_w_norm_out"][i * 128:(i + 1) * 128].unsqueeze(1),
              sem="ld_wn", writes=[("cols", 24 + i)], tag=f"t0lwo{i}")
    P.dma("gp", ctx.sb["cols"][:, 34:35],
          ctx.dram["L0_attn_slr_lam"][0:1].unsqueeze(0).partition_broadcast(128),
          sem="ld_sl", writes=[("cols", 34)], tag="t0lsl")
    P.op("ve", lambda e: e.tensor_scalar(
        ctx.sb["cols"][:, 34:35], ctx.sb["cols"][:, 34:35], -1.0, None, ALU.mult),
        reads=[("cols", 34)], writes=[("cols", 34)], tag="t0lsln")
    emit_wload(ctx, ctx.dram["L0_attn_router_w1"], D, 512, "rw1", "ld_rw1", "t0lrw1")
    emit_wload(ctx, ctx.dram["L0_attn_router_w2"], 512, 3, "rw2", "ld_rw2", "t0lrw2")
    P.dma("gp", ctx.sb["b2row"][0:1, 0:3], ctx.dram["L0_attn_router_b2"].unsqueeze(0),
          sem="ld_b2", writes=[("b2row",)], tag="t0lb2")
    for nm, dn in (("wq1", "L0_attn_slr_wq1"), ("wq2", "L0_attn_slr_wq2"),
                   ("wk1", "L0_attn_slr_wk1"), ("wk2", "L0_attn_slr_wk2"),
                   ("wv", "L0_attn_slr_wv")):
        emit_wload(ctx, ctx.dram[dn], D, 32, nm, "ld_" + nm, "t0l" + nm)
        for i in range(NT):
            P.op("ve", lambda e, nm=nm, i=i: e.tensor_scalar(
                wtile(ctx, nm, i, 32), wtile(ctx, nm, i, 32),
                ctx.sb["cols"][:, 16 + i: 17 + i], None, ALU.mult),
                reads=[(nm,), ("cols", 16 + i)], writes=[(nm,)], tag=f"t0f{nm}{i}")
    P.dma("gp", ctx.sb["wo"][0:32, :], ctx.dram["L0_attn_slr_wo"],
          sem="ld_wo", writes=[("wo",)], tag="t0lwo2")
    P.dma("gp", ctx.sb["lm"][0:64, :], ctx.dram["L0_attn_landmarks"],
          sem="ld_lm", writes=[("lm",)], tag="t0llm")
    emit_wload(ctx, ctx.dram["L0_attn_bus_compress"], D, 128, "busc", "ld_bc", "t0lbc")
    P.dma("gp", ctx.sb["busp"][:, :], ctx.dram["L0_attn_bus_publish"],
          sem="ld_bp2", writes=[("busp",)], tag="t0lbp")


def alloc_l0_attn(ctx):
    ctx.sbuf("xbf", (128, NT * (S + 1)), BF16)
    ctx.sbuf("sqc", (128, NT * 256), BF16)
    ctx.sbuf("scr32", (128, 512), F32)
    ctx.sbuf("d1s", (128, 512), BF16)
    ctx.sbuf("mixed", (128, NT * S), BF16)
    pool = ctx.sbuf("pool0", (128, NT * S), BF16)
    pa = pool.ap()
    ctx.sb["q1b"] = pa[:, 0:2048]
    ctx.sb["q2b"] = pa[:, 2048:4096]
    ctx.sb["k1b"] = pa[:, 4096:6144]
    ctx.sb["k2b"] = pa[:, 6144:8192]
    ctx.sb["ctxb"] = pa[:, 8192:10240]
    ctx.sb["nsc"] = pa[:, 10240:11264]
    ctx.sb["ebuf"] = pa[:, 11264:12288]
    ctx.sb["vtok"] = pa[:, 12288:12800]
    ctx.sb["mixo"] = pa
    ctx.sbuf("rw1", (128, 2 * 2048), BF16)
    ctx.sb["mwb"] = ctx.sb["rw1"].ap()[:, 0:4096]
    ctx.sbuf("rw2", (128, 4 * 3), BF16)
    ctx.sbuf("b2row", (1, 4), F32)
    ctx.sbuf("wq1", (128, NT * 32), BF16)
    ctx.sbuf("wq2", (128, NT * 32), BF16)
    ctx.sbuf("wk1", (128, NT * 32), BF16)
    ctx.sbuf("wk2", (128, NT * 32), BF16)
    ctx.sbuf("wv", (128, NT * 32), BF16)
    ctx.sbuf("wo", (32, D), BF16)
    ctx.sbuf("lm", (64, D), BF16)
    ctx.sbuf("lmT", (128, NT * 64), BF16)
    ctx.sbuf("busc", (128, NT * 128), BF16)
    ctx.sbuf("busp", (128, 128), BF16)
    ctx.sb["buse"] = ctx.sb["mixed"].ap()[:, 0:1024]



def emit_phi(ctx, buf_name, key, ap, scr_name, skey, tagp):
    P = ctx.P
    scr = ctx.sb[scr_name]
    w = ap.shape[1]
    P.op("ve", lambda e: e.tensor_scalar(
        scr[:, 0:w], ap, 0.0, None, ALU.min),
        reads=[key], writes=[(scr_name, skey)], tag=tagp + "m")
    P.op("ve", lambda e: e.tensor_tensor(
        ap, ap, scr[:, 0:w], ALU.subtract),
        reads=[key, (scr_name, skey)], writes=[key], tag=tagp + "r")
    P.op("ac", lambda e: e.activation(
        scr[:, 0:w], scr[:, 0:w], AF.Exp),
        reads=[(scr_name, skey)], writes=[(scr_name, skey)], tag=tagp + "e")
    P.op("ve", lambda e: e.tensor_tensor(
        ap, ap, scr[:, 0:w], ALU.add),
        reads=[key, (scr_name, skey)], writes=[key], tag=tagp + "a")


def _aob(ctx):
    return ctx.sb["xcur"].ap().bitcast(BF16)[:, 0:NT * S]


def _pbb(ctx):
    return ctx.sb["xcur"].ap().bitcast(BF16)[:, NT * S:2 * NT * S]


def emit_l1_attn(ctx):
    P = ctx.P
    x = ctx.sb["xcur"]
    emit_cast_x_to_bf(ctx, "xbf", tagp="a1cx")
    src_tiles = [((lambda f0, f1, i=i: xv(ctx, "xbf", i, False, f0, f1)),
                  [("xbf", i)]) for i in range(NT)]
    emit_ln_stats(ctx, src_tiles, "sqc", 0, 1, 1, 2, LN_EPS, "a1st")
    emit_row_pair(ctx, lambda f0, f1: row32(ctx, 3, f0, f1), R32(3),
                  lambda f0, f1: row32(ctx, 0, f0, f1), [R32(0)],
                  lambda f0, f1: row32(ctx, 2, f0, f1), [R32(2)],
                  ALU.mult, "a1mr")
    emit_row_to_dram(ctx, 2, 0, "a1sd0")
    emit_row_to_dram(ctx, 3, 1, "a1sd1")
    for c in range(4):
        f0, f1 = c * 512, (c + 1) * 512
        bc_r, kr = emit_bcast_dram(ctx, 0, 0, f"a1br{c}", f0, f1)
        bc_m, km = emit_bcast_dram(ctx, 1, 1, f"a1bm{c}", f0, f1)
        for i in range(NT):
            sc = ctx.sb["scr32"]
            P.op("ve", lambda e, i=i, f0=f0, f1=f1, bc_r=bc_r: e.tensor_tensor(
                sc[:, 0:512], x[:, i * S + f0: i * S + f1], bc_r, ALU.mult),
                reads=[("xcur", i), kr], writes=[("scr32", 0)], tag=f"a1l1_{i}_{c}")
            P.op("ve", lambda e, i=i, f0=f0, f1=f1, bc_m=bc_m: e.tensor_tensor(
                xv(ctx, "xbf", i, False, f0, f1), sc[:, 0:512], bc_m, ALU.subtract),
                reads=[("scr32", 0), km], writes=[("xbf", i)], tag=f"a1l2_{i}_{c}")
    P.op("ve", lambda e: e.engine_nop(),
         reads=[("xcur", i) for i in range(NT)] + [("xcur",)],
         writes=[("xcur",), ("aobuf",), ("pbuf",)], tag="a1xfence")
    for g in range(2):
        for part, r0 in (("q", 4 * g), ("k", 8 + 4 * g), ("v", 16 + 4 * g)):
            P.dma("gp", ctx.sb["wgrp"][:, :].rearrange("p (r c) -> p r c", r=4),
                  ctx.dram["L1_attn_W_qkv"][r0 * 128:(r0 + 4) * 128, :]
                  .rearrange("(r p) c -> p r c", p=128),
                  sem="ld_wq", writes=[("wgrp",)], tag=f"a1lw{part}{g}")
            if part != "q":
                P.dma("gp", row16(ctx, 1, 512, 1024),
                      ctx.dram["L1_attn_b_qkv"][r0 * 128:(r0 + 4) * 128].unsqueeze(0),
                      sem="ld_br", writes=[R16(1)], tag=f"a1lbr{part}{g}")
            for r in range(4):
                for cc in range(8):
                    pap = psbf(ctx, "B", 3, 128, 0, 128)
                    P.op("te", lambda e, r=r, cc=cc, pap=pap: e.transpose(
                        pap, ctx.sb["wgrp"][:, r * D + cc * 128:r * D + (cc + 1) * 128],
                        ctx.sb["ident"][:, :]),
                        reads=[("wgrp",), ("ident",)], writes=[RB(3)],
                        tag=f"a1t{part}{g}_{r}_{cc}")
                    dap = ctx.sb["wgT"][:, cc * 512 + r * 128: cc * 512 + (r + 1) * 128]
                    P.op("ac", lambda e, dap=dap, pap=pap: e.activation(dap, pap, AF.Copy),
                         reads=[RB(3)], writes=[("wgT",)], tag=f"a1te{part}{g}_{r}_{cc}")
            if part == "q":
                for sub in range(4):
                    for c in range(4):
                        bk = (sub * 4 + c) % 2
                        ap = psA(ctx, bk, 128, 0, 512)
                        for i in range(NT):
                            P.op("te", lambda e, sub=sub, c=c, i=i, ap=ap: e.matmul(
                                ap, ctx.sb["wgT"][:, i * 512 + sub * 128:
                                                  i * 512 + (sub + 1) * 128],
                                xv(ctx, "xbf", i, False, c * 512, (c + 1) * 512),
                                start=(i == 0), stop=(i == NT - 1)),
                                reads=[("wgT",), ("xbf", i)], writes=[RA(bk)],
                                tag=f"a1qm{g}_{sub}_{c}_{i}")
                        for half in range(2):
                            hl = 2 * sub + half
                            P.op("ac", lambda e, c=c, hl=hl, half=half, bk=bk, sub=sub: e.activation(
                                ctx.sb["qbuf"][0:64, hl * S + c * 512: hl * S + (c + 1) * 512],
                                ctx.sb["psA"][half * 64:(half + 1) * 64,
                                              bk * 512:(bk + 1) * 512],
                                AF.Identity,
                                bias=ctx.sb["cols"][half * 64:(half + 1) * 64,
                                                    r0 + sub: r0 + sub + 1]),
                                reads=[RA(bk), ("cols", r0 + sub)],
                                writes=[("qbuf", hl, c)], tag=f"a1qe{g}_{hl}_{c}")
                for hl in range(8):
                    for c in range(4):
                        qap = ctx.sb["qbuf"][0:64, hl * S + c * 512: hl * S + (c + 1) * 512]
                        P.op("ve", lambda e, qap=qap: e.tensor_scalar(
                            ctx.sb["scr32"][0:64, 0:512], qap, 0.0, None, ALU.min),
                            reads=[("qbuf", hl, c)], writes=[("scr32", 0)],
                            tag=f"a1qp1{g}_{hl}_{c}")
                        P.op("ve", lambda e, qap=qap: e.tensor_tensor(
                            qap, qap, ctx.sb["scr32"][0:64, 0:512], ALU.subtract),
                            reads=[("qbuf", hl, c), ("scr32", 0)], writes=[("qbuf", hl, c)],
                            tag=f"a1qp2{g}_{hl}_{c}")
                        P.op("ac", lambda e: e.activation(
                            ctx.sb["scr32"][0:64, 0:512], ctx.sb["scr32"][0:64, 0:512], AF.Exp),
                            reads=[("scr32", 0)], writes=[("scr32", 0)], tag=f"a1qp3{g}_{hl}_{c}")
                        P.op("ve", lambda e, qap=qap: e.tensor_tensor(
                            qap, qap, ctx.sb["scr32"][0:64, 0:512], ALU.add),
                            reads=[("qbuf", hl, c), ("scr32", 0)], writes=[("qbuf", hl, c)],
                            tag=f"a1qp4{g}_{hl}_{c}")
            else:
                dname = "kbuf" if part == "k" else "vbuf"
                dst = ctx.sb[dname]
                for tt in range(16):
                    bk = tt % 2
                    ap = psA(ctx, bk, 128, 0, 512)
                    for i in range(NT):
                        P.op("te", lambda e, tt=tt, i=i, ap=ap: e.matmul(
                            ap, xv(ctx, "xbf", i, False, tt * 128, (tt + 1) * 128),
                            ctx.sb["wgT"][:, i * 512:(i + 1) * 512],
                            start=(i == 0), stop=False),
                            reads=[("wgT",), ("xbf", i)], writes=[RA(bk)],
                            tag=f"a1{part}m{g}_{tt}_{i}")
                    P.op("te", lambda e, tt=tt, ap=ap: e.matmul(
                        ap, ctx.sb["ones_row"][0:1, :],
                        row16(ctx, 1, 512, 1024),
                        start=False, stop=True),
                        reads=[("ones_row",), R16(1)], writes=[RA(bk)],
                        tag=f"a1{part}b{g}_{tt}")
                    P.op("ac", lambda e, tt=tt, ap=ap, dst=dst: e.activation(
                        dst[:, tt * 512:(tt + 1) * 512], ap, AF.Copy),
                        reads=[RA(bk)], writes=[(dname, tt)], tag=f"a1{part}e{g}_{tt}")
                    if part == "k":
                        for opn in range(1):
                            kap = dst[:, tt * 512:(tt + 1) * 512]
                            P.op("ve", lambda e, kap=kap: e.tensor_scalar(
                                ctx.sb["scr32"][:, 0:512], kap, 0.0, None, ALU.min),
                                reads=[(dname, tt)], writes=[("scr32", 0)],
                                tag=f"a1kp1{g}_{tt}")
                            P.op("ve", lambda e, kap=kap: e.tensor_tensor(
                                kap, kap, ctx.sb["scr32"][:, 0:512], ALU.subtract),
                                reads=[(dname, tt), ("scr32", 0)], writes=[(dname, tt)],
                                tag=f"a1kp2{g}_{tt}")
                            P.op("ac", lambda e: e.activation(
                                ctx.sb["scr32"][:, 0:512], ctx.sb["scr32"][:, 0:512], AF.Exp),
                                reads=[("scr32", 0)], writes=[("scr32", 0)], tag=f"a1kp3{g}_{tt}")
                            P.op("ve", lambda e, kap=kap: e.tensor_tensor(
                                kap, kap, ctx.sb["scr32"][:, 0:512], ALU.add),
                                reads=[(dname, tt), ("scr32", 0)], writes=[(dname, tt)],
                                tag=f"a1kp4{g}_{tt}")
        for tt in range(16):
            P.op("te", lambda e, tt=tt: e.matmul(
                psB(ctx, 2, 1, 0, 512), ctx.sb["ones_col"][:, :],
                ctx.sb["kbuf"][:, tt * 512:(tt + 1) * 512],
                start=(tt == 0), stop=(tt == 15)),
                reads=[("kbuf", tt), ("ones_col",)], writes=[RB(2)], tag=f"a1k1m{g}_{tt}")
        P.op("ac", lambda e: e.activation(
            row16(ctx, 1, 0, 512), psB(ctx, 2, 1, 0, 512), AF.Copy),
            reads=[RB(2)], writes=[R16(1)], tag=f"a1k1e{g}")
        for hl in range(8):
            h = 8 * g + hl
            for tt in range(16):
                P.op("te", lambda e, hl=hl, tt=tt: e.matmul(
                    psB(ctx, 0, 64, 0, 64),
                    ctx.sb["kbuf"][:, tt * 512 + hl * 64: tt * 512 + (hl + 1) * 64],
                    ctx.sb["vbuf"][:, tt * 512 + hl * 64: tt * 512 + (hl + 1) * 64],
                    start=(tt == 0), stop=(tt == 15)),
                    reads=[("kbuf", tt), ("vbuf", tt)], writes=[RB(0)],
                    tag=f"a1kv{g}_{hl}_{tt}")
            P.op("ac", lambda e, hl=hl: e.activation(
                ctx.sb["kvb"][0:64, hl * 64:(hl + 1) * 64],
                psB(ctx, 0, 64, 0, 64), AF.Copy),
                reads=[RB(0)], writes=[("kvb", hl)], tag=f"a1kve{g}_{hl}")
            P.op("te", lambda e, hl=hl: e.transpose(
                psbf(ctx, "B", 1, 64, 0, 1),
                row16(ctx, 1, hl * 64, (hl + 1) * 64), ctx.sb["ident"][0:1, 0:1]),
                reads=[R16(1), ("ident",)], writes=[RB(1)], tag=f"a1k1t{g}_{hl}")
            P.op("ac", lambda e, hl=hl: e.activation(
                ctx.sb["k1c"][0:64, hl: hl + 1],
                psbf(ctx, "B", 1, 64, 0, 1), AF.Copy),
                reads=[RB(1)], writes=[("k1c", hl)], tag=f"a1k1ce{g}_{hl}")
            for c in range(4):
                qap = ctx.sb["qbuf"][0:64, hl * S + c * 512: hl * S + (c + 1) * 512]
                P.op("te", lambda e, c=c, hl=hl, qap=qap: e.matmul(
                    psA(ctx, c % 2, 64, 0, 512),
                    ctx.sb["kvb"][0:64, hl * 64:(hl + 1) * 64],
                    qap, start=True, stop=True),
                    reads=[("kvb", hl), ("qbuf", hl, c)], writes=[RA(c % 2)],
                    tag=f"a1om{g}_{hl}_{c}")
                P.op("te", lambda e, c=c, hl=hl, qap=qap: e.matmul(
                    psB(ctx, 2, 1, 0, 512),
                    ctx.sb["k1c"][0:64, hl: hl + 1],
                    qap, start=True, stop=True),
                    reads=[("k1c", hl), ("qbuf", hl, c)], writes=[RB(2)],
                    tag=f"a1dm{g}_{hl}_{c}")
                P.op("ac", lambda e, c=c: e.activation(
                    ctx.sb["ohbuf"][0:64, c * 512:(c + 1) * 512],
                    psA(ctx, c % 2, 64, 0, 512), AF.Copy),
                    reads=[RA(c % 2)], writes=[("ohbuf", c)], tag=f"a1oe{g}_{hl}_{c}")
                P.op("ve", lambda e, c=c: e.tensor_scalar(
                    row32(ctx, 3, c * 512, (c + 1) * 512),
                    psB(ctx, 2, 1, 0, 512), 1e-6, None, ALU.max),
                    reads=[RB(2)], writes=[R32(3)], tag=f"a1de{g}_{hl}_{c}")
            P.op("ve", lambda e: e.reciprocal(row32(ctx, 3), row32(ctx, 3)),
                 reads=[R32(3)], writes=[R32(3)], tag=f"a1rc{g}_{hl}")
            P.op("ve", lambda e: e.tensor_scalar(
                row16(ctx, 0), row32(ctx, 3), 1.0, None, ALU.mult),
                reads=[R32(3)], writes=[R16(0)], tag=f"a1rcb{g}_{hl}")
            for c in range(4):
                P.op("te", lambda e, c=c: e.matmul(
                    psB(ctx, c % 2, 64, 0, 512), ctx.sb["ones_row"][0:1, 0:64],
                    row16(ctx, 0, c * 512, (c + 1) * 512), start=True, stop=True),
                    reads=[R16(0), ("ones_row",)], writes=[RB(c % 2)],
                    tag=f"a1rb{g}_{hl}_{c}")
                P.op("ve", lambda e, c=c: e.tensor_tensor(
                    ctx.sb["ohbuf"][0:64, c * 512:(c + 1) * 512],
                    ctx.sb["ohbuf"][0:64, c * 512:(c + 1) * 512],
                    psB(ctx, c % 2, 64, 0, 512), ALU.mult),
                    reads=[("ohbuf", c), RB(c % 2)], writes=[("ohbuf", c)],
                    tag=f"a1os{g}_{hl}_{c}")
            ohv = ctx.sb["ohbuf"].ap().rearrange("p (j m) -> p m j", m=16)
            for m in range(16):
                dt_ = _aob(ctx)
                P.op("ve", lambda e, m=m, h=h, ohv=ohv, dt_=dt_: e.tensor_copy(
                    dt_[(m % 2) * 64:(m % 2) * 64 + 64,
                        (m // 2) * S + h * 128:(m // 2) * S + (h + 1) * 128],
                    ohv[0:64, m, :]),
                    reads=[("ohbuf", c) for c in range(4)],
                    writes=[("aobuf", m // 2, h)], tag=f"a1sc{g}_{hl}_{m}")
    P.op("ve", lambda e: e.engine_nop(),
         reads=[("kbuf", tt) for tt in range(16)] + [("vbuf", tt) for tt in range(16)],
         writes=[("kbuf",), ("vbuf",)], tag="a1fence")
    P.dma("gp", ctx.sb["kbuf"][:, 0:NT * D].rearrange("p (r c) -> p r c", r=NT),
          ctx.dram["L1_attn_W_proj"].rearrange("(r p) c -> p r c", p=128),
          sem="ld_wp", writes=[("kbuf",)], tag="a1lwp")
    for r in range(NT):
        for cc in range(NT):
            pap = psbf(ctx, "B", 3, 128, 0, 128)
            P.op("te", lambda e, r=r, cc=cc, pap=pap: e.transpose(
                pap, ctx.sb["kbuf"][:, r * D + cc * 128:r * D + (cc + 1) * 128],
                ctx.sb["ident"][:, :]),
                reads=[("kbuf",), ("ident",)], writes=[RB(3)], tag=f"a1tp{r}_{cc}")
            dap = ctx.sb["vbuf"][:, cc * D + r * 128: cc * D + (r + 1) * 128]
            P.op("ac", lambda e, dap=dap, pap=pap: e.activation(dap, pap, AF.Copy),
                 reads=[RB(3)], writes=[("vbuf",)], tag=f"a1tpe{r}_{cc}")
    for m in range(NT):
        for c in range(4):
            bk = (m * 4 + c) % 2
            ap = psA(ctx, bk, 128, 0, 512)
            for i in range(NT):
                P.op("te", lambda e, m=m, c=c, i=i, ap=ap: e.matmul(
                    ap, ctx.sb["vbuf"][:, i * D + m * 128:i * D + (m + 1) * 128],
                    _aob(ctx)[:, i * S + c * 512:(i * S) + (c + 1) * 512],
                    start=(i == 0), stop=(i == NT - 1)),
                    reads=[("vbuf",)] + [("aobuf", i, h) for h in range(16)],
                    writes=[RA(bk)], tag=f"a1pm{m}_{c}_{i}")
            P.op("ac", lambda e, m=m, c=c, ap=ap: e.activation(
                _pbb(ctx)[:, m * S + c * 512: m * S + (c + 1) * 512], ap,
                AF.Identity, bias=ctx.sb["cols"][:, 24 + m: 25 + m]),
                reads=[RA(bk), ("cols", 24 + m)], writes=[("pbuf", m)],
                tag=f"a1pe{m}_{c}")
    src2 = [((lambda f0, f1, i=i: _pbb(ctx)[:, i * S + f0: i * S + f1]),
             [("pbuf", i)]) for i in range(NT)]
    emit_ln_stats(ctx, src2, "sqc", 0, 1, 1, 2, LN_EPS, "a2st")
    emit_row_pair(ctx, lambda f0, f1: row32(ctx, 3, f0, f1), R32(3),
                  lambda f0, f1: row32(ctx, 0, f0, f1), [R32(0)],
                  lambda f0, f1: row32(ctx, 2, f0, f1), [R32(2)],
                  ALU.mult, "a2mr")
    emit_row_to_dram(ctx, 2, 0, "a2sd0")
    emit_row_to_dram(ctx, 3, 1, "a2sd1")
    for i in range(NT):
        P.op("ve", lambda e, i=i: e.tensor_scalar(
            xv(ctx, "xbf", i), xv(ctx, "xbf", i),
            ctx.sb["cols"][:, 32 + i: 33 + i], None, ALU.add),
            reads=[("xbf", i), ("cols", 32 + i)], writes=[("xbf", i)], tag=f"a2xb{i}")
    xsrc = ctx.x_src_dram
    xdst = ctx.x_dst_dram
    P.op("ve", lambda e: e.engine_nop(),
         reads=[], writes=[("wgrp",), ("xck", 0), ("xos", 0)], tag="a2wgfence")
    for c in range(4):
        f0, f1 = c * 512, (c + 1) * 512
        bc_r, kr = emit_bcast_dram(ctx, 0, 0, f"a2br{c}", f0, f1)
        bc_m, km = emit_bcast_dram(ctx, 1, 1, f"a2bm{c}", f0, f1)
        for i in range(NT):
            sl = 0
            xck = ctx.sb["xck"][:, 0:512]
            xos = ctx.sb["xos"][:, 0:512]
            P.dma("sp", xck, xsrc[i * 128:(i + 1) * 128, f0:f1],
                  sem="ld_xc", reads=[("xpk",)], writes=[("xck", sl)],
                  tag=f"a2lx{i}_{c}")
            sc = ctx.sb["scr32"]
            P.op("ve", lambda e, i=i, f0=f0, bc_r=bc_r: e.tensor_tensor(
                sc[:, 0:512], _pbb(ctx)[:, i * S + f0: i * S + f0 + 512],
                bc_r, ALU.mult),
                reads=[("pbuf", i), kr], writes=[("scr32", 0)], tag=f"a2c1_{i}_{c}")
            P.op("ve", lambda e, i=i, bc_m=bc_m: e.tensor_tensor(
                sc[:, 0:512], sc[:, 0:512], bc_m, ALU.subtract),
                reads=[("scr32", 0), km], writes=[("scr32", 0)], tag=f"a2c2_{i}_{c}")
            P.op("ve", lambda e, i=i, f0=f0: e.scalar_tensor_tensor(
                sc[:, 0:512], sc[:, 0:512], ctx.sb["cols"][:, 40 + i: 41 + i],
                xv(ctx, "xbf", i, False, f0, f0 + 512), ALU.mult, ALU.add),
                reads=[("scr32", 0), ("cols", 40 + i), ("xbf", i)],
                writes=[("scr32", 0)], tag=f"a2c3_{i}_{c}")
            P.op("ve", lambda e, xos=xos, xck=xck: e.tensor_tensor(
                xos, xck, sc[:, 0:512], ALU.add),
                reads=[("xck", sl), ("scr32", 0)], writes=[("xos", sl)],
                tag=f"a2c4_{i}_{c}")
            P.dma("sp", xdst[i * 128:(i + 1) * 128, f0:f1], xos,
                  sem="st_xc", reads=[("xos", sl)], writes=[("xdst", i, c)],
                  tag=f"a2sx{i}_{c}", store=True)
    ctx.handled_store = True


def emit_l1_attn_loads(ctx):
    P = ctx.P
    ctx.din("L1_attn_W_qkv", (3 * D, D))
    ctx.din("L1_attn_W_proj", (D, D))
    bq = ctx.din("L1_attn_b_qkv", (3 * D,))
    bp = ctx.din("L1_attn_b_proj", (D,))
    ga = ctx.din("L1_attn_gamma", (D,))
    be = ctx.din("L1_attn_beta", (D,))
    for r in range(24):
        P.dma("gp", ctx.sb["cols"][:, r: r + 1],
              bq[r * 128:(r + 1) * 128].unsqueeze(1),
              sem="ld_bq", writes=[("cols", r)], tag=f"a1lbq{r}")
    for r in range(NT):
        P.dma("gp", ctx.sb["cols"][:, 24 + r: 25 + r],
              bp[r * 128:(r + 1) * 128].unsqueeze(1),
              sem="ld_bp", writes=[("cols", 24 + r)], tag=f"a1lbp{r}")
        P.dma("gp", ctx.sb["cols"][:, 32 + r: 33 + r],
              be[r * 128:(r + 1) * 128].unsqueeze(1),
              sem="ld_bt", writes=[("cols", 32 + r)], tag=f"a1lbe{r}")
        P.dma("gp", ctx.sb["cols"][:, 40 + r: 41 + r],
              ga[r * 128:(r + 1) * 128].unsqueeze(1),
              sem="ld_ga", writes=[("cols", 40 + r)], tag=f"a1lga{r}")


def alloc_l1_attn(ctx):
    ctx.sbuf("xbf", (128, NT * (S + 1)), BF16)
    ctx.sbuf("sqc", (128, NT * 128), BF16)
    ctx.sbuf("scr32", (128, 512), F32)
    ctx.sbuf("qbuf", (64, 8 * S), BF16)
    ctx.sbuf("kbuf", (128, 16 * 512), BF16)
    ctx.sbuf("vbuf", (128, 16 * 512), BF16)
    ctx.sbuf("wgrp", (128, 4 * D), BF16)
    ctx.sb["xck"] = ctx.sb["wgrp"].ap().bitcast(F32)[:, 0:512]
    ctx.sb["xos"] = ctx.sb["wgrp"].ap().bitcast(F32)[:, 512:1024]
    ctx.sbuf("wgT", (128, NT * 512), BF16)
    ctx.sbuf("ohbuf", (64, S), BF16)
    ctx.sbuf("kvb", (64, 8 * 64), BF16)
    ctx.sb["k1c"] = ctx.sb["colsb"].ap()[0:64, 0:8]



A_CHUNKS = [(at * 128, min((at + 1) * 128, 2730)) for at in range(22)]


def _l1f_stats(ctx, tagp, shifted_mask, murstd=False):
    P = ctx.P
    src_tiles = [((lambda f0, f1, i=i: xv(ctx, "xbf", i, shifted_mask(i), f0, f1)),
                  [("xbf", i)]) for i in range(NT)]
    emit_ln_stats(ctx, src_tiles, "sqc", 0, 1, 1, 2, LN_EPS, tagp)
    emit_row_to_dram(ctx, 2, 0, tagp + "sd0")
    if murstd:
        emit_row_pair(ctx, lambda f0, f1: row32(ctx, 3, f0, f1), R32(3),
                      lambda f0, f1: row32(ctx, 0, f0, f1), [R32(0)],
                      lambda f0, f1: row32(ctx, 2, f0, f1), [R32(2)],
                      ALU.mult, tagp + "mr")
        emit_row_to_dram(ctx, 3, 1, tagp + "sd1")
    P.op("ve", lambda e: e.tensor_scalar(
        row16(ctx, 1), row32(ctx, 0), -1.0, None, ALU.mult),
        reads=[R32(0)], writes=[R16(1)], tag=tagp + "nm")


def emit_l1_ffn_layer(ctx, li, ffn_too=True):
    P = ctx.P
    x = ctx.sb["xcur"]
    L = f"L{li}"
    dq = ctx.dram["L1_ffn_Wqkv"]
    dwo = ctx.dram["L1_ffn_Wo"]
    dw1 = ctx.dram["L1_ffn_W1"]
    dw2 = ctx.dram["L1_ffn_W2"]

    P.op("ve", lambda e: e.engine_nop(),
         reads=[("w1a",), ("w1g",), ("w2c",), ("actb",), ("gel",)] +
               [("w1a", sl) for sl in range(2)] + [("w1g", sl) for sl in range(2)] +
               [("w2c", sl) for sl in range(2)] +
               [("actb", at) for at in range(22)],
         writes=[("qg",), ("kg",), ("vg",), ("aobuf",), ("wg",), ("wog",),
                 ("ebuf",), ("pp",)],
         tag=L + "fence0")
    emit_cast_x_to_bf(ctx, "xbf", tagp=L + "cx")
    for nm, cbase, roff in (("q_scale", 100, 0), ("k_scale", 101, 64)):
        dd = ctx.dram["L1_ffn_" + nm]
        for hb in range(2):
            P.dma("gp", ctx.sb["cols"][hb * 64:(hb + 1) * 64, cbase: cbase + 1],
                  dd[li].unsqueeze(1), sem="ld_qs",
                  writes=[("cols", cbase)], tag=f"{L}l{nm}{hb}")
        P.dma("gp", row32(ctx, 3, 256 + roff, 256 + roff + 64),
              dd[li].unsqueeze(0), sem="ld_qs2", writes=[("rowp32", 3)],
              tag=f"{L}l{nm}r")
    P.op("ve", lambda e: e.reduce_max(
        row32(ctx, 3, 384, 385), row32(ctx, 3, 256, 320), axis=AX.X,
        apply_absolute_value=True),
        reads=[("rowp32", 3)], writes=[("rowp32", 3)], tag=L + "cb1")
    P.op("ve", lambda e: e.reduce_max(
        row32(ctx, 3, 385, 386), row32(ctx, 3, 320, 384), axis=AX.X,
        apply_absolute_value=True),
        reads=[("rowp32", 3)], writes=[("rowp32", 3)], tag=L + "cb2")
    P.op("ve", lambda e: e.tensor_tensor(
        row32(ctx, 3, 384, 385), row32(ctx, 3, 384, 385),
        row32(ctx, 3, 385, 386), ALU.mult),
        reads=[("rowp32", 3)], writes=[("rowp32", 3)], tag=L + "cb3")
    P.op("ve", lambda e: e.tensor_scalar(
        row32(ctx, 3, 384, 385), row32(ctx, 3, 384, 385), -8.0, 70.0,
        ALU.mult, ALU.add),
        reads=[("rowp32", 3)], writes=[("rowp32", 3)], tag=L + "cb4")
    P.op("ve", lambda e: e.tensor_scalar(
        row32(ctx, 3, 384, 385), row32(ctx, 3, 384, 385), 40.0, None,
        ALU.min),
        reads=[("rowp32", 3)], writes=[("rowp32", 3)], tag=L + "cb5")
    P.op("ve", lambda e: e.tensor_scalar(
        row16(ctx, 0, 768, 769), row32(ctx, 3, 384, 385), 1.0, None,
        ALU.mult),
        reads=[("rowp32", 3)], writes=[("rowp16", 0)], tag=L + "cb6")
    P.op("te", lambda e: e.matmul(
        psA(ctx, 2, 128, 0, 1), ctx.sb["ones_row"][0:1, :],
        row16(ctx, 0, 768, 769), start=True, stop=True),
        reads=[("rowp16", 0), ("ones_row",)], writes=[RA(2)], tag=L + "cb7")
    P.op("ve", lambda e: e.tensor_scalar(
        ctx.sb["cols"][:, 102:103], psA(ctx, 2, 128, 0, 1), 1.0, None, ALU.mult),
        reads=[RA(2)], writes=[("cols", 102)], tag=L + "cb8")
    for g in range(2):
        P.op("ve", lambda e: e.engine_nop(),
             reads=[], writes=[("pp",), ("aobuf",), ("wg",), ("ebuf",), ("wog",)],
             tag=f"{L}pf{g}")
        for part, coff in (("q", 512 * g), ("k", 1024 + 512 * g)):
            dst = ctx.sb["qg" if part == "q" else "kg"]
            dname = "qg" if part == "q" else "kg"
            for sub in range(4):
                for h in range(2):
                    P.dma("gp", ctx.sb["wg"][:, h * 512:(h + 1) * 512]
                          .rearrange("p (r c) -> p r c", r=4),
                          dq[li][h * 512:(h + 1) * 512,
                                 coff + sub * 128: coff + (sub + 1) * 128]
                          .rearrange("(r p) c -> p r c", p=128),
                          sem=f"ld_wg{h}", writes=[("wg", h)],
                          tag=f"{L}lw{part}{g}_{sub}_{h}")
                for c in range(4):
                    bk = (sub * 4 + c) % 2
                    ap = psA(ctx, bk, 128, 0, 512)
                    for i in range(NT):
                        P.op("te", lambda e, c=c, i=i, ap=ap: e.matmul(
                            ap, ctx.sb["wg"][:, i * 128:(i + 1) * 128],
                            xv(ctx, "xbf", i, i >= 4, c * 512, (c + 1) * 512),
                            start=(i == 0), stop=(i == NT - 1)),
                            reads=[("wg", i // 4), ("xbf", i)], writes=[RA(bk)],
                            tag=f"{L}{part}m{g}_{sub}_{c}_{i}")
                    P.op("ac", lambda e, sub=sub, c=c, ap=ap, dst=dst: e.activation(
                        dst[:, sub * S + c * 512: sub * S + (c + 1) * 512], ap, AF.Copy),
                        reads=[RA(bk)], writes=[(dname, sub, c)],
                        tag=f"{L}{part}e{g}_{sub}_{c}")
            an, bn = ("rot_aq", "rot_bq") if part == "q" else ("rot_ak", "rot_bk")
            scol = 100 if part == "q" else 101

            def _penv(sub, c, dst=dst, dname=dname):
                pb = (sub * 4 + c) % 2
                ib2, sb2 = (2, 3) if pb == 0 else (0, 1)
                pp = ctx.sb["prep2"]
                return dict(
                    sub=sub, c=c, pb=pb, ib2=ib2, sb2=sb2,
                    sq=pp[:, pb * 1536: pb * 1536 + 512],
                    rs=pp[:, pb * 1536 + 512: pb * 1536 + 1024],
                    sc2=pp[:, pb * 1536 + 1024: pb * 1536 + 1536],
                    iv=pp[0:2, pb * 1536: pb * 1536 + 512],
                    Ksq=("pp", pb, 0), Krs=("pp", pb, 1), Ksc=("pp", pb, 2),
                    ch=dst[:, sub * S + c * 512: sub * S + (c + 1) * 512])

            def _prepA(E, part=part, g=g, dname=dname):
                sub = E["sub"]; c = E["c"]; ch = E["ch"]; sq = E["sq"]
                ib2 = E["ib2"]; Ksq = E["Ksq"]
                P.op("ac", lambda e, ch=ch, sq=sq: e.activation(
                    sq, ch, AF.Square),
                    reads=[(dname, sub, c)], writes=[Ksq],
                    tag=f"{L}{part}s{g}_{sub}_{c}")
                P.op("te", lambda e, sq=sq, ib2=ib2: e.matmul(
                    psB(ctx, ib2, 2, 0, 512), ctx.sb["blk2"][:, 0:2],
                    sq, start=True, stop=True),
                    reads=[Ksq, ("blk2",)], writes=[RB(ib2)],
                    tag=f"{L}{part}n{g}_{sub}_{c}")
                P.op("ac", lambda e, ib2=ib2: e.activation(
                    psB(ctx, ib2, 2, 0, 512), psB(ctx, ib2, 2, 0, 512), AF.Ln),
                    reads=[RB(ib2)], writes=[RB(ib2)], tag=f"{L}{part}i1{g}_{sub}_{c}")
                P.op("ac", lambda e, ib2=ib2: e.activation(
                    psB(ctx, ib2, 2, 0, 512), psB(ctx, ib2, 2, 0, 512),
                    AF.Exp, scale=-0.5),
                    reads=[RB(ib2)], writes=[RB(ib2)], tag=f"{L}{part}i2{g}_{sub}_{c}")

            def _prepB(E, part=part, g=g, dname=dname, an=an, bn=bn, scol=scol):
                sub = E["sub"]; c = E["c"]; ch = E["ch"]; iv = E["iv"]
                rs = E["rs"]; sc2 = E["sc2"]; ib2 = E["ib2"]; sb2 = E["sb2"]
                Ksq = E["Ksq"]; Krs = E["Krs"]; Ksc = E["Ksc"]
                P.op("ve", lambda e, iv=iv, ib2=ib2: e.tensor_scalar(
                    iv, psB(ctx, ib2, 2, 0, 512),
                    1e12, None, ALU.min),
                    reads=[RB(ib2)], writes=[Ksq], tag=f"{L}{part}i5{g}_{sub}_{c}")
                P.op("te", lambda e, iv=iv, ib2=ib2: e.matmul(
                    psB(ctx, ib2, 128, 0, 512), ctx.sb["blk2t"][0:2, :],
                    iv, start=True, stop=True),
                    reads=[Ksq, ("blk2t",)], writes=[RB(ib2)],
                    tag=f"{L}{part}i6{g}_{sub}_{c}")
                P.op("ve", lambda e, ch=ch, scol=scol: e.tensor_scalar(
                    ch, ch, ctx.sb["cols"][:, scol: scol + 1], None, ALU.mult),
                    reads=[(dname, sub, c), ("cols", scol)],
                    writes=[(dname, sub, c)], tag=f"{L}{part}qs{g}_{sub}_{c}")
                P.op("te", lambda e, ch=ch, sb2=sb2: e.matmul(
                    psB(ctx, sb2, 128, 0, 512), ctx.sb["p_sigma"][:, :], ch,
                    start=True, stop=True),
                    reads=[(dname, sub, c), ("p_sigma",)], writes=[RB(sb2)],
                    tag=f"{L}{part}g{g}_{sub}_{c}")
                P.op("ve", lambda e, ch=ch, c=c, an=an, rs=rs: e.tensor_tensor(
                    rs, ch,
                    ctx.sb[an][:, c * 512:(c + 1) * 512], ALU.mult),
                    reads=[(dname, sub, c), (an,)], writes=[Krs],
                    tag=f"{L}{part}r1{g}_{sub}_{c}")
                P.op("ve", lambda e, c=c, bn=bn, sc2=sc2, sb2=sb2: e.tensor_tensor(
                    sc2, psB(ctx, sb2, 128, 0, 512),
                    ctx.sb[bn][:, c * 512:(c + 1) * 512], ALU.mult),
                    reads=[RB(sb2), (bn,)], writes=[Ksc],
                    tag=f"{L}{part}r2{g}_{sub}_{c}")
                P.op("ve", lambda e, rs=rs, sc2=sc2: e.tensor_tensor(
                    rs, rs, sc2, ALU.add),
                    reads=[Krs, Ksc], writes=[Krs],
                    tag=f"{L}{part}r3{g}_{sub}_{c}")
                P.op("ve", lambda e, ch=ch, rs=rs, ib2=ib2: e.scalar_tensor_tensor(
                    ch, rs, 1.0,
                    psB(ctx, ib2, 128, 0, 512), ALU.mult, ALU.mult),
                    reads=[Krs, RB(ib2)],
                    writes=[(dname, sub, c)], tag=f"{L}{part}r4{g}_{sub}_{c}")

            penvs = [_penv(sub, c) for sub in range(4) for c in range(4)]
            for i in range(len(penvs) + 1):
                if i < len(penvs):
                    _prepA(penvs[i])
                if i > 0:
                    _prepB(penvs[i - 1])
        if getattr(ctx, "l1f_dumpqk", False) and g == 0:
            for nm2 in ("qg", "kg"):
                P.dma("gp", ctx.dram["dump_" + nm2],
                      ctx.sb[nm2][:, :].rearrange("p (r c) -> p r c", r=4),
                      sem="st_d" + nm2,
                      reads=[(nm2, sb2, c2) for sb2 in range(4) for c2 in range(4)],
                      tag="dmp" + nm2, store=True)
        for ws in range(4):
            for h in range(2):
                P.dma("gp", ctx.sb["wg"][:, h * 512:(h + 1) * 512]
                      .rearrange("p (r c) -> p r c", r=4),
                      dq[li][h * 512:(h + 1) * 512,
                             2048 + 512 * g + ws * 128: 2048 + 512 * g + (ws + 1) * 128]
                      .rearrange("(r p) c -> p r c", p=128),
                      sem=f"ld_wg{h}", writes=[("wg", h)],
                      tag=f"{L}lwv{g}_{ws}_{h}")
            for tt in range(16):
                bk = tt % 4
                ap = ctx.sb["psA"][0:128, bk * 512: bk * 512 + 128]
                for i in range(NT):
                    P.op("te", lambda e, tt=tt, i=i, ap=ap: e.matmul(
                        ap, xv(ctx, "xbf", i, i >= 4, tt * 128, (tt + 1) * 128),
                        ctx.sb["wg"][:, i * 128:(i + 1) * 128],
                        start=(i == 0), stop=(i == NT - 1)),
                        reads=[("wg", i // 4), ("xbf", i)], writes=[RA(bk)],
                        tag=f"{L}vm{g}_{ws}_{tt}_{i}")
                P.op("ac", lambda e, ws=ws, tt=tt, ap=ap: e.activation(
                    ctx.sb["vg"][:, tt * 512 + ws * 128: tt * 512 + (ws + 1) * 128],
                    ap, AF.Copy),
                    reads=[RA(bk)], writes=[("vg", tt)],
                    tag=f"{L}ve{g}_{ws}_{tt}")
        if getattr(ctx, "l1f_dumpqk", False) and g == 0:
            P.dma("gp", ctx.dram["dump_vg"],
                  ctx.sb["vg"][:, :].rearrange("p (r c) -> p r c", r=16),
                  sem="st_dvg", reads=[("vg", t2) for t2 in range(16)],
                  tag="dmpvg", store=True)
        P.op("ve", lambda e: e.engine_nop(),
             reads=[], writes=[("pp",), ("aobuf",), ("wg",), ("ebuf",), ("wog",)],
             tag=f"{L}af{g}")
        def _env(hl, ib):
            bb = (hl % 2) * 64
            tl = hl // 2
            i0 = ib * 256
            p2 = (hl * 8 + ib) % 2
            bA = 0 if p2 == 0 else 2
            bPV = 0 if p2 == 0 else 2
            bDN = 1 if p2 == 0 else 3
            ebsrc = ctx.sb["ebuf"] if p2 == 0 else ctx.sb["wg"]
            ebk0 = 0 if p2 == 0 else 2
            rrow = 3 if p2 == 0 else 2
            r16b = 0 if p2 == 0 else 1
            nscap = (ctx.sb["nsc"][0:64, 0:256] if p2 == 0
                     else ctx.sb["rsc"].ap()[0:64, 256:512])
            nk = ("rsc", p2)
            segs = []
            for ks in range(4):
                jt = 2 * ib - 2 + ks
                if jt < 0:
                    continue
                if ks == 0:
                    segs.append((ks, jt, (0, 128), 0, 0, "m_up", 0))
                elif ks == 1:
                    segs.append((ks, jt, (0, 256), 0, 128, "m_up", 128))
                elif ks == 2:
                    segs.append((ks, jt, (0, 256), 1, 0, "m_low", 0))
                else:
                    segs.append((ks, jt, (128, 256), 1, 256, "m_low", 0))
            used_banks = sorted(set(sg[3] for sg in segs))
            first_in_bank = {}
            last_in_bank = {}
            for sg in segs:
                first_in_bank.setdefault(sg[3], sg[0])
                last_in_bank[sg[3]] = sg[0]
            return dict(hl=hl, ib=ib, bb=bb, tl=tl, i0=i0, bA=bA, bPV=bPV,
                        bDN=bDN, ebsrc=ebsrc, ebk0=ebk0, rrow=rrow, r16b=r16b,
                        nscap=nscap, nk=nk, segs=segs, used_banks=used_banks,
                        first_in_bank=first_in_bank, last_in_bank=last_in_bank)

        def _stageA(E):
            hl = E["hl"]; ib = E["ib"]; bb = E["bb"]; tl = E["tl"]; i0 = E["i0"]
            bA = E["bA"]; ebsrc = E["ebsrc"]; ebk0 = E["ebk0"]
            segs = E["segs"]; used_banks = E["used_banks"]
            first_in_bank = E["first_in_bank"]; last_in_bank = E["last_in_bank"]
            for (ks, jt, (ir0, ir1), bank, boff, mk, moff) in segs:
                w = ir1 - ir0
                P.op("te", lambda e, ks=ks, jt=jt, ir0=ir0, w=w,
                     bank=bank + bA,
                     boff=boff, hl=hl, tl=tl, bb=bb, i0=i0,
                     st=(first_in_bank[bank] == ks): e.matmul(
                    ctx.sb["psA"][0:128, bank * 512 + boff: bank * 512 + boff + w],
                    ctx.sb["kg"][bb:bb + 64, tl * S + jt * 128: tl * S + (jt + 1) * 128],
                    ctx.sb["qg"][bb:bb + 64, tl * S + i0 + ir0: tl * S + i0 + ir0 + w],
                    start=st, stop=False),
                    reads=[("kg", tl, jt // 4), ("qg", tl, (i0 + ir0) // 512),
                           ("qg", tl, (i0 + ir1 - 1) // 512)],
                    writes=[RA(bank + bA)], tag=f"{L}sm{g}_{hl}_{ib}_{ks}")
            for (ks, jt, (ir0, ir1), bank, boff, mk, moff) in segs:
                P.op("te", lambda e, bank=bank + bA, boff=boff, moff=moff,
                     mk=mk, sp=(last_in_bank[bank] == ks): e.matmul(
                    ctx.sb["psA"][0:128, bank * 512 + boff + moff:
                                  bank * 512 + boff + moff + 128],
                    ctx.sb[mk][:, :], ctx.sb["ident"][:, :],
                    start=False, stop=sp),
                    reads=[(mk,), ("ident",)], writes=[RA(bank + bA)],
                    tag=f"{L}mk{g}_{hl}_{ib}_{ks}")
            for bank in used_banks:
                P.op("ac", lambda e, bank=bank, ebsrc=ebsrc, bA=bA: e.activation(
                    ebsrc[:, bank * 384:(bank + 1) * 384],
                    ctx.sb["psA"][0:128, (bank + bA) * 512: (bank + bA) * 512 + 384],
                    AF.Exp, scale=8.0, bias=ctx.sb["cols"][:, 102:103]),
                    reads=[RA(bank + bA), ("cols", 102)],
                    writes=[("ebuf", ebk0 + bank)],
                    tag=f"{L}ex{g}_{hl}_{ib}_{bank}")
        def _stageB(E):
            hl = E["hl"]; ib = E["ib"]; bb = E["bb"]; tl = E["tl"]; i0 = E["i0"]
            bPV = E["bPV"]; bDN = E["bDN"]; ebsrc = E["ebsrc"]; ebk0 = E["ebk0"]
            rrow = E["rrow"]; r16b = E["r16b"]; nscap = E["nscap"]; nk = E["nk"]
            segs = E["segs"]
            psegs = sorted(segs, key=lambda sg: -(sg[2][1] - sg[2][0]))
            for si, (ks, jt, (ir0, ir1), bank, boff, mk, moff) in enumerate(psegs):
                esl = ebsrc[:, bank * 384 + boff: bank * 384 + boff + (ir1 - ir0)]
                P.op("te", lambda e, jt=jt, ir0=ir0, ir1=ir1, hl=hl, esl=esl,
                     bPV=bPV,
                     st=(si == 0), sp=(si == len(psegs) - 1): e.matmul(
                    ctx.sb["psB"][0:64, bPV * 512 + ir0: bPV * 512 + ir1],
                    ctx.sb["vg"][:, jt * 512 + hl * 64: jt * 512 + (hl + 1) * 64],
                    esl, start=st, stop=sp),
                    reads=[("vg", jt), ("ebuf", ebk0), ("ebuf", ebk0 + 1)],
                    writes=[RB(bPV)], tag=f"{L}pv{g}_{hl}_{ib}_{ks}")
                P.op("te", lambda e, ir0=ir0, ir1=ir1, esl=esl, bDN=bDN,
                     st=(si == 0), sp=(si == len(psegs) - 1): e.matmul(
                    ctx.sb["psB"][0:1, bDN * 512 + ir0: bDN * 512 + ir1],
                    ctx.sb["ones_col"][:, :], esl, start=st, stop=sp),
                    reads=[("ebuf", ebk0), ("ebuf", ebk0 + 1), ("ones_col",)],
                    writes=[RB(bDN)], tag=f"{L}dn{g}_{hl}_{ib}_{ks}")
            P.op("ve", lambda e, rrow=rrow, bDN=bDN: e.tensor_scalar(
                row32(ctx, rrow, 0, 256), psB(ctx, bDN, 1, 0, 256), 1e-30,
                None, ALU.add),
                reads=[RB(bDN)], writes=[("rowp32", rrow)], tag=f"{L}d1{g}_{hl}_{ib}")
            P.op("ve", lambda e, rrow=rrow: e.reciprocal(
                row32(ctx, rrow, 0, 256), row32(ctx, rrow, 0, 256)),
                reads=[("rowp32", rrow)], writes=[("rowp32", rrow)], tag=f"{L}d2{g}_{hl}_{ib}")
            P.op("ac", lambda e, r16b=r16b, rrow=rrow: e.activation(
                row16(ctx, r16b, 512, 768), row32(ctx, rrow, 0, 256),
                AF.Copy),
                reads=[("rowp32", rrow)], writes=[("rowp16", r16b)], tag=f"{L}d3{g}_{hl}_{ib}")
            P.op("te", lambda e, bDN=bDN, r16b=r16b: e.matmul(
                psB(ctx, bDN, 64, 256, 512), ctx.sb["ones_row"][0:1, 0:64],
                row16(ctx, r16b, 512, 768), start=True, stop=True),
                reads=[("rowp16", r16b), ("ones_row",)], writes=[RB(bDN)],
                tag=f"{L}d4{g}_{hl}_{ib}")
            P.op("ac", lambda e, nscap=nscap, bPV=bPV: e.activation(
                nscap, psB(ctx, bPV, 64, 0, 256), AF.Copy),
                reads=[RB(bPV)], writes=[nk], tag=f"{L}d5{g}_{hl}_{ib}")
            P.op("ve", lambda e, tl=tl, bb=bb, i0=i0, nscap=nscap,
                 bDN=bDN: e.tensor_tensor(
                ctx.sb["aobuf"][bb:bb + 64, tl * S + i0: tl * S + i0 + 256],
                nscap, psB(ctx, bDN, 64, 256, 512), ALU.mult),
                reads=[nk, RB(bDN)], writes=[("aobuf", tl, hl % 2, ib)],
                tag=f"{L}d6{g}_{hl}_{ib}")

        iters = [_env(hl, ib) for hl in range(8) for ib in range(8)]
        for i in range(len(iters) + 1):
            if i < len(iters):
                _stageA(iters[i])
            if i > 0:
                _stageB(iters[i - 1])
        if getattr(ctx, "l1f_dumpqk", False):
            P.dma("gp", ctx.dram["dump_ao" if g == 0 else "dump_ao1"],
                  ctx.sb["aobuf"][:, :].rearrange("p (r c) -> p r c", r=4),
                  sem="st_dao",
                  reads=[("aobuf", j2, h2, d2) for j2 in range(4) for h2 in range(2)
                         for d2 in range(8)],
                  tag="dmpao", store=True)
        for m in range(NT):
            wsl = m % 2
            slot = ctx.sb["wog"] if wsl == 0 else ctx.sb["ebuf"]
            P.dma("gp", slot[:, 0:512]
                  .rearrange("p (r c) -> p r c", r=4),
                  dwo[li][512 * g: 512 * g + 512, m * 128:(m + 1) * 128]
                  .rearrange("(r p) c -> p r c", p=128),
                  sem=f"ld_wo{wsl}",
                  writes=[("wog", wsl)] + ([("ebuf", 0), ("ebuf", 1)] if wsl else []),
                  tag=f"{L}lwo{g}_{m}")
            for c in range(4):
                bk = (m * 4 + c) % 2
                ap = psA(ctx, bk, 128, 0, 512)
                for j in range(4):
                    P.op("te", lambda e, m=m, c=c, j=j, ap=ap, slot=slot: e.matmul(
                        ap, slot[:, j * 128: (j + 1) * 128],
                        ctx.sb["aobuf"][:, j * S + c * 512: j * S + (c + 1) * 512],
                        start=(j == 0), stop=(j == 3)),
                        reads=[("wog", wsl)] + [("aobuf", j, hh, (c * 512) // 256 + dd)
                                                for hh in range(2) for dd in range(2)],
                        writes=[RA(bk)], tag=f"{L}wo{g}_{m}_{c}_{j}")
                P.op("ve", lambda e, m=m, c=c, ap=ap: e.tensor_tensor(
                    x[:, m * S + c * 512: m * S + (c + 1) * 512],
                    x[:, m * S + c * 512: m * S + (c + 1) * 512], ap, ALU.add),
                    reads=[RA(bk), ("xcur", m)], writes=[("xcur", m)],
                    tag=f"{L}woe{g}_{m}_{c}")
    if not ffn_too:
        return
    P.op("ve", lambda e: e.engine_nop(),
         reads=[("qg",), ("kg",), ("vg",), ("wg",), ("ebuf",)] +
               [("aobuf", j, hh, d4) for j in range(4) for hh in range(2)
                for d4 in range(8)] +
               [("qg", sb, c) for sb in range(4) for c in range(4)] +
               [("kg", sb, c) for sb in range(4) for c in range(4)] +
               [("vg", t) for t in range(16)] +
               [("ebuf", b) for b in range(2)] +
               [("wog", sl) for sl in range(2)] + [("pp",)],
         writes=[("w1a",), ("w1g",), ("w2c",), ("actb",), ("gel",)] +
               [("w1a", sl) for sl in range(2)] + [("w1g", sl) for sl in range(2)] +
               [("w2c", sl) for sl in range(2)] +
               [("actb", at) for at in range(22)],
         tag=L + "fence1")
    emit_cast_x_to_bf(ctx, "xbf", tagp=L + "c2")
    _l1f_stats(ctx, L + "fs", lambda i: i >= 4, murstd=True)
    for i in range(NT):
        P.dma("gp", ctx.sb["cols"][:, 104 + i: 105 + i],
              ctx.dram["L1_ffn_ln_g"][li][i * 128:(i + 1) * 128].unsqueeze(1),
              sem="ld_lng", writes=[("cols", 104 + i)], tag=f"{L}lg{i}")
        P.dma("gp", ctx.sb["cols"][:, 112 + i: 113 + i],
              ctx.dram["L1_ffn_ln_b"][li][i * 128:(i + 1) * 128].unsqueeze(1),
              sem="ld_lnb", writes=[("cols", 112 + i)], tag=f"{L}lb{i}")
    for c in range(3, -1, -1):
        f0, f1 = c * 512, (c + 1) * 512
        bc_r, kr = emit_bcast_dram(ctx, 0, 0, f"{L}fb{c}", f0, f1)
        bc_m, km = emit_bcast_dram(ctx, 1, 1, f"{L}fm{c}", f0, f1)
        for i in range(NT):
            sh = i >= 4
            P.op("ve", lambda e, i=i, sh=sh, f0=f0, f1=f1, bc_r=bc_r: e.tensor_tensor(
                ctx.sb["scr32"][:, 0:512], xv(ctx, "xbf", i, sh, f0, f1), bc_r,
                ALU.mult),
                reads=[("xbf", i), kr], writes=[("scr32", 0)], tag=f"{L}fx1_{i}_{c}")
            P.op("ve", lambda e, i=i, bc_m=bc_m: e.tensor_tensor(
                ctx.sb["scr32"][:, 0:512], ctx.sb["scr32"][:, 0:512],
                bc_m, ALU.subtract),
                reads=[("scr32", 0), km], writes=[("scr32", 0)], tag=f"{L}fx2_{i}_{c}")
            P.op("ve", lambda e, i=i, f0=f0, f1=f1: e.tensor_scalar(
                xv(ctx, "xbf", i, False, f0, f1), ctx.sb["scr32"][:, 0:512],
                ctx.sb["cols"][:, 104 + i: 105 + i],
                ctx.sb["cols"][:, 112 + i: 113 + i], ALU.mult, ALU.add),
                reads=[("scr32", 0), ("cols", 104 + i), ("cols", 112 + i)],
                writes=[("xbf", i)], tag=f"{L}fx3_{i}_{c}")
    for at, (a0, a1) in enumerate(A_CHUNKS):
        w = a1 - a0
        P.dma("gp", ctx.sb["b1a"][0:w, at: at + 1],
              ctx.dram["L1_ffn_b1"][li][a0:a1].unsqueeze(1),
              sem="ld_b1a", writes=[("cols", "b1a", at)], tag=f"{L}lb1a{at}")
        P.dma("gp", ctx.sb["b1g"][0:w, at: at + 1],
              ctx.dram["L1_ffn_b1"][li][2730 + a0: 2730 + a1].unsqueeze(1),
              sem="ld_b1g", writes=[("cols", "b1g", at)], tag=f"{L}lb1g{at}")
    for i in range(NT):
        P.dma("gp", ctx.sb["cols"][:, 104 + i: 105 + i],
              ctx.dram["L1_ffn_b2"][li][i * 128:(i + 1) * 128].unsqueeze(1),
              sem="ld_b2c", writes=[("cols", 104 + i)], tag=f"{L}lb2{i}")
    for tau in range(4):
        f0, f1 = tau * 512, (tau + 1) * 512
        for ac2 in range(11):
            c0 = ac2 * 256
            c1 = min(c0 + 256, 2730)
            wd = c1 - c0
            sl = ac2 % 2
            P.dma("gp", ctx.sb["w1a"][:, sl * 2048: sl * 2048 + NT * wd]
                  .rearrange("p (r c) -> p r c", r=NT),
                  dw1[li][:, c0:c1].rearrange("(r p) c -> p r c", p=128),
                  sem=f"ld_w1a{sl}", writes=[("w1a", sl)], tag=f"{L}l1a{tau}_{ac2}")
            P.dma("gp", ctx.sb["w1g"][:, sl * 2048: sl * 2048 + NT * wd]
                  .rearrange("p (r c) -> p r c", r=NT),
                  dw1[li][:, 2730 + c0: 2730 + c1].rearrange("(r p) c -> p r c", p=128),
                  sem=f"ld_w1g{sl}", writes=[("w1g", sl)], tag=f"{L}l1g{tau}_{ac2}")
            for sub in range((wd + 127) // 128):
                at = ac2 * 2 + sub
                a0, a1 = A_CHUNKS[at]
                aw = a1 - a0
                off = sub * 128
                apA = ctx.sb["psA"][0:aw, 0 * 512: 0 * 512 + 512]
                apG = ctx.sb["psA"][0:aw, 1 * 512: 1 * 512 + 512]
                for i in range(NT):
                    P.op("te", lambda e, i=i, sl=sl, off=off, aw=aw, wd=wd, apA=apA,
                         f0=f0, f1=f1: e.matmul(
                        apA, ctx.sb["w1a"][:, sl * 2048 + i * wd + off: sl * 2048 + i * wd + off + aw],
                        xv(ctx, "xbf", i, False, f0, f1),
                        start=(i == 0), stop=(i == NT - 1)),
                        reads=[("w1a", sl), ("xbf", i)], writes=[RA(0)],
                        tag=f"{L}w1a{tau}_{at}_{i}")
                    P.op("te", lambda e, i=i, sl=sl, off=off, aw=aw, wd=wd, apG=apG,
                         f0=f0, f1=f1: e.matmul(
                        apG, ctx.sb["w1g"][:, sl * 2048 + i * wd + off: sl * 2048 + i * wd + off + aw],
                        xv(ctx, "xbf", i, False, f0, f1),
                        start=(i == 0), stop=(i == NT - 1)),
                        reads=[("w1g", sl), ("xbf", i)], writes=[RA(1)],
                        tag=f"{L}w1g{tau}_{at}_{i}")
                P.op("ac", lambda e, at=at, aw=aw, apG=apG: e.activation(
                    ctx.sb["gel"][0:aw, 0:512], apG, AF.Gelu,
                    bias=ctx.sb["b1g"][0:aw, at: at + 1]),
                    reads=[RA(1), ("cols", "b1g", at)], writes=[("gel",)],
                    tag=f"{L}ge{tau}_{at}")
                P.op("ve", lambda e, at=at, aw=aw, apA=apA: e.scalar_tensor_tensor(
                    ctx.sb["actb"][0:aw, at * 512:(at + 1) * 512], apA,
                    ctx.sb["b1a"][0:aw, at: at + 1], ctx.sb["gel"][0:aw, 0:512],
                    ALU.add, ALU.mult),
                    reads=[RA(0), ("cols", "b1a", at), ("gel",)], writes=[("actb", at)],
                    tag=f"{L}am{tau}_{at}")
        for ct in range(22):
            a0, a1 = A_CHUNKS[ct]
            cw = a1 - a0
            sl = ct % 2
            P.dma("gp", ctx.sb["w2c"][:, sl * 1024:(sl + 1) * 1024][0:cw, :],
                  dw2[li][a0:a1, :], sem=f"ld_w2{sl}", writes=[("w2c", sl)],
                  tag=f"{L}l2{tau}_{ct}")
            for m in range(NT):
                P.op("te", lambda e, ct=ct, m=m, sl=sl, cw=cw: e.matmul(
                    (psB(ctx, m, 128, 0, 512) if m < 4 else psA(ctx, m - 4, 128, 0, 512)),
                    ctx.sb["w2c"][0:cw, sl * 1024 + m * 128: sl * 1024 + (m + 1) * 128],
                    ctx.sb["actb"][0:cw, ct * 512:(ct + 1) * 512],
                    start=(ct == 0), stop=(ct == 21)),
                    reads=[("w2c", sl), ("actb", ct)],
                    writes=[RB(m) if m < 4 else RA(m - 4)],
                    tag=f"{L}w2{tau}_{ct}_{m}")
        for m in range(NT):
            P.op("ve", lambda e, m=m, f0=f0, f1=f1: e.scalar_tensor_tensor(
                x[:, m * S + f0: m * S + f1],
                (psB(ctx, m, 128, 0, 512) if m < 4 else psA(ctx, m - 4, 128, 0, 512)),
                ctx.sb["cols"][:, 104 + m: 105 + m],
                x[:, m * S + f0: m * S + f1], ALU.add, ALU.add),
                reads=[(RB(m) if m < 4 else RA(m - 4)), ("cols", 104 + m),
                       ("xcur", m)],
                writes=[("xcur", m)], tag=f"{L}w2e{tau}_{m}")


def emit_l1_ffn(ctx):
    P = ctx.P
    x = ctx.sb["xcur"]
    P.op("ve", lambda e: e.memset(ctx.sb["cols"][0:1, 103:104], 1e-30),
         writes=[("cols", 103)], tag="fieps")
    P.dma("sp", ctx.dram["x3park"].rearrange("(r p) c -> p r c", p=128),
          x[:, :].rearrange("p (r c) -> p r c", r=NT),
          sem="st_park", reads=[("xcur",)], writes=[("x3park",)], tag="fpark")
    emit_cast_x_to_bf(ctx, "xbf", tagp="ficx")
    _l1f_stats(ctx, "fis", lambda i: False, murstd=True)
    for c in range(4):
        f0, f1 = c * 512, (c + 1) * 512
        bc_r, kr = emit_bcast_dram(ctx, 0, 0, f"fib{c}", f0, f1)
        bc_m, km = emit_bcast_dram(ctx, 1, 1, f"fim{c}", f0, f1)
        for i in range(NT):
            P.op("ve", lambda e, i=i, f0=f0, f1=f1, bc_r=bc_r: e.tensor_tensor(
                ctx.sb["scr32"][:, 0:512], x[:, i * S + f0: i * S + f1], bc_r,
                ALU.mult),
                reads=[("xcur", i), kr], writes=[("scr32", 0)], tag=f"fi1_{i}_{c}")
            P.op("ve", lambda e, i=i, f0=f0, f1=f1, bc_m=bc_m: e.tensor_tensor(
                x[:, i * S + f0: i * S + f1], ctx.sb["scr32"][:, 0:512],
                bc_m, ALU.subtract),
                reads=[("scr32", 0), km], writes=[("xcur", i)], tag=f"fi2_{i}_{c}")
    nsub = getattr(ctx, "l1f_nsub", 12)
    for li in range(6):
        if 2 * li >= nsub:
            break
        emit_l1_ffn_layer(ctx, li, ffn_too=(2 * li + 1 < nsub))
    if nsub < 12:
        P.dma("sp", ctx.x_dst_dram.rearrange("(r p) c -> p r c", p=128),
              x[:, :].rearrange("p (r c) -> p r c", r=NT),
              sem="st_dbg", reads=[("xcur",)], tag="fdbg", store=True)
        ctx.handled_store = True
        return
    xsrc = ctx.dram["x3park"]
    xdst = ctx.x_dst_dram
    for i in range(NT):
        for c in range(4):
            f0, f1 = c * 512, (c + 1) * 512
            P.dma("sp", ctx.sb["xck"], xsrc[i * 128:(i + 1) * 128, f0:f1],
                  sem="ld_xc", reads=[("x3park",)], writes=[("bcb", 1)],
                  tag=f"ff1_{i}_{c}")
            P.op("ve", lambda e, i=i, f0=f0, f1=f1: e.tensor_tensor(
                ctx.sb["xos"], ctx.sb["xck"],
                x[:, i * S + f0: i * S + f1], ALU.add),
                reads=[("bcb", 1), ("xcur", i)], writes=[("scr32", 0)],
                tag=f"ff2_{i}_{c}")
            P.dma("sp", xdst[i * 128:(i + 1) * 128, f0:f1], ctx.sb["xos"],
                  sem="st_xc", reads=[("scr32", 0)], tag=f"ff3_{i}_{c}", store=True)
    ctx.handled_store = True


def emit_l1_ffn_loads(ctx):
    d = ctx.din
    for nm in ROT_NAMES:
        dr = ctx.din("c_" + nm, (128, S))
        ctx.P.dma("gp", ctx.sb[nm][:, :], dr, sem="ld_c", writes=[(nm,)],
                  tag="ldc_" + nm)
    d("L1_ffn_Wqkv", (6, D, 3072))
    d("L1_ffn_q_scale", (6, 64))
    d("L1_ffn_k_scale", (6, 64))
    d("L1_ffn_Wo", (6, D, D))
    d("L1_ffn_ln_g", (6, D))
    d("L1_ffn_ln_b", (6, D))
    d("L1_ffn_W1", (6, D, 5460))
    d("L1_ffn_b1", (6, 5460))
    d("L1_ffn_W2", (6, 2730, D))
    d("L1_ffn_b2", (6, D))
    ctx.dram["x3park"] = ctx.nc.dram_tensor("x3park", [D, S], F32, kind="Internal").ap()


ATTN_KEYS = [("qg",), ("kg",), ("vg",), ("aobuf",), ("wg",), ("wog",),
             ("ebuf",), ("nsc",), ("gelq",)]
FFN_KEYS = [("w1a",), ("w1g",), ("w2c",), ("actb",), ("gel",)]


def alloc_l1_ffn(ctx):
    for nm in ROT_NAMES:
        ctx.sbuf(nm, (128, S), BF16)
    ctx.sbuf("xbf", (128, NT * (S + 1)), BF16)
    ctx.sbuf("sqc", (128, NT * 128), BF16)
    ctx.sbuf("scr32", (128, 512), F32)
    ctx.sbuf("rsc", (128, 512), BF16)
    pool = ctx.sbuf("fpool", (128, 35072), BF16)
    pa = pool.ap()
    ctx.sb["qg"] = pa[:, 0:8192]
    ctx.sb["kg"] = pa[:, 8192:16384]
    ctx.sb["vg"] = pa[:, 16384:24576]
    ctx.sb["aobuf"] = pa[:, 24576:32768]
    ctx.sb["wg"] = pa[:, 32768:33792]
    ctx.sb["wog"] = pa[:, 33792:34304]
    ctx.sb["ebuf"] = pa[:, 34304:35072]
    ctx.sb["w1a"] = pa[:, 0:4096]
    ctx.sb["w1g"] = pa[:, 4096:8192]
    ctx.sb["actb"] = pa[:, 8192:19456]
    ctx.sb["w2c"] = pa[:, 19456:21504]
    ctx.sb["gel"] = pa[:, 21504:22016]
    ctx.sb["nsc"] = ctx.sb["rsc"].ap()[0:64, 0:256]
    ctx.sb["prep2"] = pa[:, 24576:27648]
    ctx.sb["b1a"] = ctx.sb["cols"].ap()[:, 48:70]
    ctx.sb["b1g"] = ctx.sb["cols"].ap()[:, 70:92]
    ctx.sb["xck"] = ctx.sb["bcb"].ap()[:, 512:1024]
    ctx.sb["xos"] = ctx.sb["scr32"].ap()[:, 0:512]



def build_block(block_name, consts_np):
    nc = bass.Bass("TRN2", target_bir_lowering=False)
    stack = ExitStack()
    P = Prog(nc)
    ctx = Ctx(nc, P, stack)
    ctx.handled_store = False
    xin = ctx.din("x_in", (D, S))
    xout = ctx.dout("x_out", (D, S))
    ctx.x_src_dram = xin
    ctx.x_dst_dram = xout
    setup_common(ctx, consts_np, const_names=BLOCK_CONSTS.get(block_name),
                 cols_w=120 if block_name == "l1_ffn" else 64,
                 sbuf_skip=ROT_NAMES)
    ctx.sbuf("xcur", (128, NT * S), F32)
    P.dma("sp", ctx.sb["xcur"][:, :].rearrange("p (r c) -> p r c", r=NT),
          xin.rearrange("(r p) c -> p r c", p=128),
          sem="ld_x", writes=[("xcur",)], tag="ldx")
    if block_name == "l0_ffn":
        alloc_l0_ffn(ctx)
        emit_l0_ffn_loads(ctx)
        emit_l0_ffn(ctx)
    elif block_name == "l1_attn":
        alloc_l1_attn(ctx)
        emit_l1_attn_loads(ctx)
        emit_l1_attn(ctx)
    elif block_name == "l0_attn":
        alloc_l0_attn(ctx)
        emit_l0_attn_loads(ctx)
        emit_l0_attn(ctx)
    elif block_name == "l1_ffn":
        alloc_l1_ffn(ctx)
        ctx.l1f_nsub = L1F_NSUB
        ctx.l1f_dumpqk = L1F_DUMPQK
        if L1F_DUMPQK:
            ctx.dout("dump_qg", (128, 4, S))
            ctx.dout("dump_kg", (128, 4, S))
            ctx.dout("dump_vg", (128, 16, 512))
            ctx.dout("dump_ao", (128, 4, S))
            ctx.dout("dump_ao1", (128, 4, S))
        emit_l1_ffn_loads(ctx)
        emit_l1_ffn(ctx)
    else:
        raise ValueError(block_name)
    if not ctx.handled_store:
        P.dma("sp", xout.rearrange("(r p) c -> p r c", p=128),
              ctx.sb["xcur"][:, :].rearrange("p (r c) -> p r c", r=NT),
              sem="st_x", reads=[("xcur",)], tag="stx", store=True)
    P.finish(final_engine="sp")
    return nc


L1F_NSUB = 12
L1F_DUMPQK = False

BLOCK_WEIGHTS = {
    "l0_ffn": ["L0_ffn_W_r", "L0_ffn_W_k", "L0_ffn_W_v", "L0_ffn_mu_r", "L0_ffn_mu_k"],
    "l1_attn": ["L1_attn_W_qkv", "L1_attn_b_qkv", "L1_attn_W_proj",
                "L1_attn_b_proj", "L1_attn_gamma", "L1_attn_beta"],
    "l0_attn": ["L0_attn_w_norm_in", "L0_attn_w_norm_out", "L0_attn_router_w1",
                "L0_attn_router_w2", "L0_attn_router_b2", "L0_attn_lambdas",
                "L0_attn_slr_wq1", "L0_attn_slr_wq2", "L0_attn_slr_wk1",
                "L0_attn_slr_wk2", "L0_attn_slr_wv", "L0_attn_slr_wo",
                "L0_attn_slr_lam", "L0_attn_landmarks", "L0_attn_mix_w",
                "L0_attn_bus_compress", "L0_attn_bus_expand",
                "L0_attn_bus_publish"],
    "l1_ffn": ["L1_ffn_Wqkv", "L1_ffn_q_scale", "L1_ffn_k_scale", "L1_ffn_Wo",
               "L1_ffn_ln_g", "L1_ffn_ln_b", "L1_ffn_W1", "L1_ffn_b1",
               "L1_ffn_W2", "L1_ffn_b2"],
}

BLOCK_CONSTS = {
    "l0_ffn": {"ident", "ones_col", "ones_row"},
    "l1_attn": {"ident", "ones_col", "ones_row"},
    "l0_attn": {"ident", "ones_col", "ones_row"},
    "l1_ffn": None,
}


def run_block(block_name, x_full, inputs, core_ids=None):
    consts = host_constants()
    nc = build_block(block_name, consts)
    n = x_full.shape[0]
    in_maps = []
    for b in range(n):
        m = {"x_in": np.ascontiguousarray(x_full[b].T)}
        for k, v in consts.items():
            if BLOCK_CONSTS.get(block_name) is not None and k not in BLOCK_CONSTS[block_name]:
                continue
            m["c_" + k] = np.ascontiguousarray(v)
        for k in BLOCK_WEIGHTS[block_name]:
            v = np.asarray(inputs[k])
            m[k] = np.ascontiguousarray(v.reshape(1) if v.ndim == 0 else v)
        in_maps.append(m)
    res = run_bass_kernel_spmd(nc, in_maps, core_ids=core_ids or list(range(n)))
    return np.stack([r["x_out"].T for r in res.results], axis=0)


BLOCK_ORDER = ["l0_attn", "l0_ffn", "l1_attn", "l1_ffn"]

ALL_WEIGHTS = [w for b in BLOCK_ORDER for w in BLOCK_WEIGHTS[b]]

FUSED_PARTS = {
    "l0_attn": (alloc_l0_attn, emit_l0_attn_loads, emit_l0_attn),
    "l0_ffn": (alloc_l0_ffn, emit_l0_ffn_loads, emit_l0_ffn),
    "l1_attn": (alloc_l1_attn, emit_l1_attn_loads, emit_l1_attn),
    "l1_ffn": (alloc_l1_ffn, emit_l1_ffn_loads, emit_l1_ffn),
}


def build_fused(consts_np):
    nc = bass.Bass("TRN2", target_bir_lowering=False)
    stack = ExitStack()
    P = Prog(nc)
    ctx = Ctx(nc, P, stack)
    ctx.handled_store = False
    xin = ctx.din("x_in", (D, S))
    xout = ctx.dout("x_out", (D, S))
    setup_common(ctx, consts_np, cols_w=120, sbuf_skip=ROT_NAMES)
    ctx.sbuf("xcur", (128, NT * S), F32)
    P.dma("sp", ctx.sb["xcur"][:, :].rearrange("p (r c) -> p r c", r=NT),
          xin.rearrange("(r p) c -> p r c", p=128),
          sem="ld_x", writes=[("xcur",)], tag="ldx")
    ctx.dram["xpk"] = nc.dram_tensor("xpk", [D, S], F32, kind="Internal").ap()
    ctx.dram["xpk2"] = nc.dram_tensor("xpk2", [D, S], F32, kind="Internal").ap()

    def scoped(name, pfx):
        al, lo, em = FUSED_PARTS[name]
        bs = ExitStack()
        bctx = copy.copy(ctx)
        bctx.sb = dict(ctx.sb)
        bctx.dram = dict(ctx.dram)
        bctx.stack = bs
        bctx.prefix = pfx
        al(bctx)
        lo(bctx)
        em(bctx)
        P.barrier(tag=pfx + "bar")
        bs.close()

    scoped("l0_attn", "A_")
    scoped("l0_ffn", "B_")
    P.dma("sp", ctx.dram["xpk"].rearrange("(r p) c -> p r c", p=128),
          ctx.sb["xcur"][:, :].rearrange("p (r c) -> p r c", r=NT),
          sem="st_pk", reads=[("xcur",)], writes=[("xpk",)], tag="parkx")
    ctx.x_src_dram = ctx.dram["xpk"]
    ctx.x_dst_dram = ctx.dram["xpk2"]
    scoped("l1_attn", "C_")
    P.dma("sp", ctx.sb["xcur"][:, :].rearrange("p (r c) -> p r c", r=NT),
          ctx.dram["xpk2"].rearrange("(r p) c -> p r c", p=128),
          sem="ld_x2", reads=[("xdst",)], writes=[("xcur",)], tag="reloadx")
    ctx.x_dst_dram = xout
    scoped("l1_ffn", "D_")
    P.finish(final_engine="sp")
    return nc


def kernel(**inputs):
    x = np.ascontiguousarray(np.asarray(inputs["L0_attn_x"], np.float32))
    consts = host_constants()
    nc = build_fused(consts)
    in_maps = []
    for b in range(8):
        m = {"x_in": np.ascontiguousarray(x[b].T)}
        for k, v in consts.items():
            m["c_" + k] = np.ascontiguousarray(v)
        for k in ALL_WEIGHTS:
            v = np.asarray(inputs[k])
            m[k] = np.ascontiguousarray(v.reshape(1) if v.ndim == 0 else v)
        in_maps.append(m)
    res = run_bass_kernel_spmd(nc, in_maps, core_ids=list(range(8)))
    return np.stack([r["x_out"].T for r in res.results], axis=0)


# version 11 (final)
# speedup vs baseline: 1.1464x; 1.0669x over previous
import copy
import math
import numpy as np
from contextlib import ExitStack

import concourse.bass as bass
import concourse.mybir as mybir
from concourse.bass_utils import run_bass_kernel_spmd

F32 = mybir.dt.float32
BF16 = mybir.dt.bfloat16
AF = mybir.ActivationFunctionType
ALU = mybir.AluOpType
AX = mybir.AxisListType

D = 1024
S = 2048
NT = 8
NEG = -1.0e37
LN_EPS = 1e-5
RMS_EPS = 1.1920929e-07


ENGS = ("sp", "gp", "te", "ve", "ac")


class _Op:
    __slots__ = ("idx", "eng", "emit", "reads", "writes", "dma_sem", "deps",
                 "need_inc", "sem", "val", "tag", "wait_list")

    def __init__(self, idx, eng, emit, reads, writes, dma_sem, tag):
        self.idx, self.eng, self.emit = idx, eng, emit
        self.reads, self.writes = reads, writes
        self.dma_sem = dma_sem
        self.deps = []
        self.need_inc = False
        self.sem = None
        self.val = 0
        self.tag = tag


def _keys(ks):
    return [k if isinstance(k, tuple) else (k,) for k in ks]


class Prog:
    def __init__(self, nc):
        self.nc = nc
        self.ops = []
        self._track = {}
        self.store_sems = set()
        self._floor = None

    def op(self, eng, emit, reads=(), writes=(), dma_sem=None, tag="", store=False):
        o = _Op(len(self.ops), eng, emit, _keys(reads), _keys(writes), dma_sem, tag)
        self.ops.append(o)
        if store and dma_sem:
            self.store_sems.add(dma_sem)
        deps = {}

        def dep(p):
            if p is not o:
                deps[p.idx] = p

        for key in o.reads:
            name, sub = key[0], key[1:] or None
            t = self._track.setdefault(name, {"w": {}, "r": {}})
            if sub is None:
                for p in t["w"].values():
                    dep(p)
            else:
                if None in t["w"]:
                    dep(t["w"][None])
                if sub in t["w"]:
                    dep(t["w"][sub])
            t["r"].setdefault(sub, []).append(o)
        for key in o.writes:
            name, sub = key[0], key[1:] or None
            t = self._track.setdefault(name, {"w": {}, "r": {}})
            if sub is None:
                for p in t["w"].values():
                    dep(p)
                for lst in t["r"].values():
                    for p in lst:
                        dep(p)
                t["w"] = {None: o}
                t["r"] = {}
            else:
                if None in t["w"]:
                    dep(t["w"][None])
                if sub in t["w"]:
                    dep(t["w"][sub])
                for p in t["r"].get(sub, ()):
                    dep(p)
                for p in t["r"].get(None, ()):
                    dep(p)
                t["w"][sub] = o
                t["r"][sub] = []
        if not deps and self._floor is not None:
            deps[self._floor.idx] = self._floor
        o.deps = list(deps.values())
        return o

    def barrier(self, tag="bar"):
        last_eng, last_sem = {}, {}
        for p in self.ops:
            if p.dma_sem is not None:
                last_sem[p.dma_sem] = p
            else:
                last_eng[p.eng] = p
        o = self.op("ve", lambda e: e.engine_nop(), tag=tag)
        deps = {}
        for p in list(last_eng.values()) + list(last_sem.values()):
            if p is not o:
                deps[p.idx] = p
        o.deps = list(deps.values())
        self._track = {}
        self._floor = o
        return o

    def dma(self, queue, out, in_, sem, reads=(), writes=(), tag="", store=False, **kw):
        def emit(eng):
            return eng.dma_start(out=out, in_=in_, **kw)
        return self.op(queue, emit, reads=reads, writes=writes, dma_sem=sem,
                       tag=tag, store=store)

    def finish(self, final_engine="gp"):
        nc = self.nc
        for o in self.ops:
            for p in o.deps:
                if p.eng == o.eng == "te" and p.dma_sem is None and o.dma_sem is None:
                    continue
                p.need_inc = True
        for o in self.ops:
            if o.dma_sem is not None:
                o.need_inc = True
        counters, sem_owner = {}, {}
        for o in self.ops:
            if not o.need_inc:
                continue
            o.sem = o.dma_sem if o.dma_sem is not None else "c_" + o.eng
            if o.dma_sem is not None:
                own = sem_owner.setdefault(o.sem, o.eng)
                assert own == o.eng, f"dma sem {o.sem}: queues {own} and {o.eng}"
            amt = 16 if o.dma_sem is not None else 1
            counters[o.sem] = counters.get(o.sem, 0) + amt
            o.val = counters[o.sem]
        self.final_counts = dict(counters)
        dma_steps = {}
        for o in self.ops:
            if o.dma_sem is not None:
                dma_steps.setdefault(o.sem, []).append((o.idx, o.val))

        def dma_wait_val(sem, consumer_idx):
            v = 0
            for (i, cv) in dma_steps[sem]:
                if i < consumer_idx:
                    v = cv
                else:
                    break
            return v

        floors = {e: {} for e in ENGS}
        for o in self.ops:
            waits = {}
            for p in o.deps:
                if p.eng == o.eng == "te" and p.dma_sem is None and o.dma_sem is None:
                    continue
                v = dma_wait_val(p.sem, o.idx) if p.dma_sem is not None else p.val
                waits[p.sem] = max(waits.get(p.sem, 0), v)
            fl = floors[o.eng]
            o.wait_list = []
            for s in sorted(waits):
                v = waits[s]
                if fl.get(s, 0) >= v:
                    continue
                fl[s] = v
                o.wait_list.append((s, v))
        self.sem_handles = {n: nc.alloc_semaphore(n) for n in counters}
        final_waits = [(s, self.final_counts[s]) for s in sorted(self.store_sems)]
        by_eng = {e: [] for e in ENGS}
        for o in self.ops:
            by_eng[o.eng].append(o)
        prog = self

        def run(eng_name, eng):
            for o in by_eng[eng_name]:
                for (s, v) in o.wait_list:
                    eng.wait_ge(prog.sem_handles[s], v)
                inst = o.emit(eng)
                if o.need_inc:
                    if isinstance(inst, list):
                        inst = inst[-1]
                    assert inst is not None, f"op {o.tag}: inc needed, no instruction"
                    inst.then_inc(prog.sem_handles[o.sem],
                                  16 if o.dma_sem is not None else 1)
            if eng_name == final_engine:
                for (s, v) in final_waits:
                    eng.wait_ge(prog.sem_handles[s], v)

        with nc.Block() as block:
            @block.sync
            def _(eng):
                run("sp", eng)

            @block.gpsimd
            def _(eng):
                run("gp", eng)

            @block.tensor
            def _(eng):
                run("te", eng)

            @block.vector
            def _(eng):
                run("ve", eng)

            @block.scalar
            def _(eng):
                run("ac", eng)



def host_constants():
    c = {}
    c["ident"] = np.eye(128, dtype=np.float32)
    c["ones_col"] = np.ones((128, 1), np.float32)
    c["ones_row"] = np.ones((1, 128), np.float32)
    cidx = np.arange(128)[:, None]
    pidx = np.arange(128)[None, :]
    c["m_up"] = np.where(cidx > pidx, NEG, 0.0).astype(np.float32)
    c["m_low"] = np.where(cidx < pidx, NEG, 0.0).astype(np.float32)
    sig = np.zeros((128, 128), np.float32)
    for p in range(128):
        h, dh = divmod(p, 64)
        sig[h * 64 + ((dh + 32) % 64), p] = 1.0
    c["p_sigma"] = sig
    DH, WINDOW = 64, 256
    half = np.arange(0, DH, 2, dtype=np.float64)
    inv_freq = 1.0 / 10000.0 ** (half / DH)
    t = np.arange(S, dtype=np.float64)
    freqs = np.concatenate([t[:, None] * inv_freq[None, :]] * 2, -1)
    base_scale = (half + 0.4 * DH) / (1.4 * DH)
    power = (t - S // 2) / (WINDOW // 2)
    scale = np.concatenate([base_scale[None, :] ** power[:, None]] * 2, -1)
    cos, sin = np.cos(freqs), np.sin(freqs)
    sgn = np.where(np.arange(DH) < 32, -1.0, 1.0)
    c["rot_aq"] = np.tile((cos * scale).T, (2, 1)).astype(np.float32)
    c["rot_bq"] = np.tile((sin * scale).T * sgn[:, None], (2, 1)).astype(np.float32)
    c["rot_ak"] = np.tile((cos / scale).T, (2, 1)).astype(np.float32)
    c["rot_bk"] = np.tile((sin / scale).T * sgn[:, None], (2, 1)).astype(np.float32)
    blk2 = np.zeros((128, 2), np.float32)
    blk2[0:64, 0] = 1.0
    blk2[64:128, 1] = 1.0
    c["blk2"] = blk2
    c["blk2t"] = blk2.T.copy()
    return c


CONST_DTYPES = {k: BF16 for k in ("ident", "ones_col", "ones_row", "m_up",
                                  "m_low", "p_sigma", "rot_aq", "rot_bq",
                                  "rot_ak", "rot_bk", "blk2", "blk2t")}



class Ctx:
    def __init__(self, nc, P, stack):
        self.nc, self.P, self.stack = nc, P, stack
        self.sb = {}
        self.dram = {}
        self.rows_alloc = {}
        self.prefix = ""

    def sbuf(self, name, shape, dtype):
        t = self.stack.enter_context(
            self.nc.sbuf_tensor(self.prefix + name, list(shape), dtype))
        self.sb[name] = t
        return t

    def psum(self, name, shape, dtype=F32):
        t = self.stack.enter_context(self.nc.psum_tensor(name, list(shape), dtype))
        self.sb[name] = t
        return t

    def din(self, name, shape, dtype=F32):
        d = self.nc.dram_tensor(name, list(shape), dtype, kind="ExternalInput").ap()
        self.dram[name] = d
        return d

    def dout(self, name, shape, dtype=F32):
        d = self.nc.dram_tensor(name, list(shape), dtype, kind="ExternalOutput").ap()
        self.dram[name] = d
        return d


ROT_NAMES = ("rot_aq", "rot_bq", "rot_ak", "rot_bk")


def setup_common(ctx, consts_np, const_names=None, cols_w=120, sbuf_skip=()):
    P = ctx.P
    for name, arr in consts_np.items():
        if const_names is not None and name not in const_names:
            continue
        if name in sbuf_skip:
            continue
        d = ctx.din("c_" + name, arr.shape)
        t = ctx.sbuf(name, arr.shape, CONST_DTYPES[name])
        P.dma("gp", t[:, :], d, sem="ld_c", writes=[(name,)], tag="ldc_" + name)
    ctx.dram["rowscr"] = ctx.nc.dram_tensor(
        "rowscr", [4, S], F32, kind="Internal").ap()
    ctx.sbuf("bcb", (128, 2 * 512), F32)
    ctx.psum("psA", (128, 2048))
    ctx.psum("psB", (128, 2048))
    ctx.sbuf("rowp32", (128, S), F32)
    ctx.sbuf("rowp16", (1, 2 * S), BF16)
    ctx.sbuf("cols", (128, cols_w), F32)
    ctx.sbuf("colsb", (128, 16), BF16)


def psA(ctx, bank, p, f0, f1):
    t = ctx.sb["psA"]
    return t[0:p, bank * 512 + f0: bank * 512 + f1]


def psB(ctx, bank, p, f0, f1):
    t = ctx.sb["psB"]
    return t[0:p, bank * 512 + f0: bank * 512 + f1]


def row32(ctx, i, f0=0, f1=S):
    assert 0 <= i < 4
    return ctx.sb["rowp32"][i * 32: i * 32 + 1, f0:f1]


def row16(ctx, i, f0=0, f1=S):
    assert 0 <= i < 2
    return ctx.sb["rowp16"][0:1, i * S + f0: i * S + f1]


def emit_row_pair(ctx, out_ap_fn, out_key, in0_ap_fn, in0_keys, in1_ap_fn,
                  in1_keys, op, tagp, via=AF.Copy):
    P = ctx.P
    for c in range(4):
        f0, f1 = c * 512, (c + 1) * 512
        pr = psA(ctx, 2, 1, 0, 512)
        P.op("ac", lambda e, f0=f0, f1=f1, pr=pr: e.activation(
            pr, in1_ap_fn(f0, f1), via),
            reads=in1_keys, writes=[RA(2)], tag=f"{tagp}p{c}")
        P.op("ve", lambda e, f0=f0, f1=f1, pr=pr: e.tensor_tensor(
            out_ap_fn(f0, f1), in0_ap_fn(f0, f1), pr, op),
            reads=in0_keys + [RA(2)], writes=[out_key], tag=f"{tagp}t{c}")


def RA(bank):
    return ("psA", bank)


def RB(bank):
    return ("psB", bank)


def R32(i):
    return ("rowp32", i)


def R16(i):
    return ("rowp16", i)



def emit_cast_x_to_bf(ctx, dst_name, shifted_buf=True, tagp="cx"):
    P = ctx.P
    x = ctx.sb["xcur"]
    dst = ctx.sb[dst_name]
    for i in range(NT):
        P.op("ve", lambda e, i=i: e.memset(dst[:, i * (S + 1): i * (S + 1) + 1], 0.0),
             writes=[(dst_name, i)], tag=f"{tagp}z{i}")
        P.op("ac", lambda e, i=i: e.activation(
            dst[:, i * (S + 1) + 1: (i + 1) * (S + 1)], x[:, i * S:(i + 1) * S], AF.Copy),
            reads=[("xcur", i)], writes=[(dst_name, i)], tag=f"{tagp}c{i}")


def xv(ctx, buf_name, i, shifted=False, f0=0, f1=S):
    t = ctx.sb[buf_name]
    off = i * (S + 1) + (0 if shifted else 1)
    return t[:, off + f0: off + f1]


def emit_ln_stats(ctx, src_tiles, sqc_name, r_mu, r_ex2, r_var, r_rstd,
                  eps, tagp, rstd_extra=None):
    P = ctx.P
    sq = ctx.sb[sqc_name]
    ones = ctx.sb["ones_col"]
    CW = sq.shape[1] // NT if hasattr(sq, "shape") else 256
    for c in range(S // CW):
        f0, f1 = c * CW, (c + 1) * CW
        for i, (apf, keys) in enumerate(src_tiles):
            P.op("ac", lambda e, apf=apf, i=i, f0=f0, f1=f1: e.activation(
                sq[:, i * CW:(i + 1) * CW], apf(f0, f1), AF.Square),
                reads=keys, writes=[(sqc_name, i)], tag=f"{tagp}sq{c}_{i}")
            P.op("te", lambda e, apf=apf, i=i, f0=f0, f1=f1: e.matmul(
                psA(ctx, 0, 1, 0, CW), ones[:, :], apf(f0, f1),
                start=(i == 0), stop=(i == NT - 1)),
                reads=keys + [("ones_col",)], writes=[RA(0)], tag=f"{tagp}mx{c}_{i}")
            P.op("te", lambda e, i=i: e.matmul(
                psA(ctx, 1, 1, 0, CW), ones[:, :], sq[:, i * CW:(i + 1) * CW],
                start=(i == 0), stop=(i == NT - 1)),
                reads=[(sqc_name, i), ("ones_col",)], writes=[RA(1)], tag=f"{tagp}mq{c}_{i}")
        P.op("ve", lambda e, c=c, f0=f0, f1=f1: e.tensor_scalar(
            row32(ctx, r_mu, f0, f1), psA(ctx, 0, 1, 0, CW),
            1.0 / D, None, ALU.mult),
            reads=[RA(0)], writes=[R32(r_mu)], tag=f"{tagp}emu{c}")
        P.op("ve", lambda e, c=c, f0=f0, f1=f1: e.tensor_scalar(
            row32(ctx, r_ex2, f0, f1), psA(ctx, 1, 1, 0, CW),
            1.0 / D, None, ALU.mult),
            reads=[RA(1)], writes=[R32(r_ex2)], tag=f"{tagp}eex{c}")
    emit_row_pair(ctx, lambda f0, f1: row32(ctx, r_var, f0, f1), R32(r_var),
                  lambda f0, f1: row32(ctx, r_ex2, f0, f1), [R32(r_ex2)],
                  lambda f0, f1: row32(ctx, r_mu, f0, f1), [R32(r_mu)],
                  ALU.subtract, f"{tagp}var", via=AF.Square)
    P.op("ve", lambda e: e.tensor_scalar(
        row32(ctx, r_var), row32(ctx, r_var), float(eps), None, ALU.add),
        reads=[R32(r_var)], writes=[R32(r_var)], tag=f"{tagp}veps")
    P.op("ac", lambda e: e.activation(
        row32(ctx, r_rstd), row32(ctx, r_var), AF.Ln),
        reads=[R32(r_var)], writes=[R32(r_rstd)], tag=f"{tagp}ln")
    P.op("ac", lambda e: e.activation(
        row32(ctx, r_rstd), row32(ctx, r_rstd), AF.Exp, scale=-0.5),
        reads=[R32(r_rstd)], writes=[R32(r_rstd)], tag=f"{tagp}exp")
    if rstd_extra is not None:
        rstd_extra()


def emit_row_to_bf(ctx, r_src, rb_dst, tagp, scale=1.0):
    P = ctx.P
    P.op("ve", lambda e: e.tensor_scalar(
        row16(ctx, rb_dst), row32(ctx, r_src), float(scale), None, ALU.mult),
        reads=[R32(r_src)], writes=[R16(rb_dst)], tag=tagp)


def emit_row_to_dram(ctx, r_idx, slot, tagp):
    P = ctx.P
    P.dma("sp", ctx.dram["rowscr"][slot: slot + 1, :], row32(ctx, r_idx),
          sem=f"st_row{slot}", reads=[R32(r_idx)], writes=[("rowscr", slot)],
          tag=tagp)


def emit_bcast_dram(ctx, slot, bcslot, tagp, f0=0, f1=S):
    P = ctx.P
    w = f1 - f0
    ap = ctx.sb["bcb"][:, bcslot * 512: bcslot * 512 + w]
    key = ("bcb", bcslot)
    P.dma("sp", ap, ctx.dram["rowscr"][slot: slot + 1, f0:f1].partition_broadcast(128),
          sem=f"ld_bc{bcslot}", reads=[("rowscr", slot)], writes=[key], tag=tagp)
    return ap, key


def emit_bcast_row(ctx, rb_idx, pool, bank, nparts, tagp, f0=0, f1=S):
    P = ctx.P
    onesr = ctx.sb["ones_row"]
    pfun = psA if pool == "A" else psB
    ap = pfun(ctx, bank, nparts, 0, f1 - f0)
    key = RA(bank) if pool == "A" else RB(bank)
    P.op("te", lambda e: e.matmul(
        ap, onesr[0:1, 0:nparts], row16(ctx, rb_idx, f0, f1), start=True, stop=True),
        reads=[R16(rb_idx), ("ones_row",)], writes=[key], tag=tagp)
    return ap, key


def psbf(ctx, pool, bank, p, f0, f1):
    t = ctx.sb["psA" if pool == "A" else "psB"]
    return t[0:p, bank * 512:(bank + 1) * 512].bitcast(BF16)[0:p, f0:f1]


def emit_wload(ctx, dram_ap, rows, cols, dst_name, sem, tagp, dtile=128):
    P = ctx.P
    rt = rows // dtile
    dst = ctx.sb[dst_name]
    src = dram_ap.rearrange("(r p) c -> p r c", p=dtile)
    d = dst[:, 0: rt * cols].rearrange("p (r c) -> p r c", r=rt)
    P.dma("gp", d, src, sem=sem, writes=[(dst_name,)], tag=tagp)


def wtile(ctx, name, r, cols, f0=0, f1=None):
    t = ctx.sb[name]
    if f1 is None:
        f1 = cols
    return t[:, r * cols + f0: r * cols + f1]


def emit_wtranspose(ctx, src_name, rows, cols, dst_name, tagp,
                    fold_col=None):
    P = ctx.P
    src, dst = ctx.sb[src_name], ctx.sb[dst_name]
    ident = ctx.sb["ident"]
    rt, ct = rows // 128, cols // 128
    for r in range(rt):
        for c in range(ct):
            pap = psbf(ctx, "B", 3, 128, 0, 128)
            P.op("te", lambda e, r=r, c=c, pap=pap: e.transpose(
                pap, src[:, r * cols + c * 128: r * cols + (c + 1) * 128],
                ident[:, :]),
                reads=[(src_name,), ("ident",)], writes=[RB(3)], tag=f"{tagp}t{r}_{c}")
            dap = dst[:, c * rows + r * 128: c * rows + (r + 1) * 128]
            P.op("ac", lambda e, dap=dap, pap=pap: e.activation(dap, pap, AF.Copy),
                 reads=[RB(3)], writes=[(dst_name, "t")], tag=f"{tagp}e{r}_{c}")


def emit_colload(ctx, dram_ap, n, dst_name, col0, sem, tagp):
    P = ctx.P
    dst = ctx.sb[dst_name]
    for r in range(n):
        P.dma("gp", dst[:, col0 + r: col0 + r + 1],
              dram_ap[r * 128:(r + 1) * 128].unsqueeze(1),
              sem=sem, writes=[(dst_name, col0 + r)], tag=f"{tagp}{r}")



def emit_l0_ffn(ctx):
    P = ctx.P
    x = ctx.sb["xcur"]
    TAU = 256
    emit_cast_x_to_bf(ctx, "xbf", tagp="f0cx")
    src_tiles = [((lambda f0, f1, i=i: xv(ctx, "xbf", i, False, f0, f1)),
                  [("xbf", i)]) for i in range(NT)]
    emit_ln_stats(ctx, src_tiles, "sqc", 0, 1, 1, 2, LN_EPS, "f0st")
    emit_row_pair(ctx, lambda f0, f1: row32(ctx, 3, f0, f1), R32(3),
                  lambda f0, f1: row32(ctx, 0, f0, f1), [R32(0)],
                  lambda f0, f1: row32(ctx, 2, f0, f1), [R32(2)],
                  ALU.mult, "f0mr")
    emit_row_to_dram(ctx, 2, 0, "f0sd0")
    emit_row_to_dram(ctx, 3, 1, "f0sd1")
    for c in range(4):
        f0, f1 = c * 512, (c + 1) * 512
        bc_r, kr = emit_bcast_dram(ctx, 0, 0, f"f0br{c}", f0, f1)
        bc_m, km = emit_bcast_dram(ctx, 1, 1, f"f0bm{c}", f0, f1)
        for i in range(NT):
            sc = ctx.sb["scr32"]
            P.op("ve", lambda e, i=i, f0=f0, f1=f1, bc_r=bc_r: e.tensor_tensor(
                sc[:, 0:512], x[:, i * S + f0: i * S + f1], bc_r, ALU.mult),
                reads=[("xcur", i), kr], writes=[("scr32", 0)], tag=f"f0l1_{i}_{c}")
            P.op("ve", lambda e, i=i, f0=f0, f1=f1, bc_m=bc_m: e.tensor_tensor(
                xv(ctx, "xbf", i, False, f0, f1), sc[:, 0:512], bc_m, ALU.subtract),
                reads=[("scr32", 0), km], writes=[("xbf", i)], tag=f"f0l2_{i}_{c}")
    P.op("ve", lambda e: e.engine_nop(),
         reads=[("wnat",)], writes=[("wnat",)], tag="f0fence")
    xr, xk = ctx.sb["mixr"], ctx.sb["mixk"]
    for c in range(S // TAU):
        f0, f1 = c * TAU, (c + 1) * TAU
        for i in range(NT):
            P.op("ve", lambda e, i=i, f0=f0, f1=f1: e.tensor_scalar(
                xr[:, i * TAU:(i + 1) * TAU], xv(ctx, "xbf", i, False, f0, f1),
                ctx.sb["cols"][:, 0 + i: 1 + i], None, ALU.mult),
                reads=[("xbf", i), ("cols", 0 + i)], writes=[("mixr", i)], tag=f"f0xr{i}_{c}")
            P.op("ve", lambda e, i=i, f0=f0, f1=f1: e.scalar_tensor_tensor(
                xr[:, i * TAU:(i + 1) * TAU], xv(ctx, "xbf", i, True, f0, f1),
                ctx.sb["cols"][:, 8 + i: 9 + i], xr[:, i * TAU:(i + 1) * TAU],
                ALU.mult, ALU.add),
                reads=[("xbf", i), ("cols", 8 + i), ("mixr", i)],
                writes=[("mixr", i)], tag=f"f0xr2{i}_{c}")
            P.op("ve", lambda e, i=i, f0=f0, f1=f1: e.tensor_scalar(
                xk[:, i * TAU:(i + 1) * TAU], xv(ctx, "xbf", i, False, f0, f1),
                ctx.sb["cols"][:, 16 + i: 17 + i], None, ALU.mult),
                reads=[("xbf", i), ("cols", 16 + i)], writes=[("mixk", i)], tag=f"f0xk{i}_{c}")
            P.op("ve", lambda e, i=i, f0=f0, f1=f1: e.scalar_tensor_tensor(
                xk[:, i * TAU:(i + 1) * TAU], xv(ctx, "xbf", i, True, f0, f1),
                ctx.sb["cols"][:, 24 + i: 25 + i], xk[:, i * TAU:(i + 1) * TAU],
                ALU.mult, ALU.add),
                reads=[("xbf", i), ("cols", 24 + i), ("mixk", i)],
                writes=[("mixk", i)], tag=f"f0xk2{i}_{c}")
        for m in range(NT):
            bk = m % 4
            ap = psA(ctx, bk, 128, 0, TAU)
            for i in range(NT):
                P.op("te", lambda e, m=m, i=i, ap=ap: e.matmul(
                    ap, wtile(ctx, "wrT", i, D, m * 128, (m + 1) * 128),
                    xr[:, i * TAU:(i + 1) * TAU],
                    start=(i == 0), stop=(i == NT - 1)),
                    reads=[("wrT", "t"), ("mixr", i)], writes=[RA(bk)],
                    tag=f"f0rm{m}_{i}_{c}")
            P.op("ac", lambda e, m=m, ap=ap: e.activation(
                ctx.sb["wnat"][:, m * TAU:(m + 1) * TAU], ap, AF.Sigmoid),
                reads=[RA(bk)], writes=[("wnat", "rb", m)], tag=f"f0re{m}_{c}")
            ap2 = psB(ctx, m % 4, 128, 0, TAU)
            for i in range(NT):
                P.op("te", lambda e, m=m, i=i, ap2=ap2: e.matmul(
                    ap2, wtile(ctx, "wkT", i, D, m * 128, (m + 1) * 128),
                    xk[:, i * TAU:(i + 1) * TAU],
                    start=(i == 0), stop=(i == NT - 1)),
                    reads=[("wkT", "t"), ("mixk", i)], writes=[RB(m % 4)],
                    tag=f"f0km{m}_{i}_{c}")
            P.op("ac", lambda e, m=m, ap2=ap2: e.activation(
                ctx.sb["wnat"][:, 2048 + m * TAU:2048 + (m + 1) * TAU], ap2, AF.Relu),
                reads=[RB(m % 4)], writes=[("wnat", "ka", m)], tag=f"f0ke{m}_{c}")
            P.op("ac", lambda e, m=m: e.activation(
                ctx.sb["wnat"][:, 2048 + m * TAU:2048 + (m + 1) * TAU],
                ctx.sb["wnat"][:, 2048 + m * TAU:2048 + (m + 1) * TAU], AF.Square),
                reads=[("wnat", "ka", m)], writes=[("wnat", "ka", m)], tag=f"f0ks{m}_{c}")
        for m in range(NT):
            bk = m % 4
            ap = psA(ctx, bk, 128, 0, TAU)
            for i in range(NT):
                P.op("te", lambda e, m=m, i=i, ap=ap: e.matmul(
                    ap, wtile(ctx, "wvT", i, D, m * 128, (m + 1) * 128),
                    ctx.sb["wnat"][:, 2048 + i * TAU:2048 + (i + 1) * TAU],
                    start=(i == 0), stop=(i == NT - 1)),
                    reads=[("wvT", "t"), ("wnat", "ka", i)], writes=[RA(bk)],
                    tag=f"f0vm{m}_{i}_{c}")
            sc = ctx.sb["scr32"]
            sl = m % 2
            P.op("ve", lambda e, m=m, ap=ap, sl=sl: e.tensor_tensor(
                sc[:, sl * 256: sl * 256 + TAU], ap,
                ctx.sb["wnat"][:, m * TAU:(m + 1) * TAU], ALU.mult),
                reads=[RA(bk), ("wnat", "rb", m)], writes=[("scr32", 0)], tag=f"f0o1{m}_{c}")
            P.op("ve", lambda e, m=m, f0=f0, f1=f1, sl=sl: e.tensor_tensor(
                x[:, m * S + f0: m * S + f1], x[:, m * S + f0: m * S + f1],
                sc[:, sl * 256: sl * 256 + TAU], ALU.add),
                reads=[("xcur", m), ("scr32", 0)], writes=[("xcur", m)],
                tag=f"f0o2{m}_{c}")


def emit_l0_ffn_loads(ctx):
    P = ctx.P
    wr = ctx.din("L0_ffn_W_r", (D, D))
    wk = ctx.din("L0_ffn_W_k", (D, D))
    wv = ctx.din("L0_ffn_W_v", (D, D))
    mur = ctx.din("L0_ffn_mu_r", (D,))
    muk = ctx.din("L0_ffn_mu_k", (D,))
    emit_wload(ctx, wr, D, D, "wnat", "ld_wr", "f0lwr")
    emit_wtranspose(ctx, "wnat", D, D, "wrT", "f0twr")
    emit_wload(ctx, wk, D, D, "wnat", "ld_wk", "f0lwk")
    emit_wtranspose(ctx, "wnat", D, D, "wkT", "f0twk")
    emit_wload(ctx, wv, D, D, "wnat", "ld_wv", "f0lwv")
    emit_wtranspose(ctx, "wnat", D, D, "wvT", "f0twv")
    emit_colload(ctx, mur, NT, "cols", 0, "ld_mu", "f0lmr")
    emit_colload(ctx, muk, NT, "cols", 16, "ld_mu", "f0lmk")
    for i in range(NT):
        P.op("ve", lambda e, i=i: e.tensor_scalar(
            ctx.sb["cols"][:, 8 + i: 9 + i], ctx.sb["cols"][:, 0 + i: 1 + i],
            -1.0, 1.0, ALU.mult, ALU.add),
            reads=[("cols", 0 + i)], writes=[("cols", 8 + i)], tag=f"f0c1m{i}")
        P.op("ve", lambda e, i=i: e.tensor_scalar(
            ctx.sb["cols"][:, 24 + i: 25 + i], ctx.sb["cols"][:, 16 + i: 17 + i],
            -1.0, 1.0, ALU.mult, ALU.add),
            reads=[("cols", 16 + i)], writes=[("cols", 24 + i)], tag=f"f0c1k{i}")


def alloc_l0_ffn(ctx):
    TAU = 256
    ctx.sbuf("xbf", (128, NT * (S + 1)), BF16)
    ctx.sbuf("sqc", (128, NT * 256), BF16)
    ctx.sbuf("scr32", (128, 512), F32)
    ctx.sbuf("mixr", (128, NT * TAU), BF16)
    ctx.sbuf("mixk", (128, NT * TAU), BF16)
    ctx.sbuf("wnat", (128, NT * D), BF16)
    ctx.sbuf("wrT", (128, NT * D), BF16)
    ctx.sbuf("wkT", (128, NT * D), BF16)
    ctx.sbuf("wvT", (128, NT * D), BF16)



def emit_l0_attn(ctx):
    P = ctx.P
    x = ctx.sb["xcur"]
    SC = 1.0 / math.sqrt(32.0)
    emit_cast_x_to_bf(ctx, "xbf", tagp="t0cx")
    src_tiles = [((lambda f0, f1, i=i: xv(ctx, "xbf", i, False, f0, f1)),
                  [("xbf", i)]) for i in range(NT)]
    emit_ln_stats(ctx, src_tiles, "sqc", 0, 1, 1, 2, LN_EPS, "t0st")
    P.op("ve", lambda e: e.reciprocal(row32(ctx, 3), row32(ctx, 1)),
         reads=[R32(1)], writes=[R32(3)], tag="t0u")
    P.op("ve", lambda e: e.tensor_scalar(
        row32(ctx, 3), row32(ctx, 3), -LN_EPS, 1.0 + RMS_EPS, ALU.mult, ALU.add),
        reads=[R32(3)], writes=[R32(3)], tag="t0u2")
    P.op("ac", lambda e: e.activation(row32(ctx, 3), row32(ctx, 3), AF.Ln),
         reads=[R32(3)], writes=[R32(3)], tag="t0ln")
    P.op("ac", lambda e: e.activation(row32(ctx, 3), row32(ctx, 3), AF.Exp, scale=-0.5),
         reads=[R32(3)], writes=[R32(3)], tag="t0ex")
    emit_row_pair(ctx, lambda f0, f1: row32(ctx, 1, f0, f1), R32(1),
                  lambda f0, f1: row32(ctx, 2, f0, f1), [R32(2)],
                  lambda f0, f1: row32(ctx, 3, f0, f1), [R32(3)],
                  ALU.mult, "t0g")
    emit_row_pair(ctx, lambda f0, f1: row32(ctx, 3, f0, f1), R32(3),
                  lambda f0, f1: row32(ctx, 0, f0, f1), [R32(0)],
                  lambda f0, f1: row32(ctx, 1, f0, f1), [R32(1)],
                  ALU.mult, "t0mg")
    emit_row_to_dram(ctx, 1, 0, "t0sd0")
    emit_row_to_dram(ctx, 3, 1, "t0sd1")
    for c in range(4):
        f0, f1 = c * 512, (c + 1) * 512
        bc_r, kr = emit_bcast_dram(ctx, 0, 0, f"t0br{c}", f0, f1)
        bc_m, km = emit_bcast_dram(ctx, 1, 1, f"t0bm{c}", f0, f1)
        for i in range(NT):
            sc = ctx.sb["scr32"]
            P.op("ve", lambda e, i=i, f0=f0, f1=f1, bc_r=bc_r: e.tensor_tensor(
                sc[:, 0:512], x[:, i * S + f0: i * S + f1], bc_r, ALU.mult),
                reads=[("xcur", i), kr], writes=[("scr32", 0)], tag=f"t0l1_{i}_{c}")
            P.op("ve", lambda e, i=i, f0=f0, f1=f1, bc_m=bc_m: e.tensor_tensor(
                xv(ctx, "xbf", i, False, f0, f1), sc[:, 0:512], bc_m, ALU.subtract),
                reads=[("scr32", 0), km], writes=[("xbf", i)], tag=f"t0l2_{i}_{c}")
    for i in range(NT):
        P.op("ve", lambda e, i=i: e.tensor_scalar(
            ctx.sb["cols"][:, 8 + i: 9 + i], ctx.sb["cols"][:, 16 + i: 17 + i],
            ctx.sb["cols"][:, 0 + i: 1 + i], None, ALU.mult),
            reads=[("cols", 0 + i), ("cols", 16 + i)], writes=[("cols", 8 + i)],
            tag=f"t0wl{i}a")
        P.op("ve", lambda e, i=i: e.tensor_tensor(
            ctx.sb["cols"][:, 8 + i: 9 + i], ctx.sb["cols"][:, 16 + i: 17 + i],
            ctx.sb["cols"][:, 8 + i: 9 + i], ALU.subtract),
            reads=[("cols", 8 + i), ("cols", 16 + i)], writes=[("cols", 8 + i)],
            tag=f"t0wl{i}b")
        for c in range(4):
            f0, f1 = c * 512, (c + 1) * 512
            P.op("ve", lambda e, i=i, f0=f0, f1=f1: e.tensor_scalar(
                ctx.sb["d1s"][:, 0:512], xv(ctx, "xbf", i, False, f0, f1),
                ctx.sb["cols"][:, 8 + i: 9 + i], None, ALU.mult),
                reads=[("xbf", i), ("cols", 8 + i)], writes=[("d1s",)],
                tag=f"t0d1_{i}_{c}")
            P.op("ve", lambda e, i=i, c=c: e.tensor_tensor_scan(
                ctx.sb["scr32"][:, 0:512],
                ctx.sb["cols"].ap()[:, 0 + i: 1 + i].broadcast_to((128, 512)),
                ctx.sb["d1s"][:, 0:512],
                0.0 if c == 0 else ctx.sb["cols"].ap()[:, 37:38],
                ALU.mult, ALU.add),
                reads=[("d1s",), ("cols", 0 + i), ("cols", 37)],
                writes=[("scr32", 0)], tag=f"t0sc{i}_{c}")
            P.op("ve", lambda e: e.tensor_copy(
                ctx.sb["cols"][:, 37:38], ctx.sb["scr32"][:, 511:512]),
                reads=[("scr32", 0)], writes=[("cols", 37)], tag=f"t0cy{i}_{c}")
            P.op("ac", lambda e, i=i, f0=f0, f1=f1: e.activation(
                ctx.sb["mixed"][:, i * S + f0: i * S + f1],
                ctx.sb["scr32"][:, 0:512], AF.Copy),
                reads=[("scr32", 0)], writes=[("mixed", i)], tag=f"t0se{i}_{c}")
    for i in range(NT):
        P.op("ve", lambda e, i=i: e.reduce_sum(
            ctx.sb["cols"][:, 38 + i: 39 + i], xv(ctx, "xbf", i), axis=AX.X),
            reads=[("xbf", i)], writes=[("cols", 38 + i)], tag=f"t0rm{i}")
        P.op("ve", lambda e, i=i: e.scalar_tensor_tensor(
            ctx.sb["colsb"][:, i: i + 1], ctx.sb["cols"][:, 38 + i: 39 + i],
            1.0 / S, ctx.sb["cols"][:, 16 + i: 17 + i], ALU.mult, ALU.mult),
            reads=[("cols", 38 + i), ("cols", 16 + i)], writes=[("colsb", i)],
            tag=f"t0ri{i}")
    for m4 in range(4):
        for i in range(NT):
            P.op("te", lambda e, m4=m4, i=i: e.matmul(
                psA(ctx, 3, 128, 0, 1),
                wtile(ctx, "rw1", i, 512, m4 * 128, (m4 + 1) * 128),
                ctx.sb["colsb"][:, i: i + 1],
                start=(i == 0), stop=(i == NT - 1)),
                reads=[("rw1",), ("colsb", i)], writes=[RA(3)], tag=f"t0h{m4}_{i}")
        P.op("ac", lambda e, m4=m4: e.activation(
            ctx.sb["colsb"][:, 8 + m4: 9 + m4], psA(ctx, 3, 128, 0, 1), AF.Silu),
            reads=[RA(3)], writes=[("colsb", 8 + m4)], tag=f"t0he{m4}")
    for m4 in range(4):
        P.op("te", lambda e, m4=m4: e.matmul(
            psA(ctx, 3, 1, 0, 3), ctx.sb["colsb"][:, 8 + m4: 9 + m4],
            wtile(ctx, "rw2", m4, 3), start=(m4 == 0), stop=(m4 == 3)),
            reads=[("rw2",), ("colsb", 8 + m4)], writes=[RA(3)], tag=f"t0lg{m4}")
    P.op("ve", lambda e: e.tensor_tensor(
        row32(ctx, 3, 0, 3), psA(ctx, 3, 1, 0, 3), ctx.sb["b2row"][0:1, 0:3],
        ALU.add), reads=[RA(3), ("b2row",)], writes=[R32(3)], tag="t0lgb")
    P.op("ve", lambda e: e.reduce_max(
        row32(ctx, 3, 4, 5), row32(ctx, 3, 0, 3), axis=AX.X, negate=True),
        reads=[R32(3)], writes=[R32(3)], tag="t0sm1")
    P.op("ac", lambda e: e.activation(
        row32(ctx, 3, 0, 3), row32(ctx, 3, 0, 3), AF.Exp,
        bias=row32(ctx, 3, 4, 5)),
        reads=[R32(3)], writes=[R32(3)], tag="t0sm2")
    P.op("ve", lambda e: e.reduce_sum(
        row32(ctx, 3, 4, 5), row32(ctx, 3, 0, 3), axis=AX.X),
        reads=[R32(3)], writes=[R32(3)], tag="t0sm3")
    P.op("ve", lambda e: e.reciprocal(row32(ctx, 3, 4, 5), row32(ctx, 3, 4, 5)),
         reads=[R32(3)], writes=[R32(3)], tag="t0sm4")
    P.op("ve", lambda e: e.tensor_tensor(
        row32(ctx, 3, 5, 6), row32(ctx, 3, 1, 2), row32(ctx, 3, 2, 3), ALU.add),
        reads=[R32(3)], writes=[R32(3)], tag="t0sm5")
    P.op("ve", lambda e: e.tensor_scalar(
        row16(ctx, 1, 0, 1), row32(ctx, 3, 5, 6), row32(ctx, 3, 4, 5), None,
        ALU.mult), reads=[R32(3)], writes=[R16(1)], tag="t0sm6")
    P.op("ve", lambda e: e.tensor_scalar(
        row16(ctx, 1, 1, 2), row32(ctx, 3, 2, 3), row32(ctx, 3, 4, 5), None,
        ALU.mult), reads=[R32(3)], writes=[R16(1)], tag="t0sm7")
    for j, cidx in ((0, 32), (1, 33)):
        P.op("te", lambda e, j=j: e.matmul(
            psA(ctx, 3, 128, 0, 1), ctx.sb["ones_row"][0:1, :],
            row16(ctx, 1, j, j + 1), start=True, stop=True),
            reads=[R16(1), ("ones_row",)], writes=[RA(3)], tag=f"t0pb{j}")
        P.op("ve", lambda e, cidx=cidx: e.tensor_scalar(
            ctx.sb["cols"][:, cidx: cidx + 1], psA(ctx, 3, 128, 0, 1), 1.0,
            None, ALU.mult),
            reads=[RA(3)], writes=[("cols", cidx)], tag=f"t0pe{cidx}")
    for nm, buf in (("wq1", "q1b"), ("wq2", "q2b"), ("wk1", "k1b"), ("wk2", "k2b")):
        for c in range(4):
            f0, f1 = c * 512, (c + 1) * 512
            for i in range(NT):
                P.op("te", lambda e, nm=nm, i=i, f0=f0, f1=f1: e.matmul(
                    psA(ctx, 3, 32, 0, 512), wtile(ctx, nm, i, 32),
                    xv(ctx, "xbf", i, False, f0, f1),
                    start=(i == 0), stop=(i == NT - 1)),
                    reads=[(nm,), ("xbf", i)], writes=[RA(3)], tag=f"t0p{nm}{c}_{i}")
            P.op("ac", lambda e, buf=buf, f0=f0, f1=f1: e.activation(
                ctx.sb[buf][0:32, f0:f1], psA(ctx, 3, 32, 0, 512), AF.Copy),
                reads=[RA(3)], writes=[(buf, c)], tag=f"t0pe{buf}{c}")
    for tt in range(16):
        for i in range(NT):
            P.op("te", lambda e, tt=tt, i=i: e.matmul(
                psA(ctx, 3, 128, 0, 32),
                xv(ctx, "xbf", i, False, tt * 128, (tt + 1) * 128),
                wtile(ctx, "wv", i, 32),
                start=(i == 0), stop=(i == NT - 1)),
                reads=[("wv",), ("xbf", i)], writes=[RA(3)], tag=f"t0v{tt}_{i}")
        P.op("ac", lambda e, tt=tt: e.activation(
            ctx.sb["vtok"][:, tt * 32:(tt + 1) * 32], psA(ctx, 3, 128, 0, 32),
            AF.Copy), reads=[RA(3)], writes=[("vtok", tt)], tag=f"t0ve{tt}")
    for mi, (qb, kb) in enumerate((("q1b", "k1b"), ("q2b", "k2b"))):
        for kt in range(16):
            for qc in range(4):
                bk = (kt * 4 + qc) % 2
                P.op("te", lambda e, kt=kt, qc=qc, qb=qb, kb=kb, bk=bk: e.matmul(
                    psA(ctx, bk, 128, 0, 512),
                    ctx.sb[kb][0:32, kt * 128:(kt + 1) * 128],
                    ctx.sb[qb][0:32, qc * 512:(qc + 1) * 512],
                    start=True, stop=True),
                    reads=[(kb, kt // 4), (qb, qc)], writes=[RA(bk)],
                    tag=f"t0s{mi}m{kt}_{qc}")
                P.op("ve", lambda e, bk=bk: e.reduce_max(
                    ctx.sb["cols"][:, 40 + bk: 41 + bk], psA(ctx, bk, 128, 0, 512),
                    axis=AX.X), reads=[RA(bk)], writes=[("cols", 40 + bk)],
                    tag=f"t0s{mi}r{kt}_{qc}")
                first = (kt == 0 and qc == 0)
                P.op("ve", lambda e, bk=bk, first=first: (
                    e.tensor_copy(ctx.sb["cols"][:, 42:43],
                                  ctx.sb["cols"][:, 40 + bk: 41 + bk]) if first
                    else e.tensor_tensor(
                        ctx.sb["cols"][:, 42:43], ctx.sb["cols"][:, 42:43],
                        ctx.sb["cols"][:, 40 + bk: 41 + bk], ALU.max)),
                    reads=[("cols", 40 + bk), ("cols", 42)],
                    writes=[("cols", 42)], tag=f"t0s{mi}x{kt}_{qc}")
        P.op("ve", lambda e: e.tensor_scalar(
            ctx.sb["colsb"][:, 12:13], ctx.sb["cols"][:, 42:43], 1.0, None,
            ALU.mult), reads=[("cols", 42)], writes=[("colsb", 12)], tag=f"t0g{mi}a")
        P.op("te", lambda e: e.transpose(
            psbf(ctx, "B", 3, 1, 0, 128), ctx.sb["colsb"][:, 12:13],
            ctx.sb["ident"][:, :]),
            reads=[("colsb", 12), ("ident",)], writes=[RB(3)], tag=f"t0g{mi}t")
        P.op("ve", lambda e: e.reduce_max(
            row32(ctx, 3, 8, 9), psbf(ctx, "B", 3, 1, 0, 128), axis=AX.X),
            reads=[RB(3)], writes=[R32(3)], tag=f"t0g{mi}r")
        P.op("ve", lambda e: e.tensor_scalar(
            row32(ctx, 3, 8, 9), row32(ctx, 3, 8, 9), -SC, 30.0, ALU.mult, ALU.add),
            reads=[R32(3)], writes=[R32(3)], tag=f"t0g{mi}b")
        P.op("ve", lambda e: e.tensor_scalar(
            row16(ctx, 1, 4, 5), row32(ctx, 3, 8, 9), 1.0, None, ALU.mult),
            reads=[R32(3)], writes=[R16(1)], tag=f"t0g{mi}c")
        P.op("te", lambda e: e.matmul(
            psA(ctx, 3, 128, 0, 1), ctx.sb["ones_row"][0:1, :],
            row16(ctx, 1, 4, 5), start=True, stop=True),
            reads=[R16(1), ("ones_row",)], writes=[RA(3)], tag=f"t0g{mi}d")
        P.op("ve", lambda e, mi=mi: e.tensor_scalar(
            ctx.sb["cols"][:, 35 + mi: 36 + mi], psA(ctx, 3, 128, 0, 1), 1.0,
            None, ALU.mult),
            reads=[RA(3)], writes=[("cols", 35 + mi)], tag=f"t0g{mi}e")
        for qc in range(4):
            f0, f1 = qc * 512, (qc + 1) * 512
            for kt in range(16):
                bk = kt % 2
                P.op("te", lambda e, kt=kt, qc=qc, qb=qb, kb=kb, bk=bk: e.matmul(
                    psA(ctx, bk, 128, 0, 512),
                    ctx.sb[kb][0:32, kt * 128:(kt + 1) * 128],
                    ctx.sb[qb][0:32, qc * 512:(qc + 1) * 512],
                    start=True, stop=True),
                    reads=[(kb, kt // 4), (qb, qc)], writes=[RA(bk)],
                    tag=f"t0e{mi}m{kt}_{qc}")
                P.op("ac", lambda e, bk=bk, mi=mi: e.activation(
                    ctx.sb["ebuf"][:, bk * 512:(bk + 1) * 512],
                    psA(ctx, bk, 128, 0, 512), AF.Exp, scale=SC,
                    bias=ctx.sb["cols"][:, 35 + mi: 36 + mi]),
                    reads=[RA(bk), ("cols", 35 + mi)], writes=[("ebuf", bk)],
                    tag=f"t0e{mi}e{kt}_{qc}")
                P.op("te", lambda e, kt=kt, bk=bk: e.matmul(
                    psB(ctx, 0, 32, 0, 512), ctx.sb["vtok"][:, kt * 32:(kt + 1) * 32],
                    ctx.sb["ebuf"][:, bk * 512:(bk + 1) * 512],
                    start=(kt == 0), stop=(kt == 15)),
                    reads=[("vtok", kt), ("ebuf", bk)], writes=[RB(0)],
                    tag=f"t0e{mi}n{kt}_{qc}")
                P.op("te", lambda e, kt=kt, bk=bk: e.matmul(
                    psB(ctx, 1, 1, 0, 512), ctx.sb["ones_col"][:, :],
                    ctx.sb["ebuf"][:, bk * 512:(bk + 1) * 512],
                    start=(kt == 0), stop=(kt == 15)),
                    reads=[("ebuf", bk), ("ones_col",)], writes=[RB(1)],
                    tag=f"t0e{mi}d{kt}_{qc}")
            P.op("ve", lambda e: e.tensor_scalar(
                row32(ctx, 3, 16, 16 + 512), psB(ctx, 1, 1, 0, 512), 1.0, 1e-30,
                ALU.mult, ALU.add), reads=[RB(1)], writes=[R32(3)], tag=f"t0d{mi}a{qc}")
            P.op("ve", lambda e: e.reciprocal(
                row32(ctx, 3, 16, 16 + 512), row32(ctx, 3, 16, 16 + 512)),
                reads=[R32(3)], writes=[R32(3)], tag=f"t0d{mi}b{qc}")
            P.op("ve", lambda e: e.tensor_scalar(
                row16(ctx, 1, 16, 16 + 512), row32(ctx, 3, 16, 16 + 512), 1.0,
                None, ALU.mult), reads=[R32(3)], writes=[R16(1)], tag=f"t0d{mi}c{qc}")
            P.op("te", lambda e: e.matmul(
                psB(ctx, 2, 32, 0, 512), ctx.sb["ones_row"][0:1, 0:32],
                row16(ctx, 1, 16, 16 + 512), start=True, stop=True),
                reads=[R16(1), ("ones_row",)], writes=[RB(2)], tag=f"t0d{mi}d{qc}")
            P.op("ac", lambda e: e.activation(
                ctx.sb["nsc"][0:32, 0:512], psB(ctx, 0, 32, 0, 512), AF.Copy),
                reads=[RB(0)], writes=[("nsc", 0)], tag=f"t0d{mi}e{qc}")
            if mi == 0:
                P.op("ve", lambda e, f0=f0, f1=f1: e.tensor_tensor(
                    ctx.sb["ctxb"][0:32, f0:f1], ctx.sb["nsc"][0:32, 0:512],
                    psB(ctx, 2, 32, 0, 512), ALU.mult),
                    reads=[("nsc", 0), RB(2)], writes=[("ctxb", qc)],
                    tag=f"t0d{mi}f{qc}")
            else:
                P.op("ve", lambda e: e.tensor_tensor(
                    ctx.sb["nsc"][0:32, 512:1024], ctx.sb["nsc"][0:32, 0:512],
                    psB(ctx, 2, 32, 0, 512), ALU.mult),
                    reads=[("nsc", 0), RB(2)], writes=[("nsc", 1)],
                    tag=f"t0d{mi}f{qc}")
                P.op("ve", lambda e, f0=f0, f1=f1: e.scalar_tensor_tensor(
                    ctx.sb["ctxb"][0:32, f0:f1], ctx.sb["nsc"][0:32, 512:1024],
                    ctx.sb["cols"][0:32, 34:35], ctx.sb["ctxb"][0:32, f0:f1],
                    ALU.mult, ALU.add),
                    reads=[("nsc", 1), ("cols", 34), ("ctxb", qc)],
                    writes=[("ctxb", qc)], tag=f"t0d{mi}g{qc}")
    for m in range(NT):
        for c in range(4):
            bk = (m * 4 + c) % 2
            P.op("te", lambda e, m=m, c=c, bk=bk: e.matmul(
                psA(ctx, bk, 128, 0, 512), ctx.sb["wo"][0:32, m * 128:(m + 1) * 128],
                ctx.sb["ctxb"][0:32, c * 512:(c + 1) * 512],
                start=True, stop=True),
                reads=[("wo",), ("ctxb", c)], writes=[RA(bk)], tag=f"t0o{m}_{c}")
            P.op("ve", lambda e, m=m, c=c, bk=bk: e.scalar_tensor_tensor(
                ctx.sb["mixed"][:, m * S + c * 512: m * S + (c + 1) * 512],
                psA(ctx, bk, 128, 0, 512), ctx.sb["cols"][:, 32:33],
                ctx.sb["mixed"][:, m * S + c * 512: m * S + (c + 1) * 512],
                ALU.mult, ALU.add),
                reads=[RA(bk), ("cols", 32), ("mixed", m)], writes=[("mixed", m)],
                tag=f"t0oe{m}_{c}")
    for r in range(8):
        pap = psbf(ctx, "B", 3, 128, 0, 64)
        P.op("te", lambda e, r=r, pap=pap: e.transpose(
            pap, ctx.sb["lm"][0:64, r * 128:(r + 1) * 128],
            ctx.sb["ident"][0:64, 0:64]),
            reads=[("lm",), ("ident",)], writes=[RB(3)], tag=f"t0lt{r}")
        P.op("ac", lambda e, r=r, pap=pap: e.activation(
            ctx.sb["lmT"][:, r * 64:(r + 1) * 64], pap, AF.Copy),
            reads=[RB(3)], writes=[("lmT", r)], tag=f"t0le{r}")
        P.op("ve", lambda e, r=r: e.tensor_scalar(
            ctx.sb["lmT"][:, r * 64:(r + 1) * 64],
            ctx.sb["lmT"][:, r * 64:(r + 1) * 64],
            ctx.sb["cols"][:, 16 + r: 17 + r], None, ALU.mult),
            reads=[("lmT", r), ("cols", 16 + r)], writes=[("lmT", r)], tag=f"t0lf{r}")
    for r in range(8):
        P.op("te", lambda e, r=r: e.matmul(
            psA(ctx, 3, 64, 0, 1), ctx.sb["lmT"][:, r * 64:(r + 1) * 64],
            xv(ctx, "xbf", r, False, 2047, 2048),
            start=(r == 0), stop=(r == NT - 1)),
            reads=[("lmT", r), ("xbf", r)], writes=[RA(3)], tag=f"t0ls{r}")
    P.op("ve", lambda e: e.tensor_scalar(
        ctx.sb["colsb"][0:64, 13:14], psA(ctx, 3, 64, 0, 1), 1.0, None, ALU.mult),
        reads=[RA(3)], writes=[("colsb", 13)], tag="t0lse")
    P.op("te", lambda e: e.transpose(
        psbf(ctx, "B", 3, 1, 0, 64), ctx.sb["colsb"][0:64, 13:14],
        ctx.sb["ident"][0:64, 0:64]),
        reads=[("colsb", 13), ("ident",)], writes=[RB(3)], tag="t0lst")
    P.op("ve", lambda e: e.tensor_scalar(
        row32(ctx, 3, 16, 16 + 64), psbf(ctx, "B", 3, 1, 0, 64), 1.0 / 32.0,
        None, ALU.mult), reads=[RB(3)], writes=[R32(3)], tag="t0lsr")
    P.op("ve", lambda e: e.reduce_max(
        row32(ctx, 3, 0, 1), row32(ctx, 3, 16, 16 + 64), axis=AX.X, negate=True),
        reads=[R32(3)], writes=[R32(3)], tag="t0lm1")
    P.op("ac", lambda e: e.activation(
        row32(ctx, 3, 16, 16 + 64), row32(ctx, 3, 16, 16 + 64), AF.Exp,
        bias=row32(ctx, 3, 0, 1)),
        reads=[R32(3)], writes=[R32(3)], tag="t0lm2")
    P.op("ve", lambda e: e.reduce_sum(
        row32(ctx, 3, 0, 1), row32(ctx, 3, 16, 16 + 64), axis=AX.X),
        reads=[R32(3)], writes=[R32(3)], tag="t0lm3")
    P.op("ve", lambda e: e.reciprocal(row32(ctx, 3, 0, 1), row32(ctx, 3, 0, 1)),
         reads=[R32(3)], writes=[R32(3)], tag="t0lm4")
    P.op("ve", lambda e: e.tensor_scalar(
        row16(ctx, 1, 16, 16 + 64), row32(ctx, 3, 16, 16 + 64),
        row32(ctx, 3, 0, 1), None, ALU.mult),
        reads=[R32(3)], writes=[R16(1)], tag="t0lm5")
    P.op("te", lambda e: e.transpose(
        psbf(ctx, "B", 3, 64, 0, 1), row16(ctx, 1, 16, 16 + 64),
        ctx.sb["ident"][0:1, 0:1]),
        reads=[R16(1), ("ident",)], writes=[RB(3)], tag="t0lm6")
    P.op("ac", lambda e: e.activation(
        ctx.sb["colsb"][0:64, 14:15], psbf(ctx, "B", 3, 64, 0, 1), AF.Copy),
        reads=[RB(3)], writes=[("colsb", 14)], tag="t0lm7")
    for r in range(8):
        P.op("te", lambda e, r=r: e.matmul(
            psA(ctx, 3, 128, 0, 1), ctx.sb["lm"][0:64, r * 128:(r + 1) * 128],
            ctx.sb["colsb"][0:64, 14:15], start=True, stop=True),
            reads=[("lm",), ("colsb", 14)], writes=[RA(3)], tag=f"t0ar{r}")
        P.op("ve", lambda e, r=r: e.scalar_tensor_tensor(
            ctx.sb["mixed"][:, r * S + 2047: r * S + 2048],
            psA(ctx, 3, 128, 0, 1), ctx.sb["cols"][:, 33:34],
            ctx.sb["mixed"][:, r * S + 2047: r * S + 2048], ALU.mult, ALU.add),
            reads=[RA(3), ("cols", 33), ("mixed", r)], writes=[("mixed", r)],
            tag=f"t0ae{r}")
    P.op("ve", lambda e: e.engine_nop(),
         reads=[("q1b",), ("q2b",), ("k1b",), ("k2b",), ("ctxb",), ("ebuf",),
                ("nsc",), ("vtok",)] +
               [("q1b", c) for c in range(4)] + [("q2b", c) for c in range(4)] +
               [("k1b", c) for c in range(4)] + [("k2b", c) for c in range(4)] +
               [("ctxb", c) for c in range(4)] + [("ebuf", b) for b in range(2)] +
               [("nsc", j) for j in range(2)] + [("vtok", t) for t in range(16)],
         writes=[("mixo",)], tag="t0fence1")
    P.op("ve", lambda e: e.engine_nop(),
         reads=[("rw1",)], writes=[("mwb",)], tag="t0fence2")
    for m in range(NT):
        P.dma("gp", ctx.sb["mwb"][:, (m % 2) * 2048:(m % 2) * 2048 + 2048]
              .rearrange("p (r c) -> p r c", r=16),
              ctx.dram["L0_attn_mix_w"][:, m * 128:(m + 1) * 128]
              .rearrange("(r p) c -> p r c", p=128),
              sem=f"ld_mw{m % 2}", writes=[("mwb", m % 2)], tag=f"t0mw{m}")
        for j in range(8):
            P.op("ve", lambda e, m=m, j=j: e.tensor_scalar(
                ctx.sb["mwb"][:, (m % 2) * 2048 + (8 + j) * 128:
                              (m % 2) * 2048 + (9 + j) * 128],
                ctx.sb["mwb"][:, (m % 2) * 2048 + (8 + j) * 128:
                              (m % 2) * 2048 + (9 + j) * 128],
                ctx.sb["cols"][:, 16 + j: 17 + j], None, ALU.mult),
                reads=[("mwb", m % 2), ("cols", 16 + j)], writes=[("mwb", m % 2)],
                tag=f"t0mf{m}_{j}")
        for c in range(4):
            bk = 2 + (m * 4 + c) % 2
            f0, f1 = c * 512, (c + 1) * 512
            for j in range(16):
                rkey = ("mixed", j) if j < 8 else ("xbf", j - 8)
                P.op("te", lambda e, m=m, c=c, j=j, bk=bk, f0=f0, f1=f1: e.matmul(
                    psA(ctx, bk, 128, 0, 512),
                    ctx.sb["mwb"][:, (m % 2) * 2048 + j * 128:(m % 2) * 2048 + (j + 1) * 128],
                    (ctx.sb["mixed"][:, j * S + f0: j * S + f1] if j < 8
                     else xv(ctx, "xbf", j - 8, False, f0, f1)),
                    start=(j == 0), stop=(j == 15)),
                    reads=[("mwb", m % 2), rkey], writes=[RA(bk)],
                    tag=f"t0mm{m}_{c}_{j}")
            P.op("ac", lambda e, m=m, bk=bk, f0=f0, f1=f1: e.activation(
                ctx.sb["mixo"][:, m * S + f0: m * S + f1], psA(ctx, bk, 128, 0, 512),
                AF.Copy), reads=[RA(bk)], writes=[("mixo", m)], tag=f"t0me{m}_{c}")
    for c in range(8):
        f0, f1 = c * 256, (c + 1) * 256
        for i in range(NT):
            P.op("ac", lambda e, i=i, f0=f0, f1=f1: e.activation(
                ctx.sb["sqc"][:, i * 256:(i + 1) * 256],
                ctx.sb["mixo"][:, i * S + f0: i * S + f1], AF.Square),
                reads=[("mixo", i)], writes=[("sqc", i)], tag=f"t0q{c}_{i}")
            P.op("te", lambda e, i=i: e.matmul(
                psA(ctx, 1, 1, 0, 256), ctx.sb["ones_col"][:, :],
                ctx.sb["sqc"][:, i * 256:(i + 1) * 256],
                start=(i == 0), stop=(i == NT - 1)),
                reads=[("sqc", i), ("ones_col",)], writes=[RA(1)], tag=f"t0qm{c}_{i}")
        P.op("ve", lambda e, f0=f0, f1=f1: e.tensor_scalar(
            row32(ctx, 3, f0, f1), psA(ctx, 1, 1, 0, 256), 1.0 / D, RMS_EPS,
            ALU.mult, ALU.add),
            reads=[RA(1)], writes=[R32(3)], tag=f"t0qe{c}")
    P.op("ac", lambda e: e.activation(row32(ctx, 3), row32(ctx, 3), AF.Ln),
         reads=[R32(3)], writes=[R32(3)], tag="t0q1")
    P.op("ac", lambda e: e.activation(row32(ctx, 3), row32(ctx, 3), AF.Exp, scale=-0.5),
         reads=[R32(3)], writes=[R32(3)], tag="t0q2")
    emit_row_pair(ctx, lambda f0, f1: row32(ctx, 1, f0, f1), R32(1),
                  lambda f0, f1: row32(ctx, 0, f0, f1), [R32(0)],
                  lambda f0, f1: row32(ctx, 2, f0, f1), [R32(2)],
                  ALU.mult, "t0fmr")
    emit_row_to_dram(ctx, 2, 0, "t0fs0")
    emit_row_to_dram(ctx, 1, 1, "t0fs1")
    emit_row_to_dram(ctx, 3, 2, "t0fs2")
    for c in range(4):
        f0, f1 = c * 512, (c + 1) * 512
        bc_r, kr = emit_bcast_dram(ctx, 0, 0, f"t0fe{c}", f0, f1)
        bc_m, km = emit_bcast_dram(ctx, 1, 1, f"t0ff{c}", f0, f1)
        for i in range(NT):
            sc = ctx.sb["scr32"]
            P.op("ve", lambda e, i=i, f0=f0, f1=f1, bc_r=bc_r: e.tensor_tensor(
                sc[:, 0:512], x[:, i * S + f0: i * S + f1], bc_r, ALU.mult),
                reads=[("xcur", i), kr], writes=[("scr32", 0)], tag=f"t0fh{i}_{c}")
            P.op("ve", lambda e, i=i, bc_m=bc_m: e.tensor_tensor(
                sc[:, 0:512], sc[:, 0:512], bc_m, ALU.subtract),
                reads=[("scr32", 0), km], writes=[("scr32", 0)], tag=f"t0fi{i}_{c}")
            P.op("ve", lambda e, i=i, f0=f0, f1=f1: e.tensor_tensor(
                x[:, i * S + f0: i * S + f1], x[:, i * S + f0: i * S + f1],
                sc[:, 0:512], ALU.add),
                reads=[("xcur", i), ("scr32", 0)], writes=[("xcur", i)],
                tag=f"t0fk0{i}_{c}")
        bc_s, ks_ = emit_bcast_dram(ctx, 2, 0, f"t0fg{c}", f0, f1)
        for i in range(NT):
            P.op("ve", lambda e, i=i, f0=f0, f1=f1, bc_s=bc_s: e.tensor_tensor(
                ctx.sb["d1s"][:, 0:512], ctx.sb["mixo"][:, i * S + f0: i * S + f1],
                bc_s, ALU.mult),
                reads=[("mixo", i), ks_], writes=[("d1s",)], tag=f"t0fj{i}_{c}")
            P.op("ve", lambda e, i=i, f0=f0, f1=f1: e.scalar_tensor_tensor(
                x[:, i * S + f0: i * S + f1], ctx.sb["d1s"][:, 0:512],
                ctx.sb["cols"][:, 24 + i: 25 + i], x[:, i * S + f0: i * S + f1],
                ALU.mult, ALU.add),
                reads=[("d1s",), ("cols", 24 + i), ("xcur", i)],
                writes=[("xcur", i)], tag=f"t0fl{i}_{c}")
    P.op("ve", lambda e: e.engine_nop(),
         reads=[], writes=[("mixed", j) for j in range(NT)] + [("buse",)],
         tag="t0busfence")
    P.dma("gp", ctx.sb["buse"][:, :], ctx.dram["L0_attn_bus_expand"],
          sem="ld_be", writes=[("buse",)], tag="t0lbe")
    for i in range(NT):
        P.op("ve", lambda e, i=i: e.reduce_sum(
            ctx.sb["cols"][:, 38 + i: 39 + i], x[:, i * S:(i + 1) * S], axis=AX.X),
            reads=[("xcur", i)], writes=[("cols", 38 + i)], tag=f"t0bm{i}")
        P.op("ve", lambda e, i=i: e.tensor_scalar(
            ctx.sb["colsb"][:, i: i + 1], ctx.sb["cols"][:, 38 + i: 39 + i],
            1.0 / S, None, ALU.mult),
            reads=[("cols", 38 + i)], writes=[("colsb", i)], tag=f"t0bb{i}")
    for i in range(NT):
        P.op("te", lambda e, i=i: e.matmul(
            psA(ctx, 3, 128, 0, 1), wtile(ctx, "busc", i, 128),
            ctx.sb["colsb"][:, i: i + 1], start=(i == 0), stop=(i == NT - 1)),
            reads=[("busc",), ("colsb", i)], writes=[RA(3)], tag=f"t0bs{i}")
    P.op("ve", lambda e: e.tensor_scalar(
        ctx.sb["colsb"][:, 12:13], psA(ctx, 3, 128, 0, 1), 1.0, None, ALU.mult),
        reads=[RA(3)], writes=[("colsb", 12)], tag="t0bse")
    P.op("te", lambda e: e.matmul(
        psA(ctx, 3, 128, 0, 1), ctx.sb["busp"][:, :], ctx.sb["colsb"][:, 12:13],
        start=True, stop=True),
        reads=[("busp",), ("colsb", 12)], writes=[RA(3)], tag="t0bp")
    P.op("ve", lambda e: e.tensor_scalar(
        ctx.sb["colsb"][:, 13:14], psA(ctx, 3, 128, 0, 1), 1.0, None, ALU.mult),
        reads=[RA(3)], writes=[("colsb", 13)], tag="t0bpe")
    for m in range(NT):
        P.op("te", lambda e, m=m: e.matmul(
            psA(ctx, 3, 128, 0, 1), ctx.sb["buse"][:, m * 128:(m + 1) * 128],
            ctx.sb["colsb"][:, 13:14], start=True, stop=True),
            reads=[("buse",), ("colsb", 13)], writes=[RA(3)], tag=f"t0be{m}")
        P.op("ve", lambda e, m=m: e.tensor_scalar(
            ctx.sb["cols"][:, 46:47], psA(ctx, 3, 128, 0, 1), 0.1, None, ALU.mult),
            reads=[RA(3)], writes=[("cols", 46)], tag=f"t0bee{m}")
        P.op("ve", lambda e, m=m: e.tensor_scalar(
            x[:, m * S:(m + 1) * S], x[:, m * S:(m + 1) * S],
            ctx.sb["cols"][:, 46:47], None, ALU.add),
            reads=[("xcur", m), ("cols", 46)], writes=[("xcur", m)], tag=f"t0ba{m}")


def emit_l0_attn_loads(ctx):
    P = ctx.P
    d = ctx.din
    d("L0_attn_w_norm_in", (D,))
    d("L0_attn_w_norm_out", (D,))
    d("L0_attn_router_w1", (D, 512))
    d("L0_attn_router_w2", (512, 3))
    d("L0_attn_router_b2", (3,))
    d("L0_attn_lambdas", (8,))
    d("L0_attn_slr_wq1", (D, 32))
    d("L0_attn_slr_wq2", (D, 32))
    d("L0_attn_slr_wk1", (D, 32))
    d("L0_attn_slr_wk2", (D, 32))
    d("L0_attn_slr_wv", (D, 32))
    d("L0_attn_slr_wo", (32, D))
    d("L0_attn_slr_lam", (1,))
    d("L0_attn_landmarks", (64, D))
    d("L0_attn_mix_w", (2 * D, D))
    d("L0_attn_bus_compress", (D, 128))
    d("L0_attn_bus_expand", (128, D))
    d("L0_attn_bus_publish", (128, 128))
    for i in range(NT):
        P.dma("gp", ctx.sb["cols"][:, 0 + i: 1 + i],
              ctx.dram["L0_attn_lambdas"][i:i + 1].unsqueeze(0).partition_broadcast(128),
              sem="ld_lam", writes=[("cols", 0 + i)], tag=f"t0llam{i}")
        P.dma("gp", ctx.sb["cols"][:, 16 + i: 17 + i],
              ctx.dram["L0_attn_w_norm_in"][i * 128:(i + 1) * 128].unsqueeze(1),
              sem="ld_wn", writes=[("cols", 16 + i)], tag=f"t0lwi{i}")
        P.dma("gp", ctx.sb["cols"][:, 24 + i: 25 + i],
              ctx.dram["L0_attn_w_norm_out"][i * 128:(i + 1) * 128].unsqueeze(1),
              sem="ld_wn", writes=[("cols", 24 + i)], tag=f"t0lwo{i}")
    P.dma("gp", ctx.sb["cols"][:, 34:35],
          ctx.dram["L0_attn_slr_lam"][0:1].unsqueeze(0).partition_broadcast(128),
          sem="ld_sl", writes=[("cols", 34)], tag="t0lsl")
    P.op("ve", lambda e: e.tensor_scalar(
        ctx.sb["cols"][:, 34:35], ctx.sb["cols"][:, 34:35], -1.0, None, ALU.mult),
        reads=[("cols", 34)], writes=[("cols", 34)], tag="t0lsln")
    emit_wload(ctx, ctx.dram["L0_attn_router_w1"], D, 512, "rw1", "ld_rw1", "t0lrw1")
    emit_wload(ctx, ctx.dram["L0_attn_router_w2"], 512, 3, "rw2", "ld_rw2", "t0lrw2")
    P.dma("gp", ctx.sb["b2row"][0:1, 0:3], ctx.dram["L0_attn_router_b2"].unsqueeze(0),
          sem="ld_b2", writes=[("b2row",)], tag="t0lb2")
    for nm, dn in (("wq1", "L0_attn_slr_wq1"), ("wq2", "L0_attn_slr_wq2"),
                   ("wk1", "L0_attn_slr_wk1"), ("wk2", "L0_attn_slr_wk2"),
                   ("wv", "L0_attn_slr_wv")):
        emit_wload(ctx, ctx.dram[dn], D, 32, nm, "ld_" + nm, "t0l" + nm)
        for i in range(NT):
            P.op("ve", lambda e, nm=nm, i=i: e.tensor_scalar(
                wtile(ctx, nm, i, 32), wtile(ctx, nm, i, 32),
                ctx.sb["cols"][:, 16 + i: 17 + i], None, ALU.mult),
                reads=[(nm,), ("cols", 16 + i)], writes=[(nm,)], tag=f"t0f{nm}{i}")
    P.dma("gp", ctx.sb["wo"][0:32, :], ctx.dram["L0_attn_slr_wo"],
          sem="ld_wo", writes=[("wo",)], tag="t0lwo2")
    P.dma("gp", ctx.sb["lm"][0:64, :], ctx.dram["L0_attn_landmarks"],
          sem="ld_lm", writes=[("lm",)], tag="t0llm")
    emit_wload(ctx, ctx.dram["L0_attn_bus_compress"], D, 128, "busc", "ld_bc", "t0lbc")
    P.dma("gp", ctx.sb["busp"][:, :], ctx.dram["L0_attn_bus_publish"],
          sem="ld_bp2", writes=[("busp",)], tag="t0lbp")


def alloc_l0_attn(ctx):
    ctx.sbuf("xbf", (128, NT * (S + 1)), BF16)
    ctx.sbuf("sqc", (128, NT * 256), BF16)
    ctx.sbuf("scr32", (128, 512), F32)
    ctx.sbuf("d1s", (128, 512), BF16)
    ctx.sbuf("mixed", (128, NT * S), BF16)
    pool = ctx.sbuf("pool0", (128, NT * S), BF16)
    pa = pool.ap()
    ctx.sb["q1b"] = pa[:, 0:2048]
    ctx.sb["q2b"] = pa[:, 2048:4096]
    ctx.sb["k1b"] = pa[:, 4096:6144]
    ctx.sb["k2b"] = pa[:, 6144:8192]
    ctx.sb["ctxb"] = pa[:, 8192:10240]
    ctx.sb["nsc"] = pa[:, 10240:11264]
    ctx.sb["ebuf"] = pa[:, 11264:12288]
    ctx.sb["vtok"] = pa[:, 12288:12800]
    ctx.sb["mixo"] = pa
    ctx.sbuf("rw1", (128, 2 * 2048), BF16)
    ctx.sb["mwb"] = ctx.sb["rw1"].ap()[:, 0:4096]
    ctx.sbuf("rw2", (128, 4 * 3), BF16)
    ctx.sbuf("b2row", (1, 4), F32)
    ctx.sbuf("wq1", (128, NT * 32), BF16)
    ctx.sbuf("wq2", (128, NT * 32), BF16)
    ctx.sbuf("wk1", (128, NT * 32), BF16)
    ctx.sbuf("wk2", (128, NT * 32), BF16)
    ctx.sbuf("wv", (128, NT * 32), BF16)
    ctx.sbuf("wo", (32, D), BF16)
    ctx.sbuf("lm", (64, D), BF16)
    ctx.sbuf("lmT", (128, NT * 64), BF16)
    ctx.sbuf("busc", (128, NT * 128), BF16)
    ctx.sbuf("busp", (128, 128), BF16)
    ctx.sb["buse"] = ctx.sb["mixed"].ap()[:, 0:1024]



def emit_phi(ctx, buf_name, key, ap, scr_name, skey, tagp):
    P = ctx.P
    scr = ctx.sb[scr_name]
    w = ap.shape[1]
    P.op("ve", lambda e: e.tensor_scalar(
        scr[:, 0:w], ap, 0.0, None, ALU.min),
        reads=[key], writes=[(scr_name, skey)], tag=tagp + "m")
    P.op("ve", lambda e: e.tensor_tensor(
        ap, ap, scr[:, 0:w], ALU.subtract),
        reads=[key, (scr_name, skey)], writes=[key], tag=tagp + "r")
    P.op("ac", lambda e: e.activation(
        scr[:, 0:w], scr[:, 0:w], AF.Exp),
        reads=[(scr_name, skey)], writes=[(scr_name, skey)], tag=tagp + "e")
    P.op("ve", lambda e: e.tensor_tensor(
        ap, ap, scr[:, 0:w], ALU.add),
        reads=[key, (scr_name, skey)], writes=[key], tag=tagp + "a")


def _aob(ctx):
    return ctx.sb["xcur"].ap().bitcast(BF16)[:, 0:NT * S]


def _pbb(ctx):
    return ctx.sb["xcur"].ap().bitcast(BF16)[:, NT * S:2 * NT * S]


def emit_l1_attn(ctx):
    P = ctx.P
    x = ctx.sb["xcur"]
    emit_cast_x_to_bf(ctx, "xbf", tagp="a1cx")
    src_tiles = [((lambda f0, f1, i=i: xv(ctx, "xbf", i, False, f0, f1)),
                  [("xbf", i)]) for i in range(NT)]
    emit_ln_stats(ctx, src_tiles, "sqc", 0, 1, 1, 2, LN_EPS, "a1st")
    emit_row_pair(ctx, lambda f0, f1: row32(ctx, 3, f0, f1), R32(3),
                  lambda f0, f1: row32(ctx, 0, f0, f1), [R32(0)],
                  lambda f0, f1: row32(ctx, 2, f0, f1), [R32(2)],
                  ALU.mult, "a1mr")
    emit_row_to_dram(ctx, 2, 0, "a1sd0")
    emit_row_to_dram(ctx, 3, 1, "a1sd1")
    for c in range(4):
        f0, f1 = c * 512, (c + 1) * 512
        bc_r, kr = emit_bcast_dram(ctx, 0, 0, f"a1br{c}", f0, f1)
        bc_m, km = emit_bcast_dram(ctx, 1, 1, f"a1bm{c}", f0, f1)
        for i in range(NT):
            sc = ctx.sb["scr32"]
            P.op("ve", lambda e, i=i, f0=f0, f1=f1, bc_r=bc_r: e.tensor_tensor(
                sc[:, 0:512], x[:, i * S + f0: i * S + f1], bc_r, ALU.mult),
                reads=[("xcur", i), kr], writes=[("scr32", 0)], tag=f"a1l1_{i}_{c}")
            P.op("ve", lambda e, i=i, f0=f0, f1=f1, bc_m=bc_m: e.tensor_tensor(
                xv(ctx, "xbf", i, False, f0, f1), sc[:, 0:512], bc_m, ALU.subtract),
                reads=[("scr32", 0), km], writes=[("xbf", i)], tag=f"a1l2_{i}_{c}")
    P.op("ve", lambda e: e.engine_nop(),
         reads=[("xcur", i) for i in range(NT)] + [("xcur",)],
         writes=[("xcur",), ("aobuf",), ("pbuf",)], tag="a1xfence")
    for g in range(2):
        for part, r0 in (("q", 4 * g), ("k", 8 + 4 * g), ("v", 16 + 4 * g)):
            P.dma("gp", ctx.sb["wgrp"][:, :].rearrange("p (r c) -> p r c", r=4),
                  ctx.dram["L1_attn_W_qkv"][r0 * 128:(r0 + 4) * 128, :]
                  .rearrange("(r p) c -> p r c", p=128),
                  sem="ld_wq", writes=[("wgrp",)], tag=f"a1lw{part}{g}")
            if part != "q":
                P.dma("gp", row16(ctx, 1, 512, 1024),
                      ctx.dram["L1_attn_b_qkv"][r0 * 128:(r0 + 4) * 128].unsqueeze(0),
                      sem="ld_br", writes=[R16(1)], tag=f"a1lbr{part}{g}")
            for r in range(4):
                for cc in range(8):
                    pap = psbf(ctx, "B", 3, 128, 0, 128)
                    P.op("te", lambda e, r=r, cc=cc, pap=pap: e.transpose(
                        pap, ctx.sb["wgrp"][:, r * D + cc * 128:r * D + (cc + 1) * 128],
                        ctx.sb["ident"][:, :]),
                        reads=[("wgrp",), ("ident",)], writes=[RB(3)],
                        tag=f"a1t{part}{g}_{r}_{cc}")
                    dap = ctx.sb["wgT"][:, cc * 512 + r * 128: cc * 512 + (r + 1) * 128]
                    P.op("ac", lambda e, dap=dap, pap=pap: e.activation(dap, pap, AF.Copy),
                         reads=[RB(3)], writes=[("wgT",)], tag=f"a1te{part}{g}_{r}_{cc}")
            if part == "q":
                for sub in range(4):
                    for c in range(4):
                        bk = (sub * 4 + c) % 2
                        ap = psA(ctx, bk, 128, 0, 512)
                        for i in range(NT):
                            P.op("te", lambda e, sub=sub, c=c, i=i, ap=ap: e.matmul(
                                ap, ctx.sb["wgT"][:, i * 512 + sub * 128:
                                                  i * 512 + (sub + 1) * 128],
                                xv(ctx, "xbf", i, False, c * 512, (c + 1) * 512),
                                start=(i == 0), stop=(i == NT - 1)),
                                reads=[("wgT",), ("xbf", i)], writes=[RA(bk)],
                                tag=f"a1qm{g}_{sub}_{c}_{i}")
                        for half in range(2):
                            hl = 2 * sub + half
                            P.op("ac", lambda e, c=c, hl=hl, half=half, bk=bk, sub=sub: e.activation(
                                ctx.sb["qbuf"][0:64, hl * S + c * 512: hl * S + (c + 1) * 512],
                                ctx.sb["psA"][half * 64:(half + 1) * 64,
                                              bk * 512:(bk + 1) * 512],
                                AF.Identity,
                                bias=ctx.sb["cols"][half * 64:(half + 1) * 64,
                                                    r0 + sub: r0 + sub + 1]),
                                reads=[RA(bk), ("cols", r0 + sub)],
                                writes=[("qbuf", hl, c)], tag=f"a1qe{g}_{hl}_{c}")
                for hl in range(8):
                    for c in range(4):
                        qap = ctx.sb["qbuf"][0:64, hl * S + c * 512: hl * S + (c + 1) * 512]
                        P.op("ve", lambda e, qap=qap: e.tensor_scalar(
                            ctx.sb["scr32"][0:64, 0:512], qap, 0.0, None, ALU.min),
                            reads=[("qbuf", hl, c)], writes=[("scr32", 0)],
                            tag=f"a1qp1{g}_{hl}_{c}")
                        P.op("ve", lambda e, qap=qap: e.tensor_tensor(
                            qap, qap, ctx.sb["scr32"][0:64, 0:512], ALU.subtract),
                            reads=[("qbuf", hl, c), ("scr32", 0)], writes=[("qbuf", hl, c)],
                            tag=f"a1qp2{g}_{hl}_{c}")
                        P.op("ac", lambda e: e.activation(
                            ctx.sb["scr32"][0:64, 0:512], ctx.sb["scr32"][0:64, 0:512], AF.Exp),
                            reads=[("scr32", 0)], writes=[("scr32", 0)], tag=f"a1qp3{g}_{hl}_{c}")
                        P.op("ve", lambda e, qap=qap: e.tensor_tensor(
                            qap, qap, ctx.sb["scr32"][0:64, 0:512], ALU.add),
                            reads=[("qbuf", hl, c), ("scr32", 0)], writes=[("qbuf", hl, c)],
                            tag=f"a1qp4{g}_{hl}_{c}")
            else:
                dname = "kbuf" if part == "k" else "vbuf"
                dst = ctx.sb[dname]
                for tt in range(16):
                    bk = tt % 2
                    ap = psA(ctx, bk, 128, 0, 512)
                    for i in range(NT):
                        P.op("te", lambda e, tt=tt, i=i, ap=ap: e.matmul(
                            ap, xv(ctx, "xbf", i, False, tt * 128, (tt + 1) * 128),
                            ctx.sb["wgT"][:, i * 512:(i + 1) * 512],
                            start=(i == 0), stop=False),
                            reads=[("wgT",), ("xbf", i)], writes=[RA(bk)],
                            tag=f"a1{part}m{g}_{tt}_{i}")
                    P.op("te", lambda e, tt=tt, ap=ap: e.matmul(
                        ap, ctx.sb["ones_row"][0:1, :],
                        row16(ctx, 1, 512, 1024),
                        start=False, stop=True),
                        reads=[("ones_row",), R16(1)], writes=[RA(bk)],
                        tag=f"a1{part}b{g}_{tt}")
                    P.op("ac", lambda e, tt=tt, ap=ap, dst=dst: e.activation(
                        dst[:, tt * 512:(tt + 1) * 512], ap, AF.Copy),
                        reads=[RA(bk)], writes=[(dname, tt)], tag=f"a1{part}e{g}_{tt}")
                    if part == "k":
                        for opn in range(1):
                            kap = dst[:, tt * 512:(tt + 1) * 512]
                            P.op("ve", lambda e, kap=kap: e.tensor_scalar(
                                ctx.sb["scr32"][:, 0:512], kap, 0.0, None, ALU.min),
                                reads=[(dname, tt)], writes=[("scr32", 0)],
                                tag=f"a1kp1{g}_{tt}")
                            P.op("ve", lambda e, kap=kap: e.tensor_tensor(
                                kap, kap, ctx.sb["scr32"][:, 0:512], ALU.subtract),
                                reads=[(dname, tt), ("scr32", 0)], writes=[(dname, tt)],
                                tag=f"a1kp2{g}_{tt}")
                            P.op("ac", lambda e: e.activation(
                                ctx.sb["scr32"][:, 0:512], ctx.sb["scr32"][:, 0:512], AF.Exp),
                                reads=[("scr32", 0)], writes=[("scr32", 0)], tag=f"a1kp3{g}_{tt}")
                            P.op("ve", lambda e, kap=kap: e.tensor_tensor(
                                kap, kap, ctx.sb["scr32"][:, 0:512], ALU.add),
                                reads=[(dname, tt), ("scr32", 0)], writes=[(dname, tt)],
                                tag=f"a1kp4{g}_{tt}")
        for tt in range(16):
            P.op("te", lambda e, tt=tt: e.matmul(
                psB(ctx, 2, 1, 0, 512), ctx.sb["ones_col"][:, :],
                ctx.sb["kbuf"][:, tt * 512:(tt + 1) * 512],
                start=(tt == 0), stop=(tt == 15)),
                reads=[("kbuf", tt), ("ones_col",)], writes=[RB(2)], tag=f"a1k1m{g}_{tt}")
        P.op("ac", lambda e: e.activation(
            row16(ctx, 1, 0, 512), psB(ctx, 2, 1, 0, 512), AF.Copy),
            reads=[RB(2)], writes=[R16(1)], tag=f"a1k1e{g}")
        for hl in range(8):
            h = 8 * g + hl
            for tt in range(16):
                P.op("te", lambda e, hl=hl, tt=tt: e.matmul(
                    psB(ctx, 0, 64, 0, 64),
                    ctx.sb["kbuf"][:, tt * 512 + hl * 64: tt * 512 + (hl + 1) * 64],
                    ctx.sb["vbuf"][:, tt * 512 + hl * 64: tt * 512 + (hl + 1) * 64],
                    start=(tt == 0), stop=(tt == 15)),
                    reads=[("kbuf", tt), ("vbuf", tt)], writes=[RB(0)],
                    tag=f"a1kv{g}_{hl}_{tt}")
            P.op("ac", lambda e, hl=hl: e.activation(
                ctx.sb["kvb"][0:64, hl * 64:(hl + 1) * 64],
                psB(ctx, 0, 64, 0, 64), AF.Copy),
                reads=[RB(0)], writes=[("kvb", hl)], tag=f"a1kve{g}_{hl}")
            P.op("te", lambda e, hl=hl: e.transpose(
                psbf(ctx, "B", 1, 64, 0, 1),
                row16(ctx, 1, hl * 64, (hl + 1) * 64), ctx.sb["ident"][0:1, 0:1]),
                reads=[R16(1), ("ident",)], writes=[RB(1)], tag=f"a1k1t{g}_{hl}")
            P.op("ac", lambda e, hl=hl: e.activation(
                ctx.sb["k1c"][0:64, hl: hl + 1],
                psbf(ctx, "B", 1, 64, 0, 1), AF.Copy),
                reads=[RB(1)], writes=[("k1c", hl)], tag=f"a1k1ce{g}_{hl}")
            for c in range(4):
                qap = ctx.sb["qbuf"][0:64, hl * S + c * 512: hl * S + (c + 1) * 512]
                P.op("te", lambda e, c=c, hl=hl, qap=qap: e.matmul(
                    psA(ctx, c % 2, 64, 0, 512),
                    ctx.sb["kvb"][0:64, hl * 64:(hl + 1) * 64],
                    qap, start=True, stop=True),
                    reads=[("kvb", hl), ("qbuf", hl, c)], writes=[RA(c % 2)],
                    tag=f"a1om{g}_{hl}_{c}")
                P.op("te", lambda e, c=c, hl=hl, qap=qap: e.matmul(
                    psB(ctx, 2, 1, 0, 512),
                    ctx.sb["k1c"][0:64, hl: hl + 1],
                    qap, start=True, stop=True),
                    reads=[("k1c", hl), ("qbuf", hl, c)], writes=[RB(2)],
                    tag=f"a1dm{g}_{hl}_{c}")
                P.op("ac", lambda e, c=c: e.activation(
                    ctx.sb["ohbuf"][0:64, c * 512:(c + 1) * 512],
                    psA(ctx, c % 2, 64, 0, 512), AF.Copy),
                    reads=[RA(c % 2)], writes=[("ohbuf", c)], tag=f"a1oe{g}_{hl}_{c}")
                P.op("ve", lambda e, c=c: e.tensor_scalar(
                    row32(ctx, 3, c * 512, (c + 1) * 512),
                    psB(ctx, 2, 1, 0, 512), 1e-6, None, ALU.max),
                    reads=[RB(2)], writes=[R32(3)], tag=f"a1de{g}_{hl}_{c}")
            P.op("ve", lambda e: e.reciprocal(row32(ctx, 3), row32(ctx, 3)),
                 reads=[R32(3)], writes=[R32(3)], tag=f"a1rc{g}_{hl}")
            P.op("ve", lambda e: e.tensor_scalar(
                row16(ctx, 0), row32(ctx, 3), 1.0, None, ALU.mult),
                reads=[R32(3)], writes=[R16(0)], tag=f"a1rcb{g}_{hl}")
            for c in range(4):
                P.op("te", lambda e, c=c: e.matmul(
                    psB(ctx, c % 2, 64, 0, 512), ctx.sb["ones_row"][0:1, 0:64],
                    row16(ctx, 0, c * 512, (c + 1) * 512), start=True, stop=True),
                    reads=[R16(0), ("ones_row",)], writes=[RB(c % 2)],
                    tag=f"a1rb{g}_{hl}_{c}")
                P.op("ve", lambda e, c=c: e.tensor_tensor(
                    ctx.sb["ohbuf"][0:64, c * 512:(c + 1) * 512],
                    ctx.sb["ohbuf"][0:64, c * 512:(c + 1) * 512],
                    psB(ctx, c % 2, 64, 0, 512), ALU.mult),
                    reads=[("ohbuf", c), RB(c % 2)], writes=[("ohbuf", c)],
                    tag=f"a1os{g}_{hl}_{c}")
            ohv = ctx.sb["ohbuf"].ap().rearrange("p (j m) -> p m j", m=16)
            for m in range(16):
                dt_ = _aob(ctx)
                P.op("ve", lambda e, m=m, h=h, ohv=ohv, dt_=dt_: e.tensor_copy(
                    dt_[(m % 2) * 64:(m % 2) * 64 + 64,
                        (m // 2) * S + h * 128:(m // 2) * S + (h + 1) * 128],
                    ohv[0:64, m, :]),
                    reads=[("ohbuf", c) for c in range(4)],
                    writes=[("aobuf", m // 2, h)], tag=f"a1sc{g}_{hl}_{m}")
    P.op("ve", lambda e: e.engine_nop(),
         reads=[("kbuf", tt) for tt in range(16)] + [("vbuf", tt) for tt in range(16)],
         writes=[("kbuf",), ("vbuf",)], tag="a1fence")
    P.dma("gp", ctx.sb["kbuf"][:, 0:NT * D].rearrange("p (r c) -> p r c", r=NT),
          ctx.dram["L1_attn_W_proj"].rearrange("(r p) c -> p r c", p=128),
          sem="ld_wp", writes=[("kbuf",)], tag="a1lwp")
    for r in range(NT):
        for cc in range(NT):
            pap = psbf(ctx, "B", 3, 128, 0, 128)
            P.op("te", lambda e, r=r, cc=cc, pap=pap: e.transpose(
                pap, ctx.sb["kbuf"][:, r * D + cc * 128:r * D + (cc + 1) * 128],
                ctx.sb["ident"][:, :]),
                reads=[("kbuf",), ("ident",)], writes=[RB(3)], tag=f"a1tp{r}_{cc}")
            dap = ctx.sb["vbuf"][:, cc * D + r * 128: cc * D + (r + 1) * 128]
            P.op("ac", lambda e, dap=dap, pap=pap: e.activation(dap, pap, AF.Copy),
                 reads=[RB(3)], writes=[("vbuf",)], tag=f"a1tpe{r}_{cc}")
    for m in range(NT):
        for c in range(4):
            bk = (m * 4 + c) % 2
            ap = psA(ctx, bk, 128, 0, 512)
            for i in range(NT):
                P.op("te", lambda e, m=m, c=c, i=i, ap=ap: e.matmul(
                    ap, ctx.sb["vbuf"][:, i * D + m * 128:i * D + (m + 1) * 128],
                    _aob(ctx)[:, i * S + c * 512:(i * S) + (c + 1) * 512],
                    start=(i == 0), stop=(i == NT - 1)),
                    reads=[("vbuf",)] + [("aobuf", i, h) for h in range(16)],
                    writes=[RA(bk)], tag=f"a1pm{m}_{c}_{i}")
            P.op("ac", lambda e, m=m, c=c, ap=ap: e.activation(
                _pbb(ctx)[:, m * S + c * 512: m * S + (c + 1) * 512], ap,
                AF.Identity, bias=ctx.sb["cols"][:, 24 + m: 25 + m]),
                reads=[RA(bk), ("cols", 24 + m)], writes=[("pbuf", m)],
                tag=f"a1pe{m}_{c}")
    src2 = [((lambda f0, f1, i=i: _pbb(ctx)[:, i * S + f0: i * S + f1]),
             [("pbuf", i)]) for i in range(NT)]
    emit_ln_stats(ctx, src2, "sqc", 0, 1, 1, 2, LN_EPS, "a2st")
    emit_row_pair(ctx, lambda f0, f1: row32(ctx, 3, f0, f1), R32(3),
                  lambda f0, f1: row32(ctx, 0, f0, f1), [R32(0)],
                  lambda f0, f1: row32(ctx, 2, f0, f1), [R32(2)],
                  ALU.mult, "a2mr")
    emit_row_to_dram(ctx, 2, 0, "a2sd0")
    emit_row_to_dram(ctx, 3, 1, "a2sd1")
    for i in range(NT):
        P.op("ve", lambda e, i=i: e.tensor_scalar(
            xv(ctx, "xbf", i), xv(ctx, "xbf", i),
            ctx.sb["cols"][:, 32 + i: 33 + i], None, ALU.add),
            reads=[("xbf", i), ("cols", 32 + i)], writes=[("xbf", i)], tag=f"a2xb{i}")
    xsrc = ctx.x_src_dram
    xdst = ctx.x_dst_dram
    P.op("ve", lambda e: e.engine_nop(),
         reads=[], writes=[("wgrp",), ("xck", 0), ("xos", 0)], tag="a2wgfence")
    for c in range(4):
        f0, f1 = c * 512, (c + 1) * 512
        bc_r, kr = emit_bcast_dram(ctx, 0, 0, f"a2br{c}", f0, f1)
        bc_m, km = emit_bcast_dram(ctx, 1, 1, f"a2bm{c}", f0, f1)
        for i in range(NT):
            sl = 0
            xck = ctx.sb["xck"][:, 0:512]
            xos = ctx.sb["xos"][:, 0:512]
            P.dma("sp", xck, xsrc[i * 128:(i + 1) * 128, f0:f1],
                  sem="ld_xc", reads=[("xpk",)], writes=[("xck", sl)],
                  tag=f"a2lx{i}_{c}")
            sc = ctx.sb["scr32"]
            P.op("ve", lambda e, i=i, f0=f0, bc_r=bc_r: e.tensor_tensor(
                sc[:, 0:512], _pbb(ctx)[:, i * S + f0: i * S + f0 + 512],
                bc_r, ALU.mult),
                reads=[("pbuf", i), kr], writes=[("scr32", 0)], tag=f"a2c1_{i}_{c}")
            P.op("ve", lambda e, i=i, bc_m=bc_m: e.tensor_tensor(
                sc[:, 0:512], sc[:, 0:512], bc_m, ALU.subtract),
                reads=[("scr32", 0), km], writes=[("scr32", 0)], tag=f"a2c2_{i}_{c}")
            P.op("ve", lambda e, i=i, f0=f0: e.scalar_tensor_tensor(
                sc[:, 0:512], sc[:, 0:512], ctx.sb["cols"][:, 40 + i: 41 + i],
                xv(ctx, "xbf", i, False, f0, f0 + 512), ALU.mult, ALU.add),
                reads=[("scr32", 0), ("cols", 40 + i), ("xbf", i)],
                writes=[("scr32", 0)], tag=f"a2c3_{i}_{c}")
            P.op("ve", lambda e, xos=xos, xck=xck: e.tensor_tensor(
                xos, xck, sc[:, 0:512], ALU.add),
                reads=[("xck", sl), ("scr32", 0)], writes=[("xos", sl)],
                tag=f"a2c4_{i}_{c}")
            P.dma("sp", xdst[i * 128:(i + 1) * 128, f0:f1], xos,
                  sem="st_xc", reads=[("xos", sl)], writes=[("xdst", i, c)],
                  tag=f"a2sx{i}_{c}", store=True)
    ctx.handled_store = True


def emit_l1_attn_loads(ctx):
    P = ctx.P
    ctx.din("L1_attn_W_qkv", (3 * D, D))
    ctx.din("L1_attn_W_proj", (D, D))
    bq = ctx.din("L1_attn_b_qkv", (3 * D,))
    bp = ctx.din("L1_attn_b_proj", (D,))
    ga = ctx.din("L1_attn_gamma", (D,))
    be = ctx.din("L1_attn_beta", (D,))
    for r in range(24):
        P.dma("gp", ctx.sb["cols"][:, r: r + 1],
              bq[r * 128:(r + 1) * 128].unsqueeze(1),
              sem="ld_bq", writes=[("cols", r)], tag=f"a1lbq{r}")
    for r in range(NT):
        P.dma("gp", ctx.sb["cols"][:, 24 + r: 25 + r],
              bp[r * 128:(r + 1) * 128].unsqueeze(1),
              sem="ld_bp", writes=[("cols", 24 + r)], tag=f"a1lbp{r}")
        P.dma("gp", ctx.sb["cols"][:, 32 + r: 33 + r],
              be[r * 128:(r + 1) * 128].unsqueeze(1),
              sem="ld_bt", writes=[("cols", 32 + r)], tag=f"a1lbe{r}")
        P.dma("gp", ctx.sb["cols"][:, 40 + r: 41 + r],
              ga[r * 128:(r + 1) * 128].unsqueeze(1),
              sem="ld_ga", writes=[("cols", 40 + r)], tag=f"a1lga{r}")


def alloc_l1_attn(ctx):
    ctx.sbuf("xbf", (128, NT * (S + 1)), BF16)
    ctx.sbuf("sqc", (128, NT * 128), BF16)
    ctx.sbuf("scr32", (128, 512), F32)
    ctx.sbuf("qbuf", (64, 8 * S), BF16)
    ctx.sbuf("kbuf", (128, 16 * 512), BF16)
    ctx.sbuf("vbuf", (128, 16 * 512), BF16)
    ctx.sbuf("wgrp", (128, 4 * D), BF16)
    ctx.sb["xck"] = ctx.sb["wgrp"].ap().bitcast(F32)[:, 0:512]
    ctx.sb["xos"] = ctx.sb["wgrp"].ap().bitcast(F32)[:, 512:1024]
    ctx.sbuf("wgT", (128, NT * 512), BF16)
    ctx.sbuf("ohbuf", (64, S), BF16)
    ctx.sbuf("kvb", (64, 8 * 64), BF16)
    ctx.sb["k1c"] = ctx.sb["colsb"].ap()[0:64, 0:8]



A_CHUNKS = [(at * 128, min((at + 1) * 128, 2730)) for at in range(22)]


def _l1f_stats(ctx, tagp, shifted_mask, murstd=False):
    P = ctx.P
    src_tiles = [((lambda f0, f1, i=i: xv(ctx, "xbf", i, shifted_mask(i), f0, f1)),
                  [("xbf", i)]) for i in range(NT)]
    emit_ln_stats(ctx, src_tiles, "sqc", 0, 1, 1, 2, LN_EPS, tagp)
    emit_row_to_dram(ctx, 2, 0, tagp + "sd0")
    if murstd:
        emit_row_pair(ctx, lambda f0, f1: row32(ctx, 3, f0, f1), R32(3),
                      lambda f0, f1: row32(ctx, 0, f0, f1), [R32(0)],
                      lambda f0, f1: row32(ctx, 2, f0, f1), [R32(2)],
                      ALU.mult, tagp + "mr")
        emit_row_to_dram(ctx, 3, 1, tagp + "sd1")
    P.op("ve", lambda e: e.tensor_scalar(
        row16(ctx, 1), row32(ctx, 0), -1.0, None, ALU.mult),
        reads=[R32(0)], writes=[R16(1)], tag=tagp + "nm")


def emit_l1_ffn_layer(ctx, li, ffn_too=True):
    P = ctx.P
    x = ctx.sb["xcur"]
    L = f"L{li}"
    dq = ctx.dram["L1_ffn_Wqkv"]
    dwo = ctx.dram["L1_ffn_Wo"]
    dw1 = ctx.dram["L1_ffn_W1"]
    dw2 = ctx.dram["L1_ffn_W2"]

    P.op("ve", lambda e: e.engine_nop(),
         reads=[("w1a",), ("w1g",), ("w2c",), ("actb",), ("gel",)] +
               [("w1a", sl) for sl in range(2)] + [("w1g", sl) for sl in range(2)] +
               [("w2c", sl) for sl in range(2)] +
               [("actb", at) for at in range(22)],
         writes=[("qg",), ("kg",), ("vg",), ("aobuf",), ("wg",), ("wog",),
                 ("ebuf",), ("pp",)],
         tag=L + "fence0")
    emit_cast_x_to_bf(ctx, "xbf", tagp=L + "cx")
    for nm, cbase, roff in (("q_scale", 100, 0), ("k_scale", 101, 64)):
        dd = ctx.dram["L1_ffn_" + nm]
        for hb in range(2):
            P.dma("gp", ctx.sb["cols"][hb * 64:(hb + 1) * 64, cbase: cbase + 1],
                  dd[li].unsqueeze(1), sem="ld_qs",
                  writes=[("cols", cbase)], tag=f"{L}l{nm}{hb}")
        P.dma("gp", row32(ctx, 3, 256 + roff, 256 + roff + 64),
              dd[li].unsqueeze(0), sem="ld_qs2", writes=[("rowp32", 3)],
              tag=f"{L}l{nm}r")
    P.op("ve", lambda e: e.reduce_max(
        row32(ctx, 3, 384, 385), row32(ctx, 3, 256, 320), axis=AX.X,
        apply_absolute_value=True),
        reads=[("rowp32", 3)], writes=[("rowp32", 3)], tag=L + "cb1")
    P.op("ve", lambda e: e.reduce_max(
        row32(ctx, 3, 385, 386), row32(ctx, 3, 320, 384), axis=AX.X,
        apply_absolute_value=True),
        reads=[("rowp32", 3)], writes=[("rowp32", 3)], tag=L + "cb2")
    P.op("ve", lambda e: e.tensor_tensor(
        row32(ctx, 3, 384, 385), row32(ctx, 3, 384, 385),
        row32(ctx, 3, 385, 386), ALU.mult),
        reads=[("rowp32", 3)], writes=[("rowp32", 3)], tag=L + "cb3")
    P.op("ve", lambda e: e.tensor_scalar(
        row32(ctx, 3, 384, 385), row32(ctx, 3, 384, 385), -8.0, 70.0,
        ALU.mult, ALU.add),
        reads=[("rowp32", 3)], writes=[("rowp32", 3)], tag=L + "cb4")
    P.op("ve", lambda e: e.tensor_scalar(
        row32(ctx, 3, 384, 385), row32(ctx, 3, 384, 385), 40.0, None,
        ALU.min),
        reads=[("rowp32", 3)], writes=[("rowp32", 3)], tag=L + "cb5")
    P.op("ve", lambda e: e.tensor_scalar(
        row16(ctx, 0, 768, 769), row32(ctx, 3, 384, 385), 1.0, None,
        ALU.mult),
        reads=[("rowp32", 3)], writes=[("rowp16", 0)], tag=L + "cb6")
    P.op("te", lambda e: e.matmul(
        psA(ctx, 2, 128, 0, 1), ctx.sb["ones_row"][0:1, :],
        row16(ctx, 0, 768, 769), start=True, stop=True),
        reads=[("rowp16", 0), ("ones_row",)], writes=[RA(2)], tag=L + "cb7")
    P.op("ve", lambda e: e.tensor_scalar(
        ctx.sb["cols"][:, 102:103], psA(ctx, 2, 128, 0, 1), 1.0, None, ALU.mult),
        reads=[RA(2)], writes=[("cols", 102)], tag=L + "cb8")
    for g in range(2):
        P.op("ve", lambda e: e.engine_nop(),
             reads=[], writes=[("pp",), ("aobuf",), ("wg",), ("ebuf",), ("wog",)],
             tag=f"{L}pf{g}")
        for part, coff in (("q", 512 * g), ("k", 1024 + 512 * g)):
            dst = ctx.sb["qg" if part == "q" else "kg"]
            dname = "qg" if part == "q" else "kg"
            for sub in range(4):
                for h in range(2):
                    P.dma("gp", ctx.sb["wg"][:, h * 512:(h + 1) * 512]
                          .rearrange("p (r c) -> p r c", r=4),
                          dq[li][h * 512:(h + 1) * 512,
                                 coff + sub * 128: coff + (sub + 1) * 128]
                          .rearrange("(r p) c -> p r c", p=128),
                          sem=f"ld_wg{h}", writes=[("wg", h)],
                          tag=f"{L}lw{part}{g}_{sub}_{h}")
                for c in range(4):
                    bk = (sub * 4 + c) % 2
                    ap = psA(ctx, bk, 128, 0, 512)
                    for i in range(NT):
                        P.op("te", lambda e, c=c, i=i, ap=ap: e.matmul(
                            ap, ctx.sb["wg"][:, i * 128:(i + 1) * 128],
                            xv(ctx, "xbf", i, i >= 4, c * 512, (c + 1) * 512),
                            start=(i == 0), stop=(i == NT - 1)),
                            reads=[("wg", i // 4), ("xbf", i)], writes=[RA(bk)],
                            tag=f"{L}{part}m{g}_{sub}_{c}_{i}")
                    P.op("ac", lambda e, sub=sub, c=c, ap=ap, dst=dst: e.activation(
                        dst[:, sub * S + c * 512: sub * S + (c + 1) * 512], ap, AF.Copy),
                        reads=[RA(bk)], writes=[(dname, sub, c)],
                        tag=f"{L}{part}e{g}_{sub}_{c}")
            an, bn = ("rot_aq", "rot_bq") if part == "q" else ("rot_ak", "rot_bk")
            scol = 100 if part == "q" else 101

            def _penv(sub, c, dst=dst, dname=dname):
                pb = (sub * 4 + c) % 2
                ib2, sb2 = (2, 3) if pb == 0 else (0, 1)
                pp = ctx.sb["prep2"]
                return dict(
                    sub=sub, c=c, pb=pb, ib2=ib2, sb2=sb2,
                    sq=pp[:, pb * 1536: pb * 1536 + 512],
                    rs=pp[:, pb * 1536 + 512: pb * 1536 + 1024],
                    sc2=pp[:, pb * 1536 + 1024: pb * 1536 + 1536],
                    iv=pp[0:2, pb * 1536: pb * 1536 + 512],
                    Ksq=("pp", pb, 0), Krs=("pp", pb, 1), Ksc=("pp", pb, 2),
                    ch=dst[:, sub * S + c * 512: sub * S + (c + 1) * 512])

            def _prepA(E, part=part, g=g, dname=dname):
                sub = E["sub"]; c = E["c"]; ch = E["ch"]; sq = E["sq"]
                ib2 = E["ib2"]; Ksq = E["Ksq"]
                P.op("ac", lambda e, ch=ch, sq=sq: e.activation(
                    sq, ch, AF.Square),
                    reads=[(dname, sub, c)], writes=[Ksq],
                    tag=f"{L}{part}s{g}_{sub}_{c}")
                P.op("te", lambda e, sq=sq, ib2=ib2: e.matmul(
                    psB(ctx, ib2, 2, 0, 512), ctx.sb["blk2"][:, 0:2],
                    sq, start=True, stop=True),
                    reads=[Ksq, ("blk2",)], writes=[RB(ib2)],
                    tag=f"{L}{part}n{g}_{sub}_{c}")
                P.op("ac", lambda e, ib2=ib2: e.activation(
                    psB(ctx, ib2, 2, 0, 512), psB(ctx, ib2, 2, 0, 512), AF.Ln),
                    reads=[RB(ib2)], writes=[RB(ib2)], tag=f"{L}{part}i1{g}_{sub}_{c}")
                P.op("ac", lambda e, ib2=ib2: e.activation(
                    psB(ctx, ib2, 2, 0, 512), psB(ctx, ib2, 2, 0, 512),
                    AF.Exp, scale=-0.5),
                    reads=[RB(ib2)], writes=[RB(ib2)], tag=f"{L}{part}i2{g}_{sub}_{c}")

            def _prepB(E, part=part, g=g, dname=dname, an=an, bn=bn, scol=scol):
                sub = E["sub"]; c = E["c"]; ch = E["ch"]; iv = E["iv"]
                rs = E["rs"]; sc2 = E["sc2"]; ib2 = E["ib2"]; sb2 = E["sb2"]
                Ksq = E["Ksq"]; Krs = E["Krs"]; Ksc = E["Ksc"]
                P.op("ve", lambda e, iv=iv, ib2=ib2: e.tensor_scalar(
                    iv, psB(ctx, ib2, 2, 0, 512),
                    1e12, None, ALU.min),
                    reads=[RB(ib2)], writes=[Ksq], tag=f"{L}{part}i5{g}_{sub}_{c}")
                P.op("te", lambda e, iv=iv, ib2=ib2: e.matmul(
                    psB(ctx, ib2, 128, 0, 512), ctx.sb["blk2t"][0:2, :],
                    iv, start=True, stop=True),
                    reads=[Ksq, ("blk2t",)], writes=[RB(ib2)],
                    tag=f"{L}{part}i6{g}_{sub}_{c}")
                P.op("ve", lambda e, ch=ch, scol=scol: e.tensor_scalar(
                    ch, ch, ctx.sb["cols"][:, scol: scol + 1], None, ALU.mult),
                    reads=[(dname, sub, c), ("cols", scol)],
                    writes=[(dname, sub, c)], tag=f"{L}{part}qs{g}_{sub}_{c}")
                P.op("te", lambda e, ch=ch, sb2=sb2: e.matmul(
                    psB(ctx, sb2, 128, 0, 512), ctx.sb["p_sigma"][:, :], ch,
                    start=True, stop=True),
                    reads=[(dname, sub, c), ("p_sigma",)], writes=[RB(sb2)],
                    tag=f"{L}{part}g{g}_{sub}_{c}")
                P.op("ve", lambda e, ch=ch, c=c, an=an, rs=rs: e.tensor_tensor(
                    rs, ch,
                    ctx.sb[an][:, c * 512:(c + 1) * 512], ALU.mult),
                    reads=[(dname, sub, c), (an,)], writes=[Krs],
                    tag=f"{L}{part}r1{g}_{sub}_{c}")
                P.op("ve", lambda e, c=c, bn=bn, sc2=sc2, sb2=sb2: e.tensor_tensor(
                    sc2, psB(ctx, sb2, 128, 0, 512),
                    ctx.sb[bn][:, c * 512:(c + 1) * 512], ALU.mult),
                    reads=[RB(sb2), (bn,)], writes=[Ksc],
                    tag=f"{L}{part}r2{g}_{sub}_{c}")
                P.op("ve", lambda e, rs=rs, sc2=sc2: e.tensor_tensor(
                    rs, rs, sc2, ALU.add),
                    reads=[Krs, Ksc], writes=[Krs],
                    tag=f"{L}{part}r3{g}_{sub}_{c}")
                P.op("ve", lambda e, ch=ch, rs=rs, ib2=ib2: e.scalar_tensor_tensor(
                    ch, rs, 1.0,
                    psB(ctx, ib2, 128, 0, 512), ALU.mult, ALU.mult),
                    reads=[Krs, RB(ib2)],
                    writes=[(dname, sub, c)], tag=f"{L}{part}r4{g}_{sub}_{c}")

            penvs = [_penv(sub, c) for sub in range(4) for c in range(4)]
            for i in range(len(penvs) + 1):
                if i < len(penvs):
                    _prepA(penvs[i])
                if i > 0:
                    _prepB(penvs[i - 1])
        if getattr(ctx, "l1f_dumpqk", False) and g == 0:
            for nm2 in ("qg", "kg"):
                P.dma("gp", ctx.dram["dump_" + nm2],
                      ctx.sb[nm2][:, :].rearrange("p (r c) -> p r c", r=4),
                      sem="st_d" + nm2,
                      reads=[(nm2, sb2, c2) for sb2 in range(4) for c2 in range(4)],
                      tag="dmp" + nm2, store=True)
        for ws in range(4):
            for h in range(2):
                P.dma("gp", ctx.sb["wg"][:, h * 512:(h + 1) * 512]
                      .rearrange("p (r c) -> p r c", r=4),
                      dq[li][h * 512:(h + 1) * 512,
                             2048 + 512 * g + ws * 128: 2048 + 512 * g + (ws + 1) * 128]
                      .rearrange("(r p) c -> p r c", p=128),
                      sem=f"ld_wg{h}", writes=[("wg", h)],
                      tag=f"{L}lwv{g}_{ws}_{h}")
            for tt in range(16):
                bk = tt % 4
                ap = ctx.sb["psA"][0:128, bk * 512: bk * 512 + 128]
                for i in range(NT):
                    P.op("te", lambda e, tt=tt, i=i, ap=ap: e.matmul(
                        ap, xv(ctx, "xbf", i, i >= 4, tt * 128, (tt + 1) * 128),
                        ctx.sb["wg"][:, i * 128:(i + 1) * 128],
                        start=(i == 0), stop=(i == NT - 1)),
                        reads=[("wg", i // 4), ("xbf", i)], writes=[RA(bk)],
                        tag=f"{L}vm{g}_{ws}_{tt}_{i}")
                P.op("ac", lambda e, ws=ws, tt=tt, ap=ap: e.activation(
                    ctx.sb["vg"][:, tt * 512 + ws * 128: tt * 512 + (ws + 1) * 128],
                    ap, AF.Copy),
                    reads=[RA(bk)], writes=[("vg", tt)],
                    tag=f"{L}ve{g}_{ws}_{tt}")
        if getattr(ctx, "l1f_dumpqk", False) and g == 0:
            P.dma("gp", ctx.dram["dump_vg"],
                  ctx.sb["vg"][:, :].rearrange("p (r c) -> p r c", r=16),
                  sem="st_dvg", reads=[("vg", t2) for t2 in range(16)],
                  tag="dmpvg", store=True)
        P.op("ve", lambda e: e.engine_nop(),
             reads=[], writes=[("pp",), ("aobuf",), ("wg",), ("ebuf",), ("wog",)],
             tag=f"{L}af{g}")
        def _env(hl, ib):
            bb = (hl % 2) * 64
            tl = hl // 2
            i0 = ib * 256
            p2 = (hl * 8 + ib) % 2
            bA = 0 if p2 == 0 else 2
            bPV = 0 if p2 == 0 else 2
            bDN = 1 if p2 == 0 else 3
            ebsrc = ctx.sb["ebuf"] if p2 == 0 else ctx.sb["wg"]
            ebk0 = 0 if p2 == 0 else 2
            rrow = 3 if p2 == 0 else 2
            r16b = 0 if p2 == 0 else 1
            nscap = (ctx.sb["nsc"][0:64, 0:256] if p2 == 0
                     else ctx.sb["rsc"].ap()[0:64, 256:512])
            nk = ("rsc", p2)
            segs = []
            for ks in range(4):
                jt = 2 * ib - 2 + ks
                if jt < 0:
                    continue
                if ks == 0:
                    segs.append((ks, jt, (0, 128), 0, 0, "m_up", 0))
                elif ks == 1:
                    segs.append((ks, jt, (0, 256), 0, 128, "m_up", 128))
                elif ks == 2:
                    segs.append((ks, jt, (0, 256), 1, 0, "m_low", 0))
                else:
                    segs.append((ks, jt, (128, 256), 1, 256, "m_low", 0))
            used_banks = sorted(set(sg[3] for sg in segs))
            first_in_bank = {}
            last_in_bank = {}
            for sg in segs:
                first_in_bank.setdefault(sg[3], sg[0])
                last_in_bank[sg[3]] = sg[0]
            return dict(hl=hl, ib=ib, bb=bb, tl=tl, i0=i0, bA=bA, bPV=bPV,
                        bDN=bDN, ebsrc=ebsrc, ebk0=ebk0, rrow=rrow, r16b=r16b,
                        nscap=nscap, nk=nk, segs=segs, used_banks=used_banks,
                        first_in_bank=first_in_bank, last_in_bank=last_in_bank)

        def _stageA(E):
            hl = E["hl"]; ib = E["ib"]; bb = E["bb"]; tl = E["tl"]; i0 = E["i0"]
            bA = E["bA"]; ebsrc = E["ebsrc"]; ebk0 = E["ebk0"]
            segs = E["segs"]; used_banks = E["used_banks"]
            first_in_bank = E["first_in_bank"]; last_in_bank = E["last_in_bank"]
            for (ks, jt, (ir0, ir1), bank, boff, mk, moff) in segs:
                w = ir1 - ir0
                P.op("te", lambda e, ks=ks, jt=jt, ir0=ir0, w=w,
                     bank=bank + bA,
                     boff=boff, hl=hl, tl=tl, bb=bb, i0=i0,
                     st=(first_in_bank[bank] == ks): e.matmul(
                    ctx.sb["psA"][0:128, bank * 512 + boff: bank * 512 + boff + w],
                    ctx.sb["kg"][bb:bb + 64, tl * S + jt * 128: tl * S + (jt + 1) * 128],
                    ctx.sb["qg"][bb:bb + 64, tl * S + i0 + ir0: tl * S + i0 + ir0 + w],
                    start=st, stop=False),
                    reads=[("kg", tl, jt // 4), ("qg", tl, (i0 + ir0) // 512),
                           ("qg", tl, (i0 + ir1 - 1) // 512)],
                    writes=[RA(bank + bA)], tag=f"{L}sm{g}_{hl}_{ib}_{ks}")
            for (ks, jt, (ir0, ir1), bank, boff, mk, moff) in segs:
                P.op("te", lambda e, bank=bank + bA, boff=boff, moff=moff,
                     mk=mk, sp=(last_in_bank[bank] == ks): e.matmul(
                    ctx.sb["psA"][0:128, bank * 512 + boff + moff:
                                  bank * 512 + boff + moff + 128],
                    ctx.sb[mk][:, :], ctx.sb["ident"][:, :],
                    start=False, stop=sp),
                    reads=[(mk,), ("ident",)], writes=[RA(bank + bA)],
                    tag=f"{L}mk{g}_{hl}_{ib}_{ks}")
            for bank in used_banks:
                P.op("ac", lambda e, bank=bank, ebsrc=ebsrc, bA=bA: e.activation(
                    ebsrc[:, bank * 384:(bank + 1) * 384],
                    ctx.sb["psA"][0:128, (bank + bA) * 512: (bank + bA) * 512 + 384],
                    AF.Exp, scale=8.0, bias=ctx.sb["cols"][:, 102:103]),
                    reads=[RA(bank + bA), ("cols", 102)],
                    writes=[("ebuf", ebk0 + bank)],
                    tag=f"{L}ex{g}_{hl}_{ib}_{bank}")
        def _stageB(E):
            hl = E["hl"]; ib = E["ib"]; bb = E["bb"]; tl = E["tl"]; i0 = E["i0"]
            bPV = E["bPV"]; bDN = E["bDN"]; ebsrc = E["ebsrc"]; ebk0 = E["ebk0"]
            rrow = E["rrow"]; r16b = E["r16b"]; nscap = E["nscap"]; nk = E["nk"]
            segs = E["segs"]
            psegs = sorted(segs, key=lambda sg: -(sg[2][1] - sg[2][0]))
            for si, (ks, jt, (ir0, ir1), bank, boff, mk, moff) in enumerate(psegs):
                esl = ebsrc[:, bank * 384 + boff: bank * 384 + boff + (ir1 - ir0)]
                P.op("te", lambda e, jt=jt, ir0=ir0, ir1=ir1, hl=hl, esl=esl,
                     bPV=bPV,
                     st=(si == 0), sp=(si == len(psegs) - 1): e.matmul(
                    ctx.sb["psB"][0:64, bPV * 512 + ir0: bPV * 512 + ir1],
                    ctx.sb["vg"][:, jt * 512 + hl * 64: jt * 512 + (hl + 1) * 64],
                    esl, start=st, stop=sp),
                    reads=[("vg", jt), ("ebuf", ebk0), ("ebuf", ebk0 + 1)],
                    writes=[RB(bPV)], tag=f"{L}pv{g}_{hl}_{ib}_{ks}")
                P.op("te", lambda e, ir0=ir0, ir1=ir1, esl=esl, bDN=bDN,
                     st=(si == 0), sp=(si == len(psegs) - 1): e.matmul(
                    ctx.sb["psB"][0:1, bDN * 512 + ir0: bDN * 512 + ir1],
                    ctx.sb["ones_col"][:, :], esl, start=st, stop=sp),
                    reads=[("ebuf", ebk0), ("ebuf", ebk0 + 1), ("ones_col",)],
                    writes=[RB(bDN)], tag=f"{L}dn{g}_{hl}_{ib}_{ks}")
            P.op("ve", lambda e, rrow=rrow, bDN=bDN: e.tensor_scalar(
                row32(ctx, rrow, 0, 256), psB(ctx, bDN, 1, 0, 256), 1e-30,
                None, ALU.add),
                reads=[RB(bDN)], writes=[("rowp32", rrow)], tag=f"{L}d1{g}_{hl}_{ib}")
            P.op("ve", lambda e, rrow=rrow: e.reciprocal(
                row32(ctx, rrow, 0, 256), row32(ctx, rrow, 0, 256)),
                reads=[("rowp32", rrow)], writes=[("rowp32", rrow)], tag=f"{L}d2{g}_{hl}_{ib}")
            P.op("ac", lambda e, r16b=r16b, rrow=rrow: e.activation(
                row16(ctx, r16b, 512, 768), row32(ctx, rrow, 0, 256),
                AF.Copy),
                reads=[("rowp32", rrow)], writes=[("rowp16", r16b)], tag=f"{L}d3{g}_{hl}_{ib}")
            P.op("te", lambda e, bDN=bDN, r16b=r16b: e.matmul(
                psB(ctx, bDN, 64, 256, 512), ctx.sb["ones_row"][0:1, 0:64],
                row16(ctx, r16b, 512, 768), start=True, stop=True),
                reads=[("rowp16", r16b), ("ones_row",)], writes=[RB(bDN)],
                tag=f"{L}d4{g}_{hl}_{ib}")
            P.op("ac", lambda e, nscap=nscap, bPV=bPV: e.activation(
                nscap, psB(ctx, bPV, 64, 0, 256), AF.Copy),
                reads=[RB(bPV)], writes=[nk], tag=f"{L}d5{g}_{hl}_{ib}")
            P.op("ve", lambda e, tl=tl, bb=bb, i0=i0, nscap=nscap,
                 bDN=bDN: e.tensor_tensor(
                ctx.sb["aobuf"][bb:bb + 64, tl * S + i0: tl * S + i0 + 256],
                nscap, psB(ctx, bDN, 64, 256, 512), ALU.mult),
                reads=[nk, RB(bDN)], writes=[("aobuf", tl, hl % 2, ib)],
                tag=f"{L}d6{g}_{hl}_{ib}")

        iters = [_env(hl, ib) for hl in range(8) for ib in range(8)]
        for i in range(len(iters) + 1):
            if i < len(iters):
                _stageA(iters[i])
            if i > 0:
                _stageB(iters[i - 1])
        if getattr(ctx, "l1f_dumpqk", False):
            P.dma("gp", ctx.dram["dump_ao" if g == 0 else "dump_ao1"],
                  ctx.sb["aobuf"][:, :].rearrange("p (r c) -> p r c", r=4),
                  sem="st_dao",
                  reads=[("aobuf", j2, h2, d2) for j2 in range(4) for h2 in range(2)
                         for d2 in range(8)],
                  tag="dmpao", store=True)
        for m in range(NT):
            wsl = m % 2
            slot = ctx.sb["wog"] if wsl == 0 else ctx.sb["ebuf"]
            P.dma("gp", slot[:, 0:512]
                  .rearrange("p (r c) -> p r c", r=4),
                  dwo[li][512 * g: 512 * g + 512, m * 128:(m + 1) * 128]
                  .rearrange("(r p) c -> p r c", p=128),
                  sem=f"ld_wo{wsl}",
                  writes=[("wog", wsl)] + ([("ebuf", 0), ("ebuf", 1)] if wsl else []),
                  tag=f"{L}lwo{g}_{m}")
            for c in range(4):
                bk = (m * 4 + c) % 2
                ap = psA(ctx, bk, 128, 0, 512)
                for j in range(4):
                    P.op("te", lambda e, m=m, c=c, j=j, ap=ap, slot=slot: e.matmul(
                        ap, slot[:, j * 128: (j + 1) * 128],
                        ctx.sb["aobuf"][:, j * S + c * 512: j * S + (c + 1) * 512],
                        start=(j == 0), stop=(j == 3)),
                        reads=[("wog", wsl)] + [("aobuf", j, hh, (c * 512) // 256 + dd)
                                                for hh in range(2) for dd in range(2)],
                        writes=[RA(bk)], tag=f"{L}wo{g}_{m}_{c}_{j}")
                P.op("ve", lambda e, m=m, c=c, ap=ap: e.tensor_tensor(
                    x[:, m * S + c * 512: m * S + (c + 1) * 512],
                    x[:, m * S + c * 512: m * S + (c + 1) * 512], ap, ALU.add),
                    reads=[RA(bk), ("xcur", m)], writes=[("xcur", m)],
                    tag=f"{L}woe{g}_{m}_{c}")
    if not ffn_too:
        return
    P.op("ve", lambda e: e.engine_nop(),
         reads=[("qg",), ("kg",), ("vg",), ("wg",), ("ebuf",)] +
               [("aobuf", j, hh, d4) for j in range(4) for hh in range(2)
                for d4 in range(8)] +
               [("qg", sb, c) for sb in range(4) for c in range(4)] +
               [("kg", sb, c) for sb in range(4) for c in range(4)] +
               [("vg", t) for t in range(16)] +
               [("ebuf", b) for b in range(2)] +
               [("wog", sl) for sl in range(2)] + [("pp",)],
         writes=[("w1a",), ("w1g",), ("w2c",), ("actb",), ("gel",)] +
               [("w1a", sl) for sl in range(2)] + [("w1g", sl) for sl in range(2)] +
               [("w2c", sl) for sl in range(2)] +
               [("actb", at) for at in range(22)],
         tag=L + "fence1")
    emit_cast_x_to_bf(ctx, "xbf", tagp=L + "c2")
    _l1f_stats(ctx, L + "fs", lambda i: i >= 4, murstd=True)
    for i in range(NT):
        P.dma("gp", ctx.sb["cols"][:, 104 + i: 105 + i],
              ctx.dram["L1_ffn_ln_g"][li][i * 128:(i + 1) * 128].unsqueeze(1),
              sem="ld_lng", writes=[("cols", 104 + i)], tag=f"{L}lg{i}")
        P.dma("gp", ctx.sb["cols"][:, 112 + i: 113 + i],
              ctx.dram["L1_ffn_ln_b"][li][i * 128:(i + 1) * 128].unsqueeze(1),
              sem="ld_lnb", writes=[("cols", 112 + i)], tag=f"{L}lb{i}")
    for c in range(3, -1, -1):
        f0, f1 = c * 512, (c + 1) * 512
        bc_r, kr = emit_bcast_dram(ctx, 0, 0, f"{L}fb{c}", f0, f1)
        bc_m, km = emit_bcast_dram(ctx, 1, 1, f"{L}fm{c}", f0, f1)
        for i in range(NT):
            sh = i >= 4
            P.op("ve", lambda e, i=i, sh=sh, f0=f0, f1=f1, bc_r=bc_r: e.tensor_tensor(
                ctx.sb["scr32"][:, 0:512], xv(ctx, "xbf", i, sh, f0, f1), bc_r,
                ALU.mult),
                reads=[("xbf", i), kr], writes=[("scr32", 0)], tag=f"{L}fx1_{i}_{c}")
            P.op("ve", lambda e, i=i, bc_m=bc_m: e.tensor_tensor(
                ctx.sb["scr32"][:, 0:512], ctx.sb["scr32"][:, 0:512],
                bc_m, ALU.subtract),
                reads=[("scr32", 0), km], writes=[("scr32", 0)], tag=f"{L}fx2_{i}_{c}")
            P.op("ve", lambda e, i=i, f0=f0, f1=f1: e.tensor_scalar(
                xv(ctx, "xbf", i, False, f0, f1), ctx.sb["scr32"][:, 0:512],
                ctx.sb["cols"][:, 104 + i: 105 + i],
                ctx.sb["cols"][:, 112 + i: 113 + i], ALU.mult, ALU.add),
                reads=[("scr32", 0), ("cols", 104 + i), ("cols", 112 + i)],
                writes=[("xbf", i)], tag=f"{L}fx3_{i}_{c}")
    for at, (a0, a1) in enumerate(A_CHUNKS):
        w = a1 - a0
        P.dma("gp", ctx.sb["b1a"][0:w, at: at + 1],
              ctx.dram["L1_ffn_b1"][li][a0:a1].unsqueeze(1),
              sem="ld_b1a", writes=[("cols", "b1a", at)], tag=f"{L}lb1a{at}")
        P.dma("gp", ctx.sb["b1g"][0:w, at: at + 1],
              ctx.dram["L1_ffn_b1"][li][2730 + a0: 2730 + a1].unsqueeze(1),
              sem="ld_b1g", writes=[("cols", "b1g", at)], tag=f"{L}lb1g{at}")
    for i in range(NT):
        P.dma("gp", ctx.sb["cols"][:, 104 + i: 105 + i],
              ctx.dram["L1_ffn_b2"][li][i * 128:(i + 1) * 128].unsqueeze(1),
              sem="ld_b2c", writes=[("cols", 104 + i)], tag=f"{L}lb2{i}")
    for tau in range(4):
        f0, f1 = tau * 512, (tau + 1) * 512
        for ac2 in range(11):
            c0 = ac2 * 256
            c1 = min(c0 + 256, 2730)
            wd = c1 - c0
            sl = ac2 % 2
            P.dma("gp", ctx.sb["w1a"][:, sl * 2048: sl * 2048 + NT * wd]
                  .rearrange("p (r c) -> p r c", r=NT),
                  dw1[li][:, c0:c1].rearrange("(r p) c -> p r c", p=128),
                  sem=f"ld_w1a{sl}", writes=[("w1a", sl)], tag=f"{L}l1a{tau}_{ac2}")
            P.dma("gp", ctx.sb["w1g"][:, sl * 2048: sl * 2048 + NT * wd]
                  .rearrange("p (r c) -> p r c", r=NT),
                  dw1[li][:, 2730 + c0: 2730 + c1].rearrange("(r p) c -> p r c", p=128),
                  sem=f"ld_w1g{sl}", writes=[("w1g", sl)], tag=f"{L}l1g{tau}_{ac2}")
            for sub in range((wd + 127) // 128):
                at = ac2 * 2 + sub
                a0, a1 = A_CHUNKS[at]
                aw = a1 - a0
                off = sub * 128
                bp = (at % 2) * 2
                apA = ctx.sb["psA"][0:aw, bp * 512: bp * 512 + 512]
                apG = ctx.sb["psA"][0:aw, (bp + 1) * 512: (bp + 1) * 512 + 512]
                gl = (ctx.sb["gel"] if at % 2 == 0
                      else ctx.sb["gel2"])
                gk = ("gel", at % 2)
                for i in range(NT):
                    P.op("te", lambda e, i=i, sl=sl, off=off, aw=aw, wd=wd, apA=apA,
                         f0=f0, f1=f1: e.matmul(
                        apA, ctx.sb["w1a"][:, sl * 2048 + i * wd + off: sl * 2048 + i * wd + off + aw],
                        xv(ctx, "xbf", i, False, f0, f1),
                        start=(i == 0), stop=(i == NT - 1)),
                        reads=[("w1a", sl), ("xbf", i)], writes=[RA(bp)],
                        tag=f"{L}w1a{tau}_{at}_{i}")
                    P.op("te", lambda e, i=i, sl=sl, off=off, aw=aw, wd=wd, apG=apG,
                         f0=f0, f1=f1: e.matmul(
                        apG, ctx.sb["w1g"][:, sl * 2048 + i * wd + off: sl * 2048 + i * wd + off + aw],
                        xv(ctx, "xbf", i, False, f0, f1),
                        start=(i == 0), stop=(i == NT - 1)),
                        reads=[("w1g", sl), ("xbf", i)], writes=[RA(bp + 1)],
                        tag=f"{L}w1g{tau}_{at}_{i}")
                P.op("ac", lambda e, at=at, aw=aw, apG=apG, gl=gl: e.activation(
                    gl[0:aw, 0:512], apG, AF.Gelu,
                    bias=ctx.sb["b1g"][0:aw, at: at + 1]),
                    reads=[RA(bp + 1), ("cols", "b1g", at)], writes=[gk],
                    tag=f"{L}ge{tau}_{at}")
                P.op("ve", lambda e, at=at, aw=aw, apA=apA, gl=gl: e.scalar_tensor_tensor(
                    ctx.sb["actb"][0:aw, at * 512:(at + 1) * 512], apA,
                    ctx.sb["b1a"][0:aw, at: at + 1], gl[0:aw, 0:512],
                    ALU.add, ALU.mult),
                    reads=[RA(bp), ("cols", "b1a", at), gk], writes=[("actb", at)],
                    tag=f"{L}am{tau}_{at}")
        for ct in range(22):
            a0, a1 = A_CHUNKS[ct]
            cw = a1 - a0
            sl = ct % 2
            P.dma("gp", ctx.sb["w2c"][:, sl * 1024:(sl + 1) * 1024][0:cw, :],
                  dw2[li][a0:a1, :], sem=f"ld_w2{sl}", writes=[("w2c", sl)],
                  tag=f"{L}l2{tau}_{ct}")
            for m in range(NT):
                P.op("te", lambda e, ct=ct, m=m, sl=sl, cw=cw: e.matmul(
                    (psB(ctx, m, 128, 0, 512) if m < 4 else psA(ctx, m - 4, 128, 0, 512)),
                    ctx.sb["w2c"][0:cw, sl * 1024 + m * 128: sl * 1024 + (m + 1) * 128],
                    ctx.sb["actb"][0:cw, ct * 512:(ct + 1) * 512],
                    start=(ct == 0), stop=(ct == 21)),
                    reads=[("w2c", sl), ("actb", ct)],
                    writes=[RB(m) if m < 4 else RA(m - 4)],
                    tag=f"{L}w2{tau}_{ct}_{m}")
        for m in range(NT):
            P.op("ve", lambda e, m=m, f0=f0, f1=f1: e.scalar_tensor_tensor(
                x[:, m * S + f0: m * S + f1],
                (psB(ctx, m, 128, 0, 512) if m < 4 else psA(ctx, m - 4, 128, 0, 512)),
                ctx.sb["cols"][:, 104 + m: 105 + m],
                x[:, m * S + f0: m * S + f1], ALU.add, ALU.add),
                reads=[(RB(m) if m < 4 else RA(m - 4)), ("cols", 104 + m),
                       ("xcur", m)],
                writes=[("xcur", m)], tag=f"{L}w2e{tau}_{m}")


def emit_l1_ffn(ctx):
    P = ctx.P
    x = ctx.sb["xcur"]
    P.op("ve", lambda e: e.memset(ctx.sb["cols"][0:1, 103:104], 1e-30),
         writes=[("cols", 103)], tag="fieps")
    P.dma("sp", ctx.dram["x3park"].rearrange("(r p) c -> p r c", p=128),
          x[:, :].rearrange("p (r c) -> p r c", r=NT),
          sem="st_park", reads=[("xcur",)], writes=[("x3park",)], tag="fpark")
    emit_cast_x_to_bf(ctx, "xbf", tagp="ficx")
    _l1f_stats(ctx, "fis", lambda i: False, murstd=True)
    for c in range(4):
        f0, f1 = c * 512, (c + 1) * 512
        bc_r, kr = emit_bcast_dram(ctx, 0, 0, f"fib{c}", f0, f1)
        bc_m, km = emit_bcast_dram(ctx, 1, 1, f"fim{c}", f0, f1)
        for i in range(NT):
            P.op("ve", lambda e, i=i, f0=f0, f1=f1, bc_r=bc_r: e.tensor_tensor(
                ctx.sb["scr32"][:, 0:512], x[:, i * S + f0: i * S + f1], bc_r,
                ALU.mult),
                reads=[("xcur", i), kr], writes=[("scr32", 0)], tag=f"fi1_{i}_{c}")
            P.op("ve", lambda e, i=i, f0=f0, f1=f1, bc_m=bc_m: e.tensor_tensor(
                x[:, i * S + f0: i * S + f1], ctx.sb["scr32"][:, 0:512],
                bc_m, ALU.subtract),
                reads=[("scr32", 0), km], writes=[("xcur", i)], tag=f"fi2_{i}_{c}")
    nsub = getattr(ctx, "l1f_nsub", 12)
    for li in range(6):
        if 2 * li >= nsub:
            break
        emit_l1_ffn_layer(ctx, li, ffn_too=(2 * li + 1 < nsub))
    if nsub < 12:
        P.dma("sp", ctx.x_dst_dram.rearrange("(r p) c -> p r c", p=128),
              x[:, :].rearrange("p (r c) -> p r c", r=NT),
              sem="st_dbg", reads=[("xcur",)], tag="fdbg", store=True)
        ctx.handled_store = True
        return
    xsrc = ctx.dram["x3park"]
    xdst = ctx.x_dst_dram
    for i in range(NT):
        for c in range(4):
            f0, f1 = c * 512, (c + 1) * 512
            P.dma("sp", ctx.sb["xck"], xsrc[i * 128:(i + 1) * 128, f0:f1],
                  sem="ld_xc", reads=[("x3park",)], writes=[("bcb", 1)],
                  tag=f"ff1_{i}_{c}")
            P.op("ve", lambda e, i=i, f0=f0, f1=f1: e.tensor_tensor(
                ctx.sb["xos"], ctx.sb["xck"],
                x[:, i * S + f0: i * S + f1], ALU.add),
                reads=[("bcb", 1), ("xcur", i)], writes=[("scr32", 0)],
                tag=f"ff2_{i}_{c}")
            P.dma("sp", xdst[i * 128:(i + 1) * 128, f0:f1], ctx.sb["xos"],
                  sem="st_xc", reads=[("scr32", 0)], tag=f"ff3_{i}_{c}", store=True)
    ctx.handled_store = True


def emit_l1_ffn_loads(ctx):
    d = ctx.din
    for nm in ROT_NAMES:
        dr = ctx.din("c_" + nm, (128, S))
        ctx.P.dma("gp", ctx.sb[nm][:, :], dr, sem="ld_c", writes=[(nm,)],
                  tag="ldc_" + nm)
    d("L1_ffn_Wqkv", (6, D, 3072))
    d("L1_ffn_q_scale", (6, 64))
    d("L1_ffn_k_scale", (6, 64))
    d("L1_ffn_Wo", (6, D, D))
    d("L1_ffn_ln_g", (6, D))
    d("L1_ffn_ln_b", (6, D))
    d("L1_ffn_W1", (6, D, 5460))
    d("L1_ffn_b1", (6, 5460))
    d("L1_ffn_W2", (6, 2730, D))
    d("L1_ffn_b2", (6, D))
    ctx.dram["x3park"] = ctx.nc.dram_tensor("x3park", [D, S], F32, kind="Internal").ap()


ATTN_KEYS = [("qg",), ("kg",), ("vg",), ("aobuf",), ("wg",), ("wog",),
             ("ebuf",), ("nsc",), ("gelq",)]
FFN_KEYS = [("w1a",), ("w1g",), ("w2c",), ("actb",), ("gel",)]


def alloc_l1_ffn(ctx):
    for nm in ROT_NAMES:
        ctx.sbuf(nm, (128, S), BF16)
    ctx.sbuf("xbf", (128, NT * (S + 1)), BF16)
    ctx.sbuf("sqc", (128, NT * 128), BF16)
    ctx.sbuf("scr32", (128, 512), F32)
    ctx.sbuf("rsc", (128, 512), BF16)
    pool = ctx.sbuf("fpool", (128, 35072), BF16)
    pa = pool.ap()
    ctx.sb["qg"] = pa[:, 0:8192]
    ctx.sb["kg"] = pa[:, 8192:16384]
    ctx.sb["vg"] = pa[:, 16384:24576]
    ctx.sb["aobuf"] = pa[:, 24576:32768]
    ctx.sb["wg"] = pa[:, 32768:33792]
    ctx.sb["wog"] = pa[:, 33792:34304]
    ctx.sb["ebuf"] = pa[:, 34304:35072]
    ctx.sb["w1a"] = pa[:, 0:4096]
    ctx.sb["w1g"] = pa[:, 4096:8192]
    ctx.sb["actb"] = pa[:, 8192:19456]
    ctx.sb["w2c"] = pa[:, 19456:21504]
    ctx.sb["gel"] = pa[:, 21504:22016]
    ctx.sb["gel2"] = pa[:, 22016:22528]
    ctx.sb["nsc"] = ctx.sb["rsc"].ap()[0:64, 0:256]
    ctx.sb["prep2"] = pa[:, 24576:27648]
    ctx.sb["b1a"] = ctx.sb["cols"].ap()[:, 48:70]
    ctx.sb["b1g"] = ctx.sb["cols"].ap()[:, 70:92]
    ctx.sb["xck"] = ctx.sb["bcb"].ap()[:, 512:1024]
    ctx.sb["xos"] = ctx.sb["scr32"].ap()[:, 0:512]



def build_block(block_name, consts_np):
    nc = bass.Bass("TRN2", target_bir_lowering=False)
    stack = ExitStack()
    P = Prog(nc)
    ctx = Ctx(nc, P, stack)
    ctx.handled_store = False
    xin = ctx.din("x_in", (D, S))
    xout = ctx.dout("x_out", (D, S))
    ctx.x_src_dram = xin
    ctx.x_dst_dram = xout
    setup_common(ctx, consts_np, const_names=BLOCK_CONSTS.get(block_name),
                 cols_w=120 if block_name == "l1_ffn" else 64,
                 sbuf_skip=ROT_NAMES)
    ctx.sbuf("xcur", (128, NT * S), F32)
    P.dma("sp", ctx.sb["xcur"][:, :].rearrange("p (r c) -> p r c", r=NT),
          xin.rearrange("(r p) c -> p r c", p=128),
          sem="ld_x", writes=[("xcur",)], tag="ldx")
    if block_name == "l0_ffn":
        alloc_l0_ffn(ctx)
        emit_l0_ffn_loads(ctx)
        emit_l0_ffn(ctx)
    elif block_name == "l1_attn":
        alloc_l1_attn(ctx)
        emit_l1_attn_loads(ctx)
        emit_l1_attn(ctx)
    elif block_name == "l0_attn":
        alloc_l0_attn(ctx)
        emit_l0_attn_loads(ctx)
        emit_l0_attn(ctx)
    elif block_name == "l1_ffn":
        alloc_l1_ffn(ctx)
        ctx.l1f_nsub = L1F_NSUB
        ctx.l1f_dumpqk = L1F_DUMPQK
        if L1F_DUMPQK:
            ctx.dout("dump_qg", (128, 4, S))
            ctx.dout("dump_kg", (128, 4, S))
            ctx.dout("dump_vg", (128, 16, 512))
            ctx.dout("dump_ao", (128, 4, S))
            ctx.dout("dump_ao1", (128, 4, S))
        emit_l1_ffn_loads(ctx)
        emit_l1_ffn(ctx)
    else:
        raise ValueError(block_name)
    if not ctx.handled_store:
        P.dma("sp", xout.rearrange("(r p) c -> p r c", p=128),
              ctx.sb["xcur"][:, :].rearrange("p (r c) -> p r c", r=NT),
              sem="st_x", reads=[("xcur",)], tag="stx", store=True)
    P.finish(final_engine="sp")
    return nc


L1F_NSUB = 12
L1F_DUMPQK = False

BLOCK_WEIGHTS = {
    "l0_ffn": ["L0_ffn_W_r", "L0_ffn_W_k", "L0_ffn_W_v", "L0_ffn_mu_r", "L0_ffn_mu_k"],
    "l1_attn": ["L1_attn_W_qkv", "L1_attn_b_qkv", "L1_attn_W_proj",
                "L1_attn_b_proj", "L1_attn_gamma", "L1_attn_beta"],
    "l0_attn": ["L0_attn_w_norm_in", "L0_attn_w_norm_out", "L0_attn_router_w1",
                "L0_attn_router_w2", "L0_attn_router_b2", "L0_attn_lambdas",
                "L0_attn_slr_wq1", "L0_attn_slr_wq2", "L0_attn_slr_wk1",
                "L0_attn_slr_wk2", "L0_attn_slr_wv", "L0_attn_slr_wo",
                "L0_attn_slr_lam", "L0_attn_landmarks", "L0_attn_mix_w",
                "L0_attn_bus_compress", "L0_attn_bus_expand",
                "L0_attn_bus_publish"],
    "l1_ffn": ["L1_ffn_Wqkv", "L1_ffn_q_scale", "L1_ffn_k_scale", "L1_ffn_Wo",
               "L1_ffn_ln_g", "L1_ffn_ln_b", "L1_ffn_W1", "L1_ffn_b1",
               "L1_ffn_W2", "L1_ffn_b2"],
}

BLOCK_CONSTS = {
    "l0_ffn": {"ident", "ones_col", "ones_row"},
    "l1_attn": {"ident", "ones_col", "ones_row"},
    "l0_attn": {"ident", "ones_col", "ones_row"},
    "l1_ffn": None,
}


def run_block(block_name, x_full, inputs, core_ids=None):
    consts = host_constants()
    nc = build_block(block_name, consts)
    n = x_full.shape[0]
    in_maps = []
    for b in range(n):
        m = {"x_in": np.ascontiguousarray(x_full[b].T)}
        for k, v in consts.items():
            if BLOCK_CONSTS.get(block_name) is not None and k not in BLOCK_CONSTS[block_name]:
                continue
            m["c_" + k] = np.ascontiguousarray(v)
        for k in BLOCK_WEIGHTS[block_name]:
            v = np.asarray(inputs[k])
            m[k] = np.ascontiguousarray(v.reshape(1) if v.ndim == 0 else v)
        in_maps.append(m)
    res = run_bass_kernel_spmd(nc, in_maps, core_ids=core_ids or list(range(n)))
    return np.stack([r["x_out"].T for r in res.results], axis=0)


BLOCK_ORDER = ["l0_attn", "l0_ffn", "l1_attn", "l1_ffn"]

ALL_WEIGHTS = [w for b in BLOCK_ORDER for w in BLOCK_WEIGHTS[b]]

FUSED_PARTS = {
    "l0_attn": (alloc_l0_attn, emit_l0_attn_loads, emit_l0_attn),
    "l0_ffn": (alloc_l0_ffn, emit_l0_ffn_loads, emit_l0_ffn),
    "l1_attn": (alloc_l1_attn, emit_l1_attn_loads, emit_l1_attn),
    "l1_ffn": (alloc_l1_ffn, emit_l1_ffn_loads, emit_l1_ffn),
}


def build_fused(consts_np):
    nc = bass.Bass("TRN2", target_bir_lowering=False)
    stack = ExitStack()
    P = Prog(nc)
    ctx = Ctx(nc, P, stack)
    ctx.handled_store = False
    xin = ctx.din("x_in", (D, S))
    xout = ctx.dout("x_out", (D, S))
    setup_common(ctx, consts_np, cols_w=120, sbuf_skip=ROT_NAMES)
    ctx.sbuf("xcur", (128, NT * S), F32)
    P.dma("sp", ctx.sb["xcur"][:, :].rearrange("p (r c) -> p r c", r=NT),
          xin.rearrange("(r p) c -> p r c", p=128),
          sem="ld_x", writes=[("xcur",)], tag="ldx")
    ctx.dram["xpk"] = nc.dram_tensor("xpk", [D, S], F32, kind="Internal").ap()
    ctx.dram["xpk2"] = nc.dram_tensor("xpk2", [D, S], F32, kind="Internal").ap()

    def scoped(name, pfx):
        al, lo, em = FUSED_PARTS[name]
        bs = ExitStack()
        bctx = copy.copy(ctx)
        bctx.sb = dict(ctx.sb)
        bctx.dram = dict(ctx.dram)
        bctx.stack = bs
        bctx.prefix = pfx
        al(bctx)
        lo(bctx)
        em(bctx)
        P.barrier(tag=pfx + "bar")
        bs.close()

    scoped("l0_attn", "A_")
    scoped("l0_ffn", "B_")
    P.dma("sp", ctx.dram["xpk"].rearrange("(r p) c -> p r c", p=128),
          ctx.sb["xcur"][:, :].rearrange("p (r c) -> p r c", r=NT),
          sem="st_pk", reads=[("xcur",)], writes=[("xpk",)], tag="parkx")
    ctx.x_src_dram = ctx.dram["xpk"]
    ctx.x_dst_dram = ctx.dram["xpk2"]
    scoped("l1_attn", "C_")
    P.dma("sp", ctx.sb["xcur"][:, :].rearrange("p (r c) -> p r c", r=NT),
          ctx.dram["xpk2"].rearrange("(r p) c -> p r c", p=128),
          sem="ld_x2", reads=[("xdst",)], writes=[("xcur",)], tag="reloadx")
    ctx.x_dst_dram = xout
    scoped("l1_ffn", "D_")
    P.finish(final_engine="sp")
    return nc


def kernel(**inputs):
    x = np.ascontiguousarray(np.asarray(inputs["L0_attn_x"], np.float32))
    consts = host_constants()
    nc = build_fused(consts)
    in_maps = []
    for b in range(8):
        m = {"x_in": np.ascontiguousarray(x[b].T)}
        for k, v in consts.items():
            m["c_" + k] = np.ascontiguousarray(v)
        for k in ALL_WEIGHTS:
            v = np.asarray(inputs[k])
            m[k] = np.ascontiguousarray(v.reshape(1) if v.ndim == 0 else v)
        in_maps.append(m)
    res = run_bass_kernel_spmd(nc, in_maps, core_ids=list(range(8)))
    return np.stack([r["x_out"].T for r in res.results], axis=0)
